# version 05
# baseline (speedup 1.0000x reference)
import contextlib
import numpy as np
import ml_dtypes
import concourse.bass as bass
import concourse.mybir as mybir

F32 = mybir.dt.float32
BF16 = mybir.dt.bfloat16
I32 = mybir.dt.int32
ALU = mybir.AluOpType
ACT = mybir.ActivationFunctionType
AX = mybir.AxisListType
LN2 = float(np.log(2.0))


def bcast(t, parts, nb, rep, col0=0):
    return bass.AP(t, col0, [[t.shape[1], parts], [1, nb], [0, rep]])


def bcast_h(t, parts, nh, nrep, col0=0, hstride=1):
    return bass.AP(t, col0, [[t.shape[1], parts], [hstride, nh], [0, nrep]])


class Prog:
    ENGS = ("v", "s", "t", "g", "q", "a")

    _UID = [0]

    def __init__(self, nc):
        self.nc = nc
        self.const_vals = []
        self.const_cols = {}
        self.ctab = None
        self.sem = {}
        self.cnt = {e: 0 for e in self.ENGS}
        self.eng = {}

    def alloc_consts(self, es, max_consts=64):
        self.ctab = es.enter_context(self.nc.sbuf_tensor("ctab", [128, max_consts], F32))
        self.max_consts = max_consts

    def alloc_sems(self, es):
        for e in self.ENGS:
            self.sem[e] = es.enter_context(self.nc.semaphore("sem_" + e))

    def C(self, val):
        val = float(val)
        if val in self.const_cols:
            return
        col = len(self.const_vals)
        assert col < self.max_consts, "const table full"
        self.const_cols[val] = col
        self.const_vals.append(val)
        self.nc.const_aps.register(F32, val, self.ctab[:, col:col + 1])

    def consts_array(self):
        arr = np.zeros((128, self.max_consts), np.float32)
        arr[:, :len(self.const_vals)] = np.array(self.const_vals, np.float32)[None, :]
        return arr

    def register_common_consts(self):
        for v in (1e-30, 162.0 * LN2, -162.0 * LN2, 1.0, 0.5, -1.0, 2.0, -0.5, 0.0,
                  1e-5, 1.0 / 4096.0, -0.5 * 1.0):
            self.C(v)

    def tick(self, key, inst, n=1):
        self.cnt[key] += n
        inst.then_inc(self.sem[key], n)
        return self.cnt[key]

    def vsync(self):
        i = self.nc.vector.engine_nop()
        self.tick("v", i)
        self.nc.vector.wait_ge(self.sem["v"], self.cnt["v"])

    def wait(self, from_key, on_key, value=None):
        if value is None:
            value = self.cnt[on_key]
        self.eng[from_key].wait_ge(self.sem[on_key], value)


def emit_floor(p, out, in_ap, itmp, ftmp, gtmp, shift=0.0):
    v, sca = p.nc.vector, p.nc.scalar
    i = v.tensor_scalar(itmp, in_ap, shift, None, ALU.add)
    p.tick("v", i)
    p.wait("s", "v")
    i = sca.copy(ftmp, itmp)
    p.tick("s", i)
    p.wait("v", "s")
    v.scalar_tensor_tensor(gtmp, in_ap, shift, ftmp, ALU.add, ALU.is_lt)
    p.vsync()
    return v.tensor_tensor(out, ftmp, gtmp, ALU.subtract)


def emit_qmx(p, vec, sca, vk, sk, src_ap, H, dst_bf, mfrac, t):
    mf = float(mfrac)
    max_norm = 7.5 if mf == 3.0 else 6.0
    nb = H // 32
    i = vec.tensor_reduce(t["amax"][:, :nb],
                          bass.AP(src_ap.tensor, src_ap.offset,
                                  [list(src_ap.ap[0])] + [[32, nb], [1, 32]]),
                          AX.X, ALU.max, apply_absolute_value=True)
    p.tick(vk, i)
    p.wait(sk, vk)
    i = sca.activation(t["lg"][:, :nb], t["amax"][:, :nb], ACT.Ln, bias=1e-30)
    p.tick(sk, i)
    p.wait(vk, sk)
    vec.tensor_scalar(t["tpos"][:, :nb], t["lg"][:, :nb], 1.0 / LN2, 160.0, ALU.mult, ALU.add)
    p.vsync()
    i = emit_floor(p, t["s160"][:, :nb], t["tpos"][:, :nb], t["ii"][:, :nb],
                   t["ff"][:, :nb], t["gg"][:, :nb])
    p.tick(vk, i)
    p.wait(sk, vk)
    i = sca.activation(t["iscale"][:, :nb], t["s160"][:, :nb], ACT.Exp,
                       bias=162.0 * LN2, scale=-LN2)
    i = sca.activation(t["scale"][:, :nb], t["s160"][:, :nb], ACT.Exp,
                       bias=-162.0 * LN2, scale=LN2)
    p.tick(sk, i)
    p.wait(vk, sk)
    y = t["y"][:, :H]
    sq = t["sq"][:, :H]
    vec.tensor_tensor(y, src_ap, bcast(t["iscale"], 128, nb, 32), ALU.mult)
    vec.tensor_tensor(sq, y, y, ALU.mult)
    vec.tensor_scalar(t["p1"][:, :H], sq, 4.0, -(2.0 ** (mf - 1)), ALU.is_ge, ALU.mult)
    vec.tensor_scalar(t["p2"][:, :H], sq, 16.0, -(2.0 ** (mf - 2)), ALU.is_ge, ALU.mult)
    vec.scalar_tensor_tensor(t["iqs"][:, :H], t["p1"][:, :H], 2.0 ** mf, t["p2"][:, :H], ALU.add, ALU.add)
    vec.tensor_scalar(t["q1"][:, :H], sq, 4.0, 2.0 ** (-mf), ALU.is_ge, ALU.mult)
    vec.tensor_scalar(t["q2"][:, :H], sq, 16.0, 2.0 ** (1 - mf), ALU.is_ge, ALU.mult)
    vec.scalar_tensor_tensor(t["qs"][:, :H], t["q1"][:, :H], 2.0 ** (-mf), t["q2"][:, :H], ALU.add, ALU.add)
    v_ = t["v"][:, :H]
    vec.tensor_tensor(v_, y, t["iqs"][:, :H], ALU.mult)
    i = vec.tensor_scalar(t["ri"][:, :H], v_, 64.5, None, ALU.add)
    p.tick("v", i)
    p.wait("s", "v")
    i = sca.copy(t["rf"][:, :H], t["ri"][:, :H])
    p.tick("s", i)
    p.wait("v", "s")
    vec.scalar_tensor_tensor(t["rg"][:, :H], v_, 64.5, t["rf"][:, :H], ALU.add, ALU.is_lt)
    vec.tensor_tensor(t["ru"][:, :H], t["rf"][:, :H], t["rg"][:, :H], ALU.subtract)
    vec.scalar_tensor_tensor(t["q0"][:, :H], t["ru"][:, :H], 64.0, t["qs"][:, :H], ALU.subtract, ALU.mult)
    vec.tensor_scalar(t["qc"][:, :H], t["q0"][:, :H], max_norm, -max_norm, ALU.min, ALU.max)
    i = vec.tensor_tensor(dst_bf, t["qc"][:, :H], bcast(t["scale"], 128, nb, 32), ALU.mult)
    return i


def qmx_scratch(nc, es, H, prefix=""):
    nb = H // 32
    t = {}
    for n in ("y", "sq", "p1", "p2", "iqs", "q1", "q2", "qs", "v", "rf", "rg", "ru", "q0", "qc"):
        t[n] = es.enter_context(nc.sbuf_tensor(prefix + n, [128, H], F32))
    t["ri"] = es.enter_context(nc.sbuf_tensor(prefix + "ri", [128, H], I32))
    for n in ("amax", "lg", "tpos", "ff", "gg", "s160", "iscale", "scale"):
        t[n] = es.enter_context(nc.sbuf_tensor(prefix + n, [128, nb], F32))
    t["ii"] = es.enter_context(nc.sbuf_tensor(prefix + "ii", [128, nb], I32))
    return t


def register_qmx_consts(p):
    for v in (1e-30, 162.0 * LN2, -162.0 * LN2):
        p.C(v)


def emit_sigma(p, vec, sca, vk, sk, src_ap, dst, e1, e2):
    i = sca.activation(e1, src_ap, ACT.Exp, scale=-1.0)
    p.tick(sk, i)
    p.wait(vk, sk)
    i = vec.tensor_scalar(e2, e1, 1e18, None, ALU.min)
    p.tick(vk, i)
    p.wait(sk, vk)
    i = sca.activation(e1, e2, ACT.Ln, bias=1.0)
    i = sca.activation(dst, e1, ACT.Exp, scale=-1.0)
    return i


def np_quantize_mx(x, ebits, mfrac, emax_elem, max_norm, block=32):
    f = np.float32
    shp = x.shape
    xb = x.reshape(shp[:-1] + (shp[-1] // block, block)).astype(f)
    amax = np.max(np.abs(xb), axis=-1, keepdims=True)
    sexp = np.floor(np.log2(np.where(amax == 0, f(1.0), amax))).astype(f) - f(emax_elem)
    sexp = np.clip(sexp, -127.0, 127.0)
    scale = np.exp2(sexp).astype(f)
    y = (xb / scale).astype(f)
    ax = np.abs(y)
    priv = np.floor(np.log2(np.where(ax == 0, f(1.0), ax))).astype(f)
    priv = np.maximum(priv, f(-2 ** (ebits - 1) + 2))
    qs = np.exp2(priv - f(mfrac)).astype(f)
    q = (np.round(y / qs) * qs).astype(f)
    q = np.clip(q, -max_norm, max_norm)
    return (q * scale).astype(f).reshape(shp)



def emit_qmx_chunked(p, src_ap_fn, W, dst_ap_fn, mfrac, t, chunk=2048, out_scale=1.0):
    for c0 in range(0, W, chunk):
        w = min(chunk, W - c0)
        emit_qmx2(p, src_ap_fn(c0, w), w, dst_ap_fn(c0, w), mfrac, t, out_scale=out_scale)


def emit_qmx2(p, src_ap, H, dst_bf, mfrac, t, out_scale=1.0):
    vec, sca = p.nc.vector, p.nc.scalar
    mf = float(mfrac)
    max_norm = 7.5 if mf == 3.0 else 6.0
    nb = H // 32
    b = [t[f"b{i}"] for i in range(6)]
    ib = t["ib"]
    i = vec.tensor_reduce(t["amax"][:, :nb],
                          bass.AP(src_ap.tensor, src_ap.offset,
                                  [list(src_ap.ap[0])] + [[32, nb], [1, 32]]),
                          AX.X, ALU.max, apply_absolute_value=True)
    p.tick("v", i)
    p.wait("s", "v")
    i = sca.activation(t["lg"][:, :nb], t["amax"][:, :nb], ACT.Ln, bias=1e-30)
    p.tick("s", i)
    p.wait("v", "s")
    vec.tensor_scalar(t["tpos"][:, :nb], t["lg"][:, :nb], 1.0 / LN2, 160.0, ALU.mult, ALU.add)
    p.vsync()
    i = emit_floor(p, t["s160"][:, :nb], t["tpos"][:, :nb], t["ii"][:, :nb],
                   t["ff"][:, :nb], t["gg"][:, :nb])
    p.tick("v", i)
    p.wait("s", "v")
    i = sca.activation(t["iscale"][:, :nb], t["s160"][:, :nb], ACT.Exp,
                       bias=162.0 * LN2, scale=-LN2)
    i = sca.activation(t["scale"][:, :nb], t["s160"][:, :nb], ACT.Exp,
                       bias=float(-162.0 * LN2 + np.log(out_scale)), scale=LN2)
    p.tick("s", i)
    p.wait("v", "s")
    y, sq = b[0][:, :H], b[1][:, :H]
    vec.tensor_tensor(y, src_ap, bcast(t["iscale"], 128, nb, 32), ALU.mult)
    vec.tensor_tensor(sq, y, y, ALU.mult)
    vec.tensor_scalar(b[2][:, :H], sq, 4.0, -(2.0 ** (mf - 1)), ALU.is_ge, ALU.mult)
    vec.tensor_scalar(b[3][:, :H], sq, 16.0, -(2.0 ** (mf - 2)), ALU.is_ge, ALU.mult)
    vec.scalar_tensor_tensor(b[4][:, :H], b[2][:, :H], 2.0 ** mf, b[3][:, :H], ALU.add, ALU.add)
    vec.tensor_scalar(b[3][:, :H], sq, 4.0, 2.0 ** (-mf), ALU.is_ge, ALU.mult)
    vec.tensor_scalar(b[5][:, :H], sq, 16.0, 2.0 ** (1 - mf), ALU.is_ge, ALU.mult)
    vec.scalar_tensor_tensor(b[1][:, :H], b[3][:, :H], 2.0 ** (-mf), b[5][:, :H], ALU.add, ALU.add)
    vec.tensor_tensor(b[2][:, :H], y, b[4][:, :H], ALU.mult)
    i = vec.tensor_scalar(ib[:, :H], b[2][:, :H], 64.5, None, ALU.add)
    p.tick("v", i)
    p.wait("s", "v")
    i = sca.copy(b[0][:, :H], ib[:, :H])
    p.tick("s", i)
    p.wait("v", "s")
    vec.scalar_tensor_tensor(b[3][:, :H], b[2][:, :H], 64.5, b[0][:, :H], ALU.add, ALU.is_lt)
    vec.tensor_tensor(b[4][:, :H], b[0][:, :H], b[3][:, :H], ALU.subtract)
    vec.scalar_tensor_tensor(b[5][:, :H], b[4][:, :H], 64.0, b[1][:, :H], ALU.subtract, ALU.mult)
    vec.tensor_scalar(b[2][:, :H], b[5][:, :H], max_norm, -max_norm, ALU.min, ALU.max)
    i = vec.tensor_tensor(dst_bf, b[2][:, :H], bcast(t["scale"], 128, nb, 32), ALU.mult)
    p.tick("v", i)
    return i


def qmx_scratch2(nc, es, W, prefix="q"):
    nb = W // 32
    t = {}
    for i in range(6):
        t[f"b{i}"] = es.enter_context(nc.sbuf_tensor(f"{prefix}b{i}", [128, W], F32))
    t["ib"] = es.enter_context(nc.sbuf_tensor(prefix + "ib", [128, W], I32))
    for n in ("amax", "lg", "tpos", "ff", "gg", "s160", "iscale", "scale"):
        t[n] = es.enter_context(nc.sbuf_tensor(prefix + n, [128, nb], F32))
    t["ii"] = es.enter_context(nc.sbuf_tensor(prefix + "ii", [128, nb], I32))
    return t


def dma(p, key, dst, src, slow=False):
    eng = p.nc.sync if key == "q" else p.nc.scalar
    i = eng.dma_start(dst, src, allow_slow_non_contiguous=True) if slow else eng.dma_start(dst, src)
    p.cnt[key] += 16
    i.then_inc(p.sem[key], 16)
    return p.cnt[key]


def barrier(p):
    nc = p.nc
    for k in ("q", "a"):
        if p.cnt[k]:
            nc.sync.wait_ge(p.sem[k], p.cnt[k])
    for k in ("v", "s", "t", "g"):
        if p.cnt[k]:
            nc.sync.wait_ge(p.sem[k], p.cnt[k])
    nc.all_engine_barrier()


def emit_transpose_batch(p, src_fn, n, psum_bf, dst_ap, ident, wait_q=None):
    nc = p.nc
    if not isinstance(ident, bass.AP):
        ident = ident[:, :]
    for j in range(n):
        i = nc.tensor.transpose(psum_bf[:, j * 128:(j + 1) * 128], src_fn(j), ident)
    p.tick("t", i)
    p.wait("s", "t")
    if wait_q is not None:
        p.wait("s", "q", wait_q)
    i = nc.scalar.copy(dst_ap, psum_bf[:, :n * 128])
    p.tick("s", i)


def emit_ln(p, x_ap, W, xn_out, lnt, sq_scr):
    nc = p.nc
    vec, sca = nc.vector, nc.scalar
    sums, mcol, m2, vcol, lnv, rcol, mr = (lnt[:, i:i + 1] for i in range(7))
    i = vec.tensor_reduce(sums, x_ap, AX.X, ALU.add)
    p.tick("v", i)
    p.wait("s", "v")
    i = sca.activation(sq_scr, x_ap, ACT.Square, accum_out=lnt[:, 7:8])
    p.tick("s", i)
    p.vsync()
    vec.tensor_scalar(mcol, sums, 1.0 / W, None, ALU.mult)
    p.vsync()
    vec.tensor_tensor(m2, mcol, mcol, ALU.mult)
    p.wait("v", "s")
    p.vsync()
    i = vec.scalar_tensor_tensor(vcol, lnt[:, 7:8], 1.0 / W, m2, ALU.mult, ALU.subtract)
    p.tick("v", i)
    p.wait("s", "v")
    sca.activation(lnv, vcol, ACT.Ln, bias=1e-5)
    i = sca.activation(rcol, lnv, ACT.Exp, scale=-0.5)
    p.tick("s", i)
    p.wait("v", "s")
    vec.tensor_tensor(mr, mcol, rcol, ALU.mult)
    p.vsync()
    i = vec.tensor_scalar(xn_out, x_ap, rcol, mr, ALU.mult, ALU.subtract)
    p.tick("v", i)


def emit_ffn_weight_prep(p, w_in, IT, H, wqT_out, qt, ident, out_scale=1.0, dbg=None):
    nc = p.nc
    es = contextlib.ExitStack()
    U = _u(p)
    wsb = [es.enter_context(nc.sbuf_tensor(f"wp_wsb{i}" + U, [128, H], F32)) for i in range(2)]
    wq = [es.enter_context(nc.sbuf_tensor(f"wp_wq{i}" + U, [128, H], BF16)) for i in range(2)]
    stg = es.enter_context(nc.sbuf_tensor("wp_stg" + U, [128, 2048], BF16))
    pt = es.enter_context(nc.psum_tensor("wp_pt" + U, [128, 2048], BF16))
    NG = H // 2048 if H >= 2048 else 1
    CW = min(H, 2048)
    qdone = [0, 0]
    tdone = [0, 0]
    evac_s = 0
    outq = 0
    with es:
        dma(p, "q", wsb[0][:, :], bass.AP(w_in, 0, [[H, 128], [1, H]]))
        ld = [p.cnt["q"], 0]
        for it in range(IT):
            b = it % 2
            if it + 1 < IT:
                if it >= 1:
                    p.wait("q", "v", qdone[(it + 1) % 2])
                dma(p, "q", wsb[(it + 1) % 2][:, :],
                    bass.AP(w_in, (it + 1) * 128 * H, [[H, 128], [1, H]]))
                ld[(it + 1) % 2] = p.cnt["q"]
            p.wait("v", "q", ld[b])
            p.wait("s", "q", ld[b])
            if it >= 2:
                p.wait("v", "t", tdone[b])
            for c0 in range(0, H, CW):
                emit_qmx2(p, wsb[b][:, c0:c0 + CW], CW, wq[b][:, c0:c0 + CW], 1.0, qt,
                          out_scale=out_scale)
            qdone[b] = p.cnt["v"]
            for g0 in range(0, H // 128, 16):
                ng = min(16, H // 128 - g0)
                p.wait("t", "v", qdone[b])
                p.wait("t", "s", evac_s)
                emit_transpose_batch(
                    p, lambda j: wq[b][:, (g0 + j) * 128:(g0 + j + 1) * 128], ng,
                    pt, stg[:, :ng * 128], ident, wait_q=outq)
                evac_s = p.cnt["s"]
                p.wait("q", "s", evac_s)
                W2 = IT * 128
                dma(p, "q", bass.AP(wqT_out, (g0 * 128) * W2 + it * 128,
                                    [[W2, 128], [W2 * 128, ng], [1, 128]]),
                    stg[:, :ng * 128].rearrange("p (j c) -> p j c", j=ng))
                outq = p.cnt["q"]
            tdone[b] = p.cnt["t"]


def emit_plain_weight_prep(p, w_in, IT, H, wT_out, ident, tiled=False):
    nc = p.nc
    es = contextlib.ExitStack()
    U = _u(p)
    wsb = [es.enter_context(nc.sbuf_tensor(f"pp_wsb{i}" + U, [128, H], F32)) for i in range(2)]
    wq = [es.enter_context(nc.sbuf_tensor(f"pp_wq{i}" + U, [128, H], BF16)) for i in range(2)]
    stg = es.enter_context(nc.sbuf_tensor("pp_stg" + U, [128, 2048], BF16))
    pt = es.enter_context(nc.psum_tensor("pp_pt" + U, [128, 2048], BF16))
    NJ = H // 128
    W2 = IT * 128
    qdone = [0, 0]
    tdone = [0, 0]
    evac_s = 0
    outq = 0
    with es:
        dma(p, "q", wsb[0][:, :], bass.AP(w_in, 0, [[H, 128], [1, H]]))
        ld = [p.cnt["q"], 0]
        for it in range(IT):
            b = it % 2
            if it + 1 < IT:
                if it >= 1:
                    p.wait("q", "v", qdone[(it + 1) % 2])
                dma(p, "q", wsb[(it + 1) % 2][:, :],
                    bass.AP(w_in, (it + 1) * 128 * H, [[H, 128], [1, H]]))
                ld[(it + 1) % 2] = p.cnt["q"]
            p.wait("v", "q", ld[b])
            if it >= 2:
                p.wait("v", "t", tdone[b])
            i = nc.vector.tensor_copy(wq[b][:, :], wsb[b][:, :])
            p.tick("v", i)
            qdone[b] = p.cnt["v"]
            for g0 in range(0, NJ, 16):
                ng = min(16, NJ - g0)
                p.wait("t", "v", qdone[b])
                p.wait("t", "s", evac_s)
                emit_transpose_batch(
                    p, lambda j: wq[b][:, (g0 + j) * 128:(g0 + j + 1) * 128], ng,
                    pt, stg[:, :ng * 128], ident, wait_q=outq)
                evac_s = p.cnt["s"]
                p.wait("q", "s", evac_s)
                if tiled:
                    dma(p, "q", bass.AP(wT_out, (it * NJ + g0) * 16384,
                                        [[128, 128], [16384, ng], [1, 128]]),
                        stg[:, :ng * 128].rearrange("p (j c) -> p j c", j=ng))
                else:
                    dma(p, "q", bass.AP(wT_out, (g0 * 128) * W2 + it * 128,
                                        [[W2, 128], [W2 * 128, ng], [1, 128]]),
                        stg[:, :ng * 128].rearrange("p (j c) -> p j c", j=ng))
                outq = p.cnt["q"]
            tdone[b] = p.cnt["t"]


def _u(p):
    Prog._UID[0] += 1
    return f"_{Prog._UID[0]}"


def emit_ffn_stage(p, src, dst, wgT, wuT, wdT, interT, ident):
    nc = p.nc
    import contextlib
    T, H, I = 4096, 2048, 8192
    NT = T // 128
    TCH = 2048
    NTC = TCH // 128
    for tch in range(T // TCH):
        es = contextlib.ExitStack()
        U = _u(p)
        qt = qmx_scratch2(nc, es, 1024, prefix="ffq" + U)
        xn_qT = es.enter_context(nc.sbuf_tensor("ff_xnqT" + U, [128, 16 * TCH], BF16))
        with es:
            es1b = contextlib.ExitStack()
            xb = [es1b.enter_context(nc.sbuf_tensor(f"ff_xb{i}{U}", [128, H], F32)) for i in range(2)]
            xn = es1b.enter_context(nc.sbuf_tensor("ff_xn" + U, [128, H], F32))
            xq = es1b.enter_context(nc.sbuf_tensor("ff_xq" + U, [128, H], BF16))
            lnt = es1b.enter_context(nc.sbuf_tensor("ff_lnt" + U, [128, 8], F32))
            ptr = es1b.enter_context(nc.psum_tensor("ff_ptr" + U, [128, 2048], BF16))
            _noop = es1b
            with es1b:
                vdone = [0] * NTC
                for t in range(NTC):
                    tt = tch * NTC + t
                    if t >= 2:
                        p.wait("q", "v", vdone[t - 2])
                    dma(p, "q", xb[t % 2][:, :], bass.AP(src, tt * 128 * H, [[H, 128], [1, H]]))
                    p.wait("v", "q", p.cnt["q"])
                    p.wait("s", "q", p.cnt["q"])
                    if t >= 1:
                        p.wait("v", "t")
                    emit_ln(p, xb[t % 2][:, :], H, xn[:, :], lnt, xn[:, :])
                    for c0 in (0, 1024):
                        emit_qmx2(p, xn[:, c0:c0 + 1024], 1024, xq[:, c0:c0 + 1024], 3.0, qt)
                    vdone[t] = p.cnt["v"]
                    p.wait("t", "v")
                    if t > 0:
                        p.wait("t", "s")
                    emit_transpose_batch(
                        p, lambda j: xq[:, j * 128:(j + 1) * 128], 16, ptr,
                        bass.AP(xn_qT, t * 128, [[16 * TCH, 128], [TCH, 16], [1, 128]]),
                        ident)
                barrier(p)
            es2 = contextlib.ExitStack()
            U2 = _u(p)
            GW = 512
            NJ = GW // 128
            wg_sb = [es2.enter_context(nc.sbuf_tensor(f"ff_wg{i}{U2}", [128, 16 * GW], BF16)) for i in range(2)]
            wu_sb = [es2.enter_context(nc.sbuf_tensor(f"ff_wu{i}{U2}", [128, 16 * GW], BF16)) for i in range(2)]
            gbuf = es2.enter_context(nc.sbuf_tensor("ff_gbuf" + U2, [128, GW], F32))
            ubuf = es2.enter_context(nc.sbuf_tensor("ff_ubuf" + U2, [128, GW], F32))
            sgbuf = es2.enter_context(nc.sbuf_tensor("ff_sg" + U2, [128, GW], F32))
            e1b = es2.enter_context(nc.sbuf_tensor("ff_e1" + U2, [128, GW], F32))
            inb = es2.enter_context(nc.sbuf_tensor("ff_in" + U2, [128, GW], F32))
            iq = [es2.enter_context(nc.sbuf_tensor(f"ff_iq{i}{U2}", [128, GW], BF16)) for i in range(2)]
            itq = [es2.enter_context(nc.sbuf_tensor(f"ff_itq{i}{U2}", [128, GW], BF16)) for i in range(2)]
            psg = es2.enter_context(nc.psum_tensor("ff_psg" + U2, [128, 512], F32))
            psu = es2.enter_context(nc.psum_tensor("ff_psu" + U2, [128, 512], F32))
            ptr2 = es2.enter_context(nc.psum_tensor("ff_ptr2" + U2, [128, GW], BF16))
            NG = I // GW
            NK = NG * NTC

            def load_wslab(g, buf):
                for wT, sb in ((wgT, wg_sb[buf]), (wuT, wu_sb[buf])):
                    for j in range(16):
                        dma(p, "q", sb[:, j * GW:(j + 1) * GW],
                            bass.AP(wT, j * 128 * I + g * GW, [[I, 128], [1, GW]]))
                return p.cnt["q"]

            def emit_tr(k):
                p.wait("t", "v", qx_tick[k])
                if k >= 2:
                    p.wait("t", "s", ev_tick[k - 2])
                kk = k
                emit_transpose_batch(
                    p, lambda j: iq[kk % 2][:, j * 128:(j + 1) * 128], NJ, ptr2,
                    itq[kk % 2][:, :GW], ident,
                    wait_q=(qitq[k - 2] if k >= 2 else None))
                ev_tick[k] = p.cnt["s"]
                p.wait("q", "s", ev_tick[k])
                g_, t_ = divmod(k, NTC)
                tt_ = tch * NTC + t_
                dma(p, "q", bass.AP(interT, (tt_ * 64 + g_ * NJ) * 16384,
                                    [[128, 128], [16384, NJ], [1, 128]]),
                    itq[kk % 2][:, :GW].rearrange("p (j c) -> p j c", j=NJ))
                qitq[k] = p.cnt["q"]

            with es2:
                wready = [load_wslab(0, 0), 0]
                gdone = [0] * NG
                sc_tick = [0] * NK
                qx_tick = [0] * NK
                ev_tick = [0] * NK
                qitq = [0] * NK
                inb_tick = [0] * NK
                for g in range(NG):
                    if g + 1 < NG:
                        if g >= 1:
                            p.wait("q", "t", gdone[g - 1])
                        wready[(g + 1) % 2] = load_wslab(g + 1, (g + 1) % 2)
                    for t in range(NTC):
                        k = g * NTC + t
                        p.wait("t", "q", wready[g % 2])
                        p.wait("t", "s", sc_tick[k - 1] if k else 0)
                        for j in range(16):
                            nc.tensor.matmul(
                                psg[:, :],
                                bass.AP(xn_qT, j * TCH + t * 128, [[16 * TCH, 128], [1, 128]]),
                                wg_sb[g % 2][:, j * GW:j * GW + GW],
                                start=(j == 0), stop=(j == 15))
                        for j in range(16):
                            i = nc.tensor.matmul(
                                psu[:, :],
                                bass.AP(xn_qT, j * TCH + t * 128, [[16 * TCH, 128], [1, 128]]),
                                wu_sb[g % 2][:, j * GW:j * GW + GW],
                                start=(j == 0), stop=(j == 15))
                        p.tick("t", i)
                        mm_tick = p.cnt["t"]
                        gdone[g] = mm_tick
                        if k >= 1:
                            emit_tr(k - 1)
                        p.wait("s", "t", mm_tick)
                        if k >= 1:
                            p.wait("s", "v", inb_tick[k - 1])
                        nc.scalar.copy(gbuf[:, :], psg[:, :])
                        i = nc.scalar.copy(ubuf[:, :], psu[:, :])
                        p.tick("s", i)
                        sc_tick[k] = p.cnt["s"]
                        i = nc.scalar.activation(e1b[:, :], gbuf[:, :], ACT.Exp, scale=-1.0)
                        p.tick("s", i)
                        p.wait("v", "s")
                        i = nc.vector.tensor_scalar(sgbuf[:, :], e1b[:, :], 1e18, None, ALU.min)
                        p.tick("v", i)
                        p.wait("s", "v")
                        nc.scalar.activation(e1b[:, :], sgbuf[:, :], ACT.Ln, bias=1.0)
                        i = nc.scalar.activation(sgbuf[:, :], e1b[:, :], ACT.Exp, scale=-1.0)
                        p.tick("s", i)
                        p.wait("v", "s")
                        nc.vector.tensor_tensor(inb[:, :], gbuf[:, :], sgbuf[:, :], ALU.mult)
                        i = nc.vector.tensor_tensor(inb[:, :], inb[:, :], ubuf[:, :], ALU.mult)
                        p.tick("v", i)
                        inb_tick[k] = p.cnt["v"]
                        if k >= 2:
                            p.wait("v", "t")
                        emit_qmx2(p, inb[:, :], GW, iq[k % 2][:, :], 3.0, qt)
                        qx_tick[k] = p.cnt["v"]
                emit_tr(NK - 1)
        barrier(p)
    es3 = contextlib.ExitStack()
    U3 = _u(p)
    wd_sb = es3.enter_context(nc.sbuf_tensor("ff_wd" + U3, [128, 64 * 1024], BF16))
    it_sb = [es3.enter_context(nc.sbuf_tensor(f"ff_it{i}{U3}", [128, 64 * 128], BF16)) for i in range(2)]
    xs = [es3.enter_context(nc.sbuf_tensor(f"ff_xs{i}{U3}", [128, 1024], F32)) for i in range(2)]
    outb = [es3.enter_context(nc.sbuf_tensor(f"ff_ob{i}{U3}", [128, 1024], F32)) for i in range(2)]
    pso = es3.enter_context(nc.psum_tensor("ff_pso" + U3, [128, 1024], F32))
    with es3:
        for half in range(2):
            h0 = half * 1024
            if half == 1:
                p.wait("q", "t")
            for j in range(64):
                dma(p, "q", wd_sb[:, j * 1024:(j + 1) * 1024],
                    bass.AP(wdT, j * 128 * H + h0, [[H, 128], [1, 1024]]))
            wdready = p.cnt["q"]
            tmm = [0] * NT
            vev = [0] * NT
            qout = [0] * NT
            for t in range(NT):
                bi = t % 2
                if t >= 2:
                    p.wait("q", "t", tmm[t - 2])
                    p.wait("q", "v", vev[t - 2])
                dma(p, "q", it_sb[bi][:, :],
                    bass.AP(interT, t * 64 * 16384, [[128, 128], [16384, 64], [1, 128]]))
                dma(p, "q", xs[bi][:, :],
                    bass.AP(src, t * 128 * H + h0, [[H, 128], [1, 1024]]))
                p.wait("t", "q", max(p.cnt["q"], wdready))
                if t >= 1:
                    p.wait("t", "v", vev[t - 1])
                for j in range(64):
                    nc.tensor.matmul(pso[:, 0:512], it_sb[bi][:, j * 128:(j + 1) * 128],
                                     wd_sb[:, j * 1024:j * 1024 + 512],
                                     start=(j == 0), stop=(j == 63))
                    i = nc.tensor.matmul(pso[:, 512:1024], it_sb[bi][:, j * 128:(j + 1) * 128],
                                         wd_sb[:, j * 1024 + 512:j * 1024 + 1024],
                                         start=(j == 0), stop=(j == 63))
                p.tick("t", i)
                tmm[t] = p.cnt["t"]
                p.wait("v", "t")
                if t >= 2:
                    p.wait("v", "q", qout[t - 2])
                i = nc.vector.tensor_tensor(outb[bi][:, :], pso[:, :], xs[bi][:, :], ALU.add)
                p.tick("v", i)
                vev[t] = p.cnt["v"]
                p.wait("q", "v", vev[t])
                dma(p, "q", bass.AP(dst, t * 128 * H + h0, [[H, 128], [1, 1024]]),
                    outb[bi][:, :])
                qout[t] = p.cnt["q"]
    barrier(p)


"""Mamba-2 SSD stage emitter for the macaron megakernel.

Stage B: dst = src + Mamba(LN(src)) per core (one sequence of 4096 tokens).
4096/CH chunks of CH tokens, carrying conv halo (3 cols) and SSD state across chunks.
"""

CH = 256
NTT = CH // 128
NBL = CH // 64
NOT = 67


def emit_sigma_psum(p, src_ap, dst, e1, e2, mul_with=None, bias_col=None):
    nc = p.nc
    if bias_col is None:
        i = nc.scalar.activation(e1, src_ap, ACT.Exp, scale=-1.0)
    else:
        i = nc.scalar.activation(e1, src_ap, ACT.Exp, scale=-1.0, bias=bias_col)
    p.tick("s", i)
    p.wait("v", "s")
    i = nc.vector.tensor_scalar(e2, e1, 1e18, None, ALU.min)
    p.tick("v", i)
    p.wait("s", "v")
    nc.scalar.activation(e1, e2, ACT.Ln, bias=1.0)
    i = nc.scalar.activation(e2, e1, ACT.Exp, scale=-1.0)
    p.tick("s", i)
    p.wait("v", "s")
    i = nc.vector.tensor_tensor(dst, mul_with, e2, ALU.mult)
    p.tick("v", i)
    return i


def emit_mamba_stage(p, src, dst, winT_t, woT, params, cm):
    nc = p.nc
    ident, identf = cm["ident"], cm["identf"]
    L1, M1m, ones1, onescol, Ieye64 = cm["L1"], cm["M1m"], cm["ones1"], cm["onescol"], cm["Ieye64"]
    onesf = cm["onesf"]
    es = contextlib.ExitStack()
    U = _u(p)
    T = lambda n, s, d=F32: es.enter_context(nc.sbuf_tensor(n + U, s, d))
    Sf = [T(f"mb_Sf{i}", [128, 4096]) for i in range(2)]
    Sbf = T("mb_Sbf", [128, 4096], BF16)
    convw = T("mb_convw", [128, 136])
    convb = T("mb_convb", [128, 34])
    convbn = T("mb_convbn", [128, 34])
    dtb = T("mb_dtb", [64, 1])
    acol = T("mb_acol", [64, 1])
    gcol = T("mb_gcol", [128, 32])
    halo = T("mb_halo", [128, 102])
    ddiag = T("mb_ddiag", [64, 4096], BF16)
    drow = T("mb_drow", [1, 64])
    drowb = T("mb_drowb", [1, 64], BF16)
    dmat_s = T("mb_dmat", [64, 64])
    with es:
        dma(p, "q", convw[:, :], bass.AP(params["conv_w"], 0, [[4, 128], [512, 34], [1, 4]]))
        dma(p, "q", convb[:, :34], bass.AP(params["conv_b"], 0, [[1, 128], [128, 34]]), slow=True)
        dma(p, "q", dtb[:, :], bass.AP(params["dt_bias"], 0, [[1, 64], [1, 1]]), slow=True)
        dma(p, "q", acol[:, :], bass.AP(params["A_log"], 0, [[1, 64], [1, 1]]), slow=True)
        dma(p, "q", drow[:, :], bass.AP(params["Dv"], 0, [[0, 1], [1, 64]]))
        dma(p, "q", gcol[:, :], bass.AP(params["norm_w"], 0, [[1, 128], [128, 32]]), slow=True)
        p.wait("v", "q", p.cnt["q"])
        p.wait("s", "q", p.cnt["q"])
        nc.vector.tensor_scalar(convbn[:, :34], convb[:, :34], -1.0, None, ALU.mult)
        nc.vector.memset(halo[:, :], 0.0)
        nc.vector.memset(Sf[0][:, :], 0.0)
        i = nc.vector.memset(Sbf[:, :], 0.0)
        p.tick("v", i)
        i = nc.scalar.activation(acol[:, :], acol[:, :], ACT.Exp)
        p.tick("s", i)
        p.wait("v", "s")
        p.vsync()
        i = nc.vector.tensor_scalar(acol[:, :], acol[:, :], -1.0, None, ALU.mult)
        p.tick("v", i)
        p.wait("s", "v")
        i = nc.scalar.copy(drowb[:, :], drow[:, :])
        p.tick("s", i)
        p.wait("t", "s")
        with nc.psum_tensor("mb_pdm" + U, [64, 64], F32) as pdm:
            i = nc.tensor.matmul(pdm[:, :], ones1[:1, :64], drowb[:, :], start=True, stop=True)
            p.tick("t", i)
            p.wait("v", "t")
            i = nc.vector.tensor_copy(dmat_s[:, :], pdm[:, :])
            p.tick("v", i)
            p.vsync()
            i = nc.vector.tensor_tensor(
                ddiag[:, :],
                bass.AP(dmat_s, 0, [[64, 64], [1, 64], [0, 64]]),
                bass.AP(Ieye64, 0, [[64, 64], [0, 64], [1, 64]]),
                ALU.mult)
            p.tick("v", i)
        barrier(p)

        for ch in range(4096 // CH):
            esC = contextlib.ExitStack()
            UC = _u(p)
            TC = lambda n, s, d=F32: esC.enter_context(nc.sbuf_tensor(n + UC, s, d))
            y3 = TC("mb_y3", [128, 32 * CH], BF16)
            rT = TC("mb_rT", [128, NTT])
            sz = TC("mb_sz", [128, 32 * CH], BF16)
            xbcs = TC("mb_xbcs", [128, 34 * CH], BF16)
            ext = TC("mb_ext", [128, CH + 3])
            accb = TC("mb_accb", [128, CH])
            e1 = TC("mb_e1", [128, CH])
            e2 = TC("mb_e2", [128, CH])
            dtT = TC("mb_dtT", [64, NBL * 64])
            adT = TC("mb_adT", [64, NBL * 64])
            BT = TC("mb_BT", [64, NBL * 128], BF16)
            with esC:
                esA = contextlib.ExitStack()
                TA = lambda n, s, d=F32: esA.enter_context(nc.sbuf_tensor(n + UC + "A", s, d))
                xc = [TA(f"mb_xc{i}", [128, 2048]) for i in range(2)]
                lnt = TA("mb_lnt", [128, 8])
                xnbf = TA("mb_xnbf", [128, 2048], BF16)
                xnT = TA("mb_xnT", [128, 16 * CH], BF16)
                wo_sb = [TA(f"mb_wo{i}", [128, 16 * 128], BF16) for i in range(2)]
                dtx = ext[0:64, 0:CH]
                dtv = accb[0:64, 0:CH]
                adv = e1[0:64, 0:CH]
                with esA:
                    vdone = [0, 0]
                    with nc.psum_tensor("mb_ptr" + UC, [128, 2048], BF16) as ptr:
                        for tt in range(NTT):
                            if tt >= 2:
                                p.wait("q", "v", vdone[tt % 2])
                            dma(p, "q", xc[tt % 2][:, :],
                                bass.AP(src, (ch * NTT + tt) * 128 * 2048, [[2048, 128], [1, 2048]]))
                            p.wait("v", "q", p.cnt["q"])
                            p.wait("s", "q", p.cnt["q"])
                            if tt > 0:
                                p.wait("v", "t")
                            emit_ln(p, xc[tt % 2][:, :], 2048, xnbf[:, :], lnt,
                                    y3[:, 0:2048])
                            vdone[tt % 2] = p.cnt["v"]
                            p.wait("t", "v")
                            if tt > 0:
                                p.wait("t", "s")
                            emit_transpose_batch(
                                p, lambda j: xnbf[:, j * 128:(j + 1) * 128], 16, ptr,
                                bass.AP(xnT, tt * 128, [[16 * CH, 128], [CH, 16], [1, 128]]),
                                ident)
                    def load_wot(ot, buf):
                        dma(p, "q", wo_sb[buf][:, :],
                            bass.AP(winT_t, ot * 16 * 16384, [[128, 128], [16384, 16], [1, 128]]))
                        return p.cnt["q"]
                    with (nc.psum_tensor("mb_po0" + UC, [128, CH], F32) as po0,
                          nc.psum_tensor("mb_po1" + UC, [128, CH], F32) as po1,
                          nc.psum_tensor("mb_pt2" + UC, [128, 64], F32) as pt2):
                        pos = [po0, po1]
                        p.wait("t", "s")
                        wready = [load_wot(0, 0), 0]
                        lastuse = [0, 0]
                        ev_v = [0, 0]
                        ev_s = [0, 0]
                        for ot in range(NOT):
                            po = pos[ot % 2]
                            if ot + 1 < NOT:
                                if ot >= 1:
                                    p.wait("q", "t", lastuse[(ot + 1) % 2])
                                wready[(ot + 1) % 2] = load_wot(ot + 1, (ot + 1) % 2)
                            p.wait("t", "q", wready[ot % 2])
                            if ot >= 2:
                                p.wait("t", "v", ev_v[ot % 2])
                                p.wait("t", "s", ev_s[ot % 2])
                            for j in range(16):
                                i = nc.tensor.matmul(
                                    po[:, :], wo_sb[ot % 2][:, j * 128:(j + 1) * 128],
                                    xnT[:, j * CH:j * CH + CH],
                                    start=(j == 0), stop=(j == 15))
                            p.tick("t", i)
                            lastuse[ot % 2] = p.cnt["t"]
                            p.wait("s", "t")
                            p.wait("v", "t")
                            if ot < 32:
                                emit_sigma_psum(p, po[:, :], sz[:, ot * CH:(ot + 1) * CH],
                                                e1[:, :], e2[:, :], mul_with=po[:, :])
                                ev_v[ot % 2] = p.cnt["v"]
                                ev_s[ot % 2] = p.cnt["s"]
                            elif ot < 66:
                                c = ot - 32
                                nc.vector.tensor_copy(ext[:, 0:3], halo[:, c * 3:c * 3 + 3])
                                i = nc.scalar.copy(ext[:, 3:CH + 3], po[:, :])
                                p.tick("s", i)
                                ev_s[ot % 2] = p.cnt["s"]
                                ev_v[ot % 2] = p.cnt["v"]
                                p.wait("v", "s")
                                p.vsync()
                                i = nc.vector.tensor_copy(halo[:, c * 3:c * 3 + 3], ext[:, CH:CH + 3])
                                p.tick("v", i)
                                p.vsync()
                                nc.vector.tensor_scalar(accb[:, :], ext[:, 0:CH],
                                                        convw[:, c * 4:c * 4 + 1], None, ALU.mult)
                                for k in (1, 2, 3):
                                    nc.vector.scalar_tensor_tensor(
                                        accb[:, :], ext[:, k:k + CH], convw[:, c * 4 + k:c * 4 + k + 1],
                                        accb[:, :], ALU.mult, ALU.add)
                                i = nc.vector.tensor_scalar(e2[:, :], accb[:, :],
                                                            convb[:, c:c + 1], None, ALU.add)
                                p.tick("v", i)
                                p.wait("s", "v")
                                emit_sigma_psum(p, accb[:, :], xbcs[:, c * CH:(c + 1) * CH],
                                                e1[:, :], accb[:, :], mul_with=e2[:, :],
                                                bias_col=convbn[:, c:c + 1])
                            else:
                                i = nc.vector.tensor_scalar(dtx, po[0:64, :], dtb[:, 0:1],
                                                            None, ALU.add)
                                p.tick("v", i)
                                ev_v[ot % 2] = p.cnt["v"]
                                ev_s[ot % 2] = p.cnt["s"]
                                p.wait("s", "v")
                                nc.scalar.activation(e2[0:64, :], dtx, ACT.Abs)
                                nc.scalar.activation(e2[0:64, :], e2[0:64, :], ACT.Exp, scale=-1.0)
                                i = nc.scalar.activation(e2[0:64, :], e2[0:64, :], ACT.Ln, bias=1.0)
                                p.tick("s", i)
                                p.wait("v", "s")
                                nc.vector.scalar_tensor_tensor(dtv, dtx, 0.0,
                                                               e2[0:64, :], ALU.max, ALU.add)
                                p.vsync()
                                i = nc.vector.tensor_scalar(adv, dtv, acol[:, 0:1],
                                                            None, ALU.mult)
                                p.tick("v", i)
                                p.wait("t", "v")
                                for tt in range(NTT):
                                    i = nc.tensor.transpose(pt2[0:128, :], dtv[:, tt * 128:(tt + 1) * 128],
                                                            identf[0:64, 0:64])
                                    p.tick("t", i)
                                    p.wait("s", "t")
                                    nc.scalar.copy(dtT[:, (2 * tt) * 64:(2 * tt + 1) * 64], pt2[0:64, :])
                                    i = nc.scalar.copy(dtT[:, (2 * tt + 1) * 64:(2 * tt + 2) * 64], pt2[64:128, :])
                                    p.tick("s", i)
                                    p.wait("t", "s")
                                    i = nc.tensor.transpose(pt2[0:128, :], adv[:, tt * 128:(tt + 1) * 128],
                                                            identf[0:64, 0:64])
                                    p.tick("t", i)
                                    p.wait("s", "t")
                                    nc.scalar.copy(adT[:, (2 * tt) * 64:(2 * tt + 1) * 64], pt2[0:64, :])
                                    i = nc.scalar.copy(adT[:, (2 * tt + 1) * 64:(2 * tt + 2) * 64], pt2[64:128, :])
                                    p.tick("s", i)
                                    p.wait("t", "s")
                    with nc.psum_tensor("mb_ptb" + UC, [128, NTT * 128], BF16) as ptb:
                        p.wait("t", "v")
                        for tt in range(NTT):
                            i = nc.tensor.transpose(ptb[:, tt * 128:(tt + 1) * 128],
                                                    xbcs[:, 32 * CH + tt * 128:32 * CH + (tt + 1) * 128],
                                                    ident[:, :])
                        p.tick("t", i)
                        p.wait("s", "t")
                        for tt in range(NTT):
                            nc.scalar.copy(BT[:, (2 * tt) * 128:(2 * tt + 1) * 128],
                                           ptb[0:64, tt * 128:(tt + 1) * 128])
                            i = nc.scalar.copy(BT[:, (2 * tt + 1) * 128:(2 * tt + 2) * 128],
                                               ptb[64:128, tt * 128:(tt + 1) * 128])
                        p.tick("s", i)
                    barrier(p)
                esB = contextlib.ExitStack()
                TB = lambda n, s, d=F32: esB.enter_context(nc.sbuf_tensor(n + UC + "S", s, d))
                XTb = TB("mb_XTb", [64, 4096], BF16)
                rhsD = TB("mb_rhsD", [64, 2048])
                rhsH = TB("mb_rhsH", [64, 2048], BF16)
                rhsL = TB("mb_rhsL", [64, 2048], BF16)
                Mh = TB("mb_Mh", [64, 4096], BF16)
                Gm = TB("mb_Gm", [64, 64], BF16)
                dt63 = TB("mb_dt63", [64, 64])
                en1 = TB("mb_en1", [64, 64])
                decdtT = TB("mb_decdt", [64, 64])
                expAcT = TB("mb_expAcT", [64, 64])
                lam = TB("mb_lam", [1, 64])
                lamlogm = TB("mb_lamlogm", [128, 64])
                lamm = TB("mb_lamm", [128, 64])
                Bd = TB("mb_Bd", [64, 2048], BF16)
                Yb = TB("mb_Yb", [64, 4096], BF16)
                y2b = TB("mb_y2", [128, 1024])
                sq16 = TB("mb_sq16", [128, 1024], BF16)
                with esB:
                    for b in range(NBL):
                        dtTb = dtT[0:64, b * 64:(b + 1) * 64]
                        adTb = adT[0:64, b * 64:(b + 1) * 64]
                        Cblk = xbcs[:, 33 * CH + b * 64:33 * CH + (b + 1) * 64]
                        Bblk = xbcs[:, 32 * CH + b * 64:32 * CH + (b + 1) * 64]
                        with nc.psum_tensor(f"mb_pxt{b}" + UC, [64, 2048], BF16) as pxt:
                            for half in range(2):
                                p.wait("t", "s")
                                p.wait("t", "v")
                                for cj in range(16):
                                    ct = half * 16 + cj
                                    i = nc.tensor.transpose(
                                        pxt[:, cj * 128:(cj + 1) * 128],
                                        xbcs[:, ct * CH + b * 64:ct * CH + (b + 1) * 64],
                                        ident[:, :])
                                p.tick("t", i)
                                p.wait("s", "t")
                                i = nc.scalar.copy(XTb[:, half * 2048:(half + 1) * 2048],
                                                   pxt[:, :])
                                p.tick("s", i)
                        with (nc.psum_tensor(f"mb_pG{b}" + UC, [64, 64], F32) as pG,
                              nc.psum_tensor(f"mb_pD{b}" + UC, [64, 2048], F32) as pD):
                            p.wait("t", "v")
                            p.wait("t", "s")
                            i = nc.tensor.matmul(pG[:, :], Bblk, Cblk, start=True, stop=True)
                            p.tick("t", i)
                            p.wait("v", "t")
                            i = nc.vector.tensor_tensor(Gm[:, :], pG[:, :], M1m[:, :], ALU.mult)
                            p.tick("v", i)
                            for hg in range(2):
                                p.wait("v", "s")
                                p.vsync()
                                nc.vector.tensor_tensor(
                                    rhsD[:, :],
                                    bass.AP(adT, b * 64 + hg * 32, [[NBL * 64, 64], [1, 32], [0, 64]]),
                                    bass.AP(M1m, 0, [[64, 64], [0, 32], [1, 64]]),
                                    ALU.mult)
                                nc.vector.tensor_copy(rhsH[:, :], rhsD[:, :])
                                i = nc.vector.tensor_tensor(rhsL[:, :], rhsD[:, :], rhsH[:, :], ALU.subtract)
                                p.tick("v", i)
                                p.wait("t", "v")
                                p.wait("t", "s")
                                for seg in range(4):
                                    nc.tensor.matmul(pD[:, seg * 512:(seg + 1) * 512],
                                                     L1[:, :], rhsH[:, seg * 512:(seg + 1) * 512],
                                                     start=True, stop=False)
                                    i = nc.tensor.matmul(pD[:, seg * 512:(seg + 1) * 512],
                                                         L1[:, :], rhsL[:, seg * 512:(seg + 1) * 512],
                                                         start=False, stop=True)
                                p.tick("t", i)
                                p.wait("s", "t")
                                i = nc.scalar.activation(Mh[:, hg * 2048:(hg + 1) * 2048],
                                                         pD[:, :], ACT.Exp)
                                p.tick("s", i)
                                p.wait("v", "s")
                                i = nc.vector.tensor_copy(
                                    dt63[:, hg * 32:hg * 32 + 32],
                                    bass.AP(pD, 63, [[2048, 64], [64, 32], [1, 1]]))
                                p.tick("v", i)
                                p.vsync()
                                nc.vector.tensor_tensor(
                                    Mh[:, hg * 2048:(hg + 1) * 2048],
                                    Mh[:, hg * 2048:(hg + 1) * 2048],
                                    bass.AP(Gm, 0, [[64, 64], [0, 32], [1, 64]]),
                                    ALU.mult)
                                i = nc.vector.tensor_tensor(
                                    Mh[:, hg * 2048:(hg + 1) * 2048],
                                    Mh[:, hg * 2048:(hg + 1) * 2048],
                                    bass.AP(dtT, b * 64 + hg * 32, [[NBL * 64, 64], [1, 32], [0, 64]]),
                                    ALU.mult)
                                p.tick("v", i)
                        p.vsync()
                        i = nc.vector.tensor_tensor(lam[0:1, :], dt63[0:1, :], adTb[0:1, :], ALU.add)
                        p.tick("v", i)
                        p.wait("s", "v")
                        i = nc.scalar.activation(decdtT[:, :], dt63[:, :], ACT.Exp)
                        p.tick("s", i)
                        p.wait("t", "v")
                        with nc.psum_tensor(f"mb_pL{b}" + UC, [128, 64], F32) as pL:
                            i = nc.tensor.matmul(pL[:, :], onesf[0:1, :], lam[0:1, :],
                                                 start=True, stop=True)
                            p.tick("t", i)
                            p.wait("v", "t")
                            i = nc.vector.tensor_copy(lamlogm[:, :], pL[:, :])
                            p.tick("v", i)
                        p.vsync()
                        nc.vector.tensor_tensor(en1[:, :], lamlogm[0:64, :], dt63[:, :], ALU.subtract)
                        i = nc.vector.tensor_tensor(decdtT[:, :], decdtT[:, :], dtTb, ALU.mult)
                        p.tick("v", i)
                        p.wait("s", "v")
                        nc.scalar.activation(expAcT[:, :], en1[:, :], ACT.Exp)
                        i = nc.scalar.activation(lamm[:, :], lamlogm[:, :], ACT.Exp)
                        p.tick("s", i)
                        p.wait("v", "s")
                        p.wait("t", "s")
                        for hg in range(4):
                            i = nc.vector.tensor_tensor(
                                Bd[:, :],
                                bass.AP(BT, b * 128, [[NBL * 128, 64], [0, 16], [1, 128]]),
                                bass.AP(decdtT, hg * 16, [[64, 64], [1, 16], [0, 128]]),
                                ALU.mult)
                            p.tick("v", i)
                            with (nc.psum_tensor(f"mb_pYd{b}_{hg}" + UC, [64, 1024], F32) as pYd,
                                  nc.psum_tensor(f"mb_pYo{b}_{hg}" + UC, [64, 1024], F32) as pYo,
                                  nc.psum_tensor(f"mb_pS{b}_{hg}" + UC, [128, 1024], F32) as pS):
                                p.wait("t", "v")
                                p.wait("t", "s")
                                for hl in range(16):
                                    h = hg * 16 + hl
                                    xsl = XTb[:, h * 64:(h + 1) * 64]
                                    nc.tensor.matmul(pYd[:, hl * 64:(hl + 1) * 64],
                                                     Mh[:, h * 64:(h + 1) * 64], xsl,
                                                     start=True, stop=False)
                                    nc.tensor.matmul(pYd[:, hl * 64:(hl + 1) * 64],
                                                     ddiag[:, h * 64:(h + 1) * 64], xsl,
                                                     start=False, stop=True)
                                    nc.tensor.matmul(pYo[:, hl * 64:(hl + 1) * 64],
                                                     Cblk, Sbf[:, h * 64:(h + 1) * 64],
                                                     start=True, stop=True)
                                    i = nc.tensor.matmul(pS[:, hl * 64:(hl + 1) * 64],
                                                         Bd[:, hl * 128:(hl + 1) * 128], xsl,
                                                         start=True, stop=True)
                                p.tick("t", i)
                                p.wait("v", "t")
                                so = Sf[(ch * NBL + b) % 2]
                                sn = Sf[(ch * NBL + b + 1) % 2]
                                nc.vector.tensor_tensor(
                                    sn[:, hg * 1024:(hg + 1) * 1024],
                                    so[:, hg * 1024:(hg + 1) * 1024],
                                    bass.AP(lamm, hg * 16, [[64, 128], [1, 16], [0, 64]]),
                                    ALU.mult)
                                nc.vector.tensor_tensor(
                                    sn[:, hg * 1024:(hg + 1) * 1024],
                                    sn[:, hg * 1024:(hg + 1) * 1024],
                                    pS[:, :], ALU.add)
                                nc.vector.tensor_tensor(
                                    y2b[0:64, :], pYo[:, :],
                                    bass.AP(expAcT, hg * 16, [[64, 64], [1, 16], [0, 64]]),
                                    ALU.mult)
                                i = nc.vector.tensor_tensor(
                                    Yb[:, hg * 1024:(hg + 1) * 1024],
                                    y2b[0:64, :], pYd[:, :], ALU.add)
                                p.tick("v", i)
                        sn = Sf[(ch * NBL + b + 1) % 2]
                        i = nc.vector.tensor_copy(Sbf[:, :], sn[:, :])
                        p.tick("v", i)
                        with (nc.psum_tensor(f"mb_pyt{b}" + UC, [128, 1024], BF16) as pyt,
                              nc.psum_tensor(f"mb_prm{b}" + UC, [1, 64], F32) as prm,
                              nc.psum_tensor(f"mb_prt{b}" + UC, [64, 1], F32) as prt):
                            for half in range(2):
                                p.wait("t", "v")
                                for cj in range(16):
                                    ct = half * 16 + cj
                                    i = nc.tensor.transpose(
                                        pyt[:, cj * 64:(cj + 1) * 64],
                                        Yb[:, ct * 128:(ct + 1) * 128],
                                        ident[0:64, 0:64])
                                p.tick("t", i)
                                p.wait("v", "t")
                                nc.vector.tensor_tensor(
                                    y2b[:, :], pyt[:, :],
                                    bass.AP(sz, half * 16 * CH + b * 64,
                                            [[32 * CH, 128], [CH, 16], [1, 64]]),
                                    ALU.mult)
                                i = nc.vector.tensor_tensor(
                                    bass.AP(y3, half * 16 * CH + b * 64,
                                            [[32 * CH, 128], [CH, 16], [1, 64]]),
                                    y2b[:, :],
                                    bass.AP(gcol, half * 16, [[32, 128], [1, 16], [0, 64]]),
                                    ALU.mult)
                                p.tick("v", i)
                                p.wait("s", "v")
                                i = nc.scalar.activation(sq16[:, :], y2b[:, :], ACT.Square)
                                p.tick("s", i)
                                p.wait("t", "s")
                                for cj in range(16):
                                    i = nc.tensor.matmul(prm[:, :], onescol[:, 0:1],
                                                         sq16[:, cj * 64:(cj + 1) * 64],
                                                         start=(half == 0 and cj == 0),
                                                         stop=(half == 1 and cj == 15))
                                p.tick("t", i)
                            p.wait("s", "t")
                            nc.scalar.activation(e1[0:1, 0:64], prm[:, :], ACT.Ln,
                                                 scale=1.0 / 4096.0, bias=1e-5)
                            i = nc.scalar.activation(lam[0:1, :], e1[0:1, 0:64], ACT.Exp,
                                                     scale=-0.5)
                            p.tick("s", i)
                            p.wait("t", "s")
                            i = nc.tensor.matmul(prt[:, :], lam[0:1, :], onesf[0:1, 0:1],
                                                 start=True, stop=True)
                            p.tick("t", i)
                            p.wait("v", "t")
                            tt0 = b // 2
                            pl = (b % 2) * 64
                            i = nc.vector.tensor_copy(rT[pl:pl + 64, tt0:tt0 + 1], prt[:, :])
                            p.tick("v", i)
                    barrier(p)
                esD = contextlib.ExitStack()
                TD = lambda n, s, d=F32: esD.enter_context(nc.sbuf_tensor(n + UC + "D", s, d))
                wos = [TD(f"mb_wos{i}", [128, 2048], BF16) for i in range(2)]
                xs2 = [TD(f"mb_xs{i}", [128, 2048]) for i in range(2)]
                ob = [TD(f"mb_ob{i}", [128, 2048]) for i in range(2)]
                with esD:
                    with (nc.psum_tensor("mb_pout0" + UC, [128, 2048], F32) as pout0,
                          nc.psum_tensor("mb_pout1" + UC, [128, 2048], F32) as pout1):
                        pouts = [pout0, pout1]
                        for tt in range(NTT):
                            dma(p, "q", xs2[tt][:, :],
                                bass.AP(src, (ch * NTT + tt) * 128 * 2048, [[2048, 128], [1, 2048]]))
                        def load_wos(cj, buf):
                            dma(p, "q", wos[buf][:, :],
                                bass.AP(woT, cj * 128 * 2048, [[2048, 128], [1, 2048]]))
                            return p.cnt["q"]
                        wready = [load_wos(0, 0), 0]
                        lastw = [0, 0]
                        for cj in range(32):
                            if cj + 1 < 32:
                                if cj >= 1:
                                    p.wait("q", "t", lastw[(cj + 1) % 2])
                                wready[(cj + 1) % 2] = load_wos(cj + 1, (cj + 1) % 2)
                            p.wait("t", "q", wready[cj % 2])
                            if cj == 0:
                                p.wait("t", "v")
                            for tt in range(NTT):
                                for dch in range(4):
                                    i = nc.tensor.matmul(
                                        pouts[tt][:, dch * 512:(dch + 1) * 512],
                                        y3[:, cj * CH + tt * 128:cj * CH + (tt + 1) * 128],
                                        wos[cj % 2][:, dch * 512:(dch + 1) * 512],
                                        start=(cj == 0), stop=(cj == 31))
                            p.tick("t", i)
                            lastw[cj % 2] = p.cnt["t"]
                        p.wait("v", "t")
                        for tt in range(NTT):
                            i = nc.vector.scalar_tensor_tensor(
                                ob[tt][:, :], pouts[tt][:, :], rT[:, tt:tt + 1],
                                xs2[tt][:, :], ALU.mult, ALU.add)
                            p.tick("v", i)
                            p.wait("q", "v")
                            dma(p, "q", bass.AP(dst, (ch * NTT + tt) * 128 * 2048,
                                                [[2048, 128], [1, 2048]]),
                                ob[tt][:, :])
    barrier(p)
    barrier(p)


import contextlib
from concourse.bass_utils import run_bass_kernel_spmd


def _build_program():
    nc = bass.Bass("TRN2", target_bir_lowering=False, detect_race_conditions=False)
    p = Prog(nc)
    T, H, I = 4096, 2048, 8192
    x_in = nc.dram_tensor("x", [T, H], F32, kind="ExternalInput")
    wg_in = nc.dram_tensor("w_gate", [I, H], F32, kind="ExternalInput")
    wu_in = nc.dram_tensor("w_up", [I, H], F32, kind="ExternalInput")
    wd_in = nc.dram_tensor("w_down", [H, I], F32, kind="ExternalInput")
    wip = nc.dram_tensor("in_proj_pad", [8576, H], F32, kind="ExternalInput")
    wop = nc.dram_tensor("out_proj", [H, 4096], F32, kind="ExternalInput")
    cw_in = nc.dram_tensor("conv_w", [4352, 4], F32, kind="ExternalInput")
    cb_in = nc.dram_tensor("conv_b", [4352], F32, kind="ExternalInput")
    al_in = nc.dram_tensor("A_log", [64], F32, kind="ExternalInput")
    dv_in = nc.dram_tensor("Dv", [64], F32, kind="ExternalInput")
    dtb_in = nc.dram_tensor("dt_bias", [64], F32, kind="ExternalInput")
    nw_in = nc.dram_tensor("norm_w", [4096], F32, kind="ExternalInput")
    c_in = nc.dram_tensor("consts", [128, 64], F32, kind="ExternalInput")
    id_in = nc.dram_tensor("ident", [128, 128], BF16, kind="ExternalInput")
    idf_in = nc.dram_tensor("identf", [128, 128], F32, kind="ExternalInput")
    L1_in = nc.dram_tensor("L1", [64, 64], BF16, kind="ExternalInput")
    M1_in = nc.dram_tensor("M1m", [64, 64], BF16, kind="ExternalInput")
    o1_in = nc.dram_tensor("ones1", [1, 128], BF16, kind="ExternalInput")
    oc_in = nc.dram_tensor("onescol", [128, 1], BF16, kind="ExternalInput")
    ie_in = nc.dram_tensor("Ieye64", [64, 64], BF16, kind="ExternalInput")
    of_in = nc.dram_tensor("onesf", [1, 128], F32, kind="ExternalInput")
    out = nc.dram_tensor("out", [T, H], F32, kind="ExternalOutput")
    wgT = nc.dram_tensor("wgT", [H, I], BF16)
    wuT = nc.dram_tensor("wuT", [H, I], BF16)
    wdT = nc.dram_tensor("wdT", [I, H], BF16)
    interT = nc.dram_tensor("interT", [64 * 32 * 16384], BF16)
    winT_t = nc.dram_tensor("winT_t", [67 * 16 * 16384], BF16)
    woT = nc.dram_tensor("woT", [4096, 2048], BF16)
    resid1 = nc.dram_tensor("resid1", [T, H], F32)
    resid2 = nc.dram_tensor("resid2", [T, H], F32)

    es0 = contextlib.ExitStack()
    p.alloc_consts(es0)
    p.alloc_sems(es0)
    identt = es0.enter_context(nc.sbuf_tensor("identt", [128, 128], BF16))
    identf = es0.enter_context(nc.sbuf_tensor("identft", [128, 128], F32))
    L1t = es0.enter_context(nc.sbuf_tensor("L1t", [64, 64], BF16))
    M1t = es0.enter_context(nc.sbuf_tensor("M1t", [64, 64], BF16))
    o1t = es0.enter_context(nc.sbuf_tensor("o1t", [1, 128], BF16))
    oct_ = es0.enter_context(nc.sbuf_tensor("oct", [128, 1], BF16))
    iet = es0.enter_context(nc.sbuf_tensor("iet", [64, 64], BF16))
    oft = es0.enter_context(nc.sbuf_tensor("oft", [1, 128], F32))
    with es0:
        register_qmx_consts(p)
        for v in (1.0, -1.0, 0.5, -0.5, 1e18, 1e-5, 1.0 / H, 1.0 / 4096.0,
                  float(-162.0 * LN2 + np.log(0.5))):
            p.C(v)
        p.eng.update(v=nc.vector, s=nc.scalar, t=nc.tensor, g=nc.gpsimd,
                     q=nc.sync, a=nc.scalar)
        for dst_t, src_t in ((p.ctab, c_in), (identt, id_in), (identf, idf_in),
                             (L1t, L1_in), (M1t, M1_in), (o1t, o1_in),
                             (oct_, oc_in), (iet, ie_in), (oft, of_in)):
            dma(p, "q", dst_t[:, :], src_t.ap())
        nc.sync.wait_ge(p.sem["q"], p.cnt["q"])
        nc.all_engine_barrier()
        es1 = contextlib.ExitStack()
        qt = qmx_scratch2(nc, es1, 2048, prefix="pq")
        with es1:
            emit_ffn_weight_prep(p, wg_in, 64, H, wgT, qt, identt)
            emit_ffn_weight_prep(p, wu_in, 64, H, wuT, qt, identt)
            emit_ffn_weight_prep(p, wd_in, 16, I, wdT, qt, identt, out_scale=0.5)
            emit_plain_weight_prep(p, wip, 67, H, winT_t, identt, tiled=True)
            emit_plain_weight_prep(p, wop, 16, 4096, woT, identt)
            barrier(p)
        params = dict(conv_w=cw_in, conv_b=cb_in, A_log=al_in, Dv=dv_in,
                      dt_bias=dtb_in, norm_w=nw_in)
        cm = dict(ident=identt, identf=identf, L1=L1t, M1m=M1t, ones1=o1t,
                  onescol=oct_, Ieye64=iet, onesf=oft)
        emit_ffn_stage(p, x_in, resid1, wgT, wuT, wdT, interT, identt)
        emit_mamba_stage(p, resid1, resid2, winT_t, woT, params, cm)
        emit_ffn_stage(p, resid2, out, wgT, wuT, wdT, interT, identt)
    return nc, p


_CACHED = {}


def kernel(L0_attn_x, L0_attn_in_proj_w, L0_attn_conv_w, L0_attn_conv_b,
           L0_attn_A_log, L0_attn_D, L0_attn_dt_bias, L0_attn_norm_w,
           L0_attn_out_proj_w, L0_ffn_w_gate, L0_ffn_w_up, L0_ffn_w_down):
    import ml_dtypes
    if "nc" not in _CACHED:
        _CACHED["nc"], _CACHED["p"] = _build_program()
    nc, p = _CACHED["nc"], _CACHED["p"]
    x = np.ascontiguousarray(np.asarray(L0_attn_x, dtype=np.float32))
    in_proj_pad = np.zeros((8576, 2048), np.float32)
    in_proj_pad[:8512] = np.asarray(L0_attn_in_proj_w, np.float32)
    kk, ll = np.meshgrid(np.arange(64), np.arange(64), indexing="ij")
    common = {
        "w_gate": np.ascontiguousarray(np.asarray(L0_ffn_w_gate, np.float32)),
        "w_up": np.ascontiguousarray(np.asarray(L0_ffn_w_up, np.float32)),
        "w_down": np.ascontiguousarray(np.asarray(L0_ffn_w_down, np.float32)),
        "in_proj_pad": in_proj_pad,
        "out_proj": np.ascontiguousarray(np.asarray(L0_attn_out_proj_w, np.float32)),
        "conv_w": np.ascontiguousarray(np.asarray(L0_attn_conv_w, np.float32).reshape(4352, 4)),
        "conv_b": np.ascontiguousarray(np.asarray(L0_attn_conv_b, np.float32)),
        "A_log": np.ascontiguousarray(np.asarray(L0_attn_A_log, np.float32)),
        "Dv": np.ascontiguousarray(np.asarray(L0_attn_D, np.float32)),
        "dt_bias": np.ascontiguousarray(np.asarray(L0_attn_dt_bias, np.float32)),
        "norm_w": np.ascontiguousarray(np.asarray(L0_attn_norm_w, np.float32)),
        "consts": p.consts_array(),
        "ident": np.eye(128).astype(ml_dtypes.bfloat16),
        "identf": np.eye(128).astype(np.float32),
        "L1": (kk > ll).astype(ml_dtypes.bfloat16),
        "M1m": (kk <= ll).astype(ml_dtypes.bfloat16),
        "ones1": np.ones((1, 128), ml_dtypes.bfloat16),
        "onescol": np.ones((128, 1), ml_dtypes.bfloat16),
        "Ieye64": np.eye(64).astype(ml_dtypes.bfloat16),
        "onesf": np.ones((1, 128), np.float32),
    }
    in_maps = [dict(common, x=np.ascontiguousarray(x[b])) for b in range(8)]
    res = run_bass_kernel_spmd(nc, in_maps, core_ids=list(range(8)))
    return np.stack([np.asarray(res.results[b]["out"]) for b in range(8)], axis=0)


# version 06
# speedup vs baseline: 1.0190x; 1.0190x over previous
import contextlib
import numpy as np
import ml_dtypes
import concourse.bass as bass
import concourse.mybir as mybir

F32 = mybir.dt.float32
BF16 = mybir.dt.bfloat16
I32 = mybir.dt.int32
ALU = mybir.AluOpType
ACT = mybir.ActivationFunctionType
AX = mybir.AxisListType
LN2 = float(np.log(2.0))


def bcast(t, parts, nb, rep, col0=0):
    return bass.AP(t, col0, [[t.shape[1], parts], [1, nb], [0, rep]])


def bcast_h(t, parts, nh, nrep, col0=0, hstride=1):
    return bass.AP(t, col0, [[t.shape[1], parts], [hstride, nh], [0, nrep]])


class Prog:
    ENGS = ("v", "s", "t", "g", "q", "a")

    _UID = [0]

    def __init__(self, nc):
        self.nc = nc
        self.const_vals = []
        self.const_cols = {}
        self.ctab = None
        self.sem = {}
        self.cnt = {e: 0 for e in self.ENGS}
        self.eng = {}

    def alloc_consts(self, es, max_consts=64):
        self.ctab = es.enter_context(self.nc.sbuf_tensor("ctab", [128, max_consts], F32))
        self.max_consts = max_consts

    def alloc_sems(self, es):
        for e in self.ENGS:
            self.sem[e] = es.enter_context(self.nc.semaphore("sem_" + e))

    def C(self, val):
        val = float(val)
        if val in self.const_cols:
            return
        col = len(self.const_vals)
        assert col < self.max_consts, "const table full"
        self.const_cols[val] = col
        self.const_vals.append(val)
        self.nc.const_aps.register(F32, val, self.ctab[:, col:col + 1])

    def consts_array(self):
        arr = np.zeros((128, self.max_consts), np.float32)
        arr[:, :len(self.const_vals)] = np.array(self.const_vals, np.float32)[None, :]
        return arr

    def register_common_consts(self):
        for v in (1e-30, 162.0 * LN2, -162.0 * LN2, 1.0, 0.5, -1.0, 2.0, -0.5, 0.0,
                  1e-5, 1.0 / 4096.0, -0.5 * 1.0):
            self.C(v)

    def tick(self, key, inst, n=1):
        self.cnt[key] += n
        inst.then_inc(self.sem[key], n)
        return self.cnt[key]

    def vsync(self):
        i = self.nc.vector.engine_nop()
        self.tick("v", i)
        self.nc.vector.wait_ge(self.sem["v"], self.cnt["v"])

    def wait(self, from_key, on_key, value=None):
        if value is None:
            value = self.cnt[on_key]
        self.eng[from_key].wait_ge(self.sem[on_key], value)


def emit_floor(p, out, in_ap, itmp, ftmp, gtmp, shift=0.0):
    v, sca = p.nc.vector, p.nc.scalar
    i = v.tensor_scalar(itmp, in_ap, shift, None, ALU.add)
    p.tick("v", i)
    p.wait("s", "v")
    i = sca.copy(ftmp, itmp)
    p.tick("s", i)
    p.wait("v", "s")
    v.scalar_tensor_tensor(gtmp, in_ap, shift, ftmp, ALU.add, ALU.is_lt)
    p.vsync()
    return v.tensor_tensor(out, ftmp, gtmp, ALU.subtract)


def emit_qmx(p, vec, sca, vk, sk, src_ap, H, dst_bf, mfrac, t):
    mf = float(mfrac)
    max_norm = 7.5 if mf == 3.0 else 6.0
    nb = H // 32
    i = vec.tensor_reduce(t["amax"][:, :nb],
                          bass.AP(src_ap.tensor, src_ap.offset,
                                  [list(src_ap.ap[0])] + [[32, nb], [1, 32]]),
                          AX.X, ALU.max, apply_absolute_value=True)
    p.tick(vk, i)
    p.wait(sk, vk)
    i = sca.activation(t["lg"][:, :nb], t["amax"][:, :nb], ACT.Ln, bias=1e-30)
    p.tick(sk, i)
    p.wait(vk, sk)
    vec.tensor_scalar(t["tpos"][:, :nb], t["lg"][:, :nb], 1.0 / LN2, 160.0, ALU.mult, ALU.add)
    p.vsync()
    i = emit_floor(p, t["s160"][:, :nb], t["tpos"][:, :nb], t["ii"][:, :nb],
                   t["ff"][:, :nb], t["gg"][:, :nb])
    p.tick(vk, i)
    p.wait(sk, vk)
    i = sca.activation(t["iscale"][:, :nb], t["s160"][:, :nb], ACT.Exp,
                       bias=162.0 * LN2, scale=-LN2)
    i = sca.activation(t["scale"][:, :nb], t["s160"][:, :nb], ACT.Exp,
                       bias=-162.0 * LN2, scale=LN2)
    p.tick(sk, i)
    p.wait(vk, sk)
    y = t["y"][:, :H]
    sq = t["sq"][:, :H]
    vec.tensor_tensor(y, src_ap, bcast(t["iscale"], 128, nb, 32), ALU.mult)
    vec.tensor_tensor(sq, y, y, ALU.mult)
    vec.tensor_scalar(t["p1"][:, :H], sq, 4.0, -(2.0 ** (mf - 1)), ALU.is_ge, ALU.mult)
    vec.tensor_scalar(t["p2"][:, :H], sq, 16.0, -(2.0 ** (mf - 2)), ALU.is_ge, ALU.mult)
    vec.scalar_tensor_tensor(t["iqs"][:, :H], t["p1"][:, :H], 2.0 ** mf, t["p2"][:, :H], ALU.add, ALU.add)
    vec.tensor_scalar(t["q1"][:, :H], sq, 4.0, 2.0 ** (-mf), ALU.is_ge, ALU.mult)
    vec.tensor_scalar(t["q2"][:, :H], sq, 16.0, 2.0 ** (1 - mf), ALU.is_ge, ALU.mult)
    vec.scalar_tensor_tensor(t["qs"][:, :H], t["q1"][:, :H], 2.0 ** (-mf), t["q2"][:, :H], ALU.add, ALU.add)
    v_ = t["v"][:, :H]
    vec.tensor_tensor(v_, y, t["iqs"][:, :H], ALU.mult)
    i = vec.tensor_scalar(t["ri"][:, :H], v_, 64.5, None, ALU.add)
    p.tick("v", i)
    p.wait("s", "v")
    i = sca.copy(t["rf"][:, :H], t["ri"][:, :H])
    p.tick("s", i)
    p.wait("v", "s")
    vec.scalar_tensor_tensor(t["rg"][:, :H], v_, 64.5, t["rf"][:, :H], ALU.add, ALU.is_lt)
    vec.tensor_tensor(t["ru"][:, :H], t["rf"][:, :H], t["rg"][:, :H], ALU.subtract)
    vec.scalar_tensor_tensor(t["q0"][:, :H], t["ru"][:, :H], 64.0, t["qs"][:, :H], ALU.subtract, ALU.mult)
    vec.tensor_scalar(t["qc"][:, :H], t["q0"][:, :H], max_norm, -max_norm, ALU.min, ALU.max)
    i = vec.tensor_tensor(dst_bf, t["qc"][:, :H], bcast(t["scale"], 128, nb, 32), ALU.mult)
    return i


def qmx_scratch(nc, es, H, prefix=""):
    nb = H // 32
    t = {}
    for n in ("y", "sq", "p1", "p2", "iqs", "q1", "q2", "qs", "v", "rf", "rg", "ru", "q0", "qc"):
        t[n] = es.enter_context(nc.sbuf_tensor(prefix + n, [128, H], F32))
    t["ri"] = es.enter_context(nc.sbuf_tensor(prefix + "ri", [128, H], I32))
    for n in ("amax", "lg", "tpos", "ff", "gg", "s160", "iscale", "scale"):
        t[n] = es.enter_context(nc.sbuf_tensor(prefix + n, [128, nb], F32))
    t["ii"] = es.enter_context(nc.sbuf_tensor(prefix + "ii", [128, nb], I32))
    return t


def register_qmx_consts(p):
    for v in (1e-30, 162.0 * LN2, -162.0 * LN2, 64.5):
        p.C(v)


def emit_sigma(p, vec, sca, vk, sk, src_ap, dst, e1, e2):
    i = sca.activation(e1, src_ap, ACT.Exp, scale=-1.0)
    p.tick(sk, i)
    p.wait(vk, sk)
    i = vec.tensor_scalar(e2, e1, 1e18, None, ALU.min)
    p.tick(vk, i)
    p.wait(sk, vk)
    i = sca.activation(e1, e2, ACT.Ln, bias=1.0)
    i = sca.activation(dst, e1, ACT.Exp, scale=-1.0)
    return i


def np_quantize_mx(x, ebits, mfrac, emax_elem, max_norm, block=32):
    f = np.float32
    shp = x.shape
    xb = x.reshape(shp[:-1] + (shp[-1] // block, block)).astype(f)
    amax = np.max(np.abs(xb), axis=-1, keepdims=True)
    sexp = np.floor(np.log2(np.where(amax == 0, f(1.0), amax))).astype(f) - f(emax_elem)
    sexp = np.clip(sexp, -127.0, 127.0)
    scale = np.exp2(sexp).astype(f)
    y = (xb / scale).astype(f)
    ax = np.abs(y)
    priv = np.floor(np.log2(np.where(ax == 0, f(1.0), ax))).astype(f)
    priv = np.maximum(priv, f(-2 ** (ebits - 1) + 2))
    qs = np.exp2(priv - f(mfrac)).astype(f)
    q = (np.round(y / qs) * qs).astype(f)
    q = np.clip(q, -max_norm, max_norm)
    return (q * scale).astype(f).reshape(shp)



def emit_qmx_chunked(p, src_ap_fn, W, dst_ap_fn, mfrac, t, chunk=2048, out_scale=1.0):
    for c0 in range(0, W, chunk):
        w = min(chunk, W - c0)
        emit_qmx2(p, src_ap_fn(c0, w), w, dst_ap_fn(c0, w), mfrac, t, out_scale=out_scale)


def emit_qmx2(p, src_ap, H, dst_bf, mfrac, t, out_scale=1.0):
    vec, sca = p.nc.vector, p.nc.scalar
    mf = float(mfrac)
    max_norm = 7.5 if mf == 3.0 else 6.0
    nb = H // 32
    b = [t[f"b{i}"] for i in range(6)]
    ib = t["ib"]
    i = vec.tensor_reduce(t["amax"][:, :nb],
                          bass.AP(src_ap.tensor, src_ap.offset,
                                  [list(src_ap.ap[0])] + [[32, nb], [1, 32]]),
                          AX.X, ALU.max, apply_absolute_value=True)
    p.tick("v", i)
    p.wait("s", "v")
    i = sca.activation(t["lg"][:, :nb], t["amax"][:, :nb], ACT.Ln, bias=1e-30)
    p.tick("s", i)
    p.wait("v", "s")
    vec.tensor_scalar(t["tpos"][:, :nb], t["lg"][:, :nb], 1.0 / LN2, 160.0, ALU.mult, ALU.add)
    p.vsync()
    i = emit_floor(p, t["s160"][:, :nb], t["tpos"][:, :nb], t["ii"][:, :nb],
                   t["ff"][:, :nb], t["gg"][:, :nb])
    p.tick("v", i)
    p.wait("s", "v")
    i = sca.activation(t["iscale"][:, :nb], t["s160"][:, :nb], ACT.Exp,
                       bias=162.0 * LN2, scale=-LN2)
    i = sca.activation(t["scale"][:, :nb], t["s160"][:, :nb], ACT.Exp,
                       bias=float(-162.0 * LN2 + np.log(out_scale)), scale=LN2)
    p.tick("s", i)
    p.wait("v", "s")
    y, sq = b[0][:, :H], b[1][:, :H]
    vec.tensor_tensor(y, src_ap, bcast(t["iscale"], 128, nb, 32), ALU.mult)
    vec.tensor_tensor(sq, y, y, ALU.mult)
    vec.tensor_scalar(b[2][:, :H], sq, 4.0, -(2.0 ** (mf - 1)), ALU.is_ge, ALU.mult)
    vec.tensor_scalar(b[3][:, :H], sq, 16.0, -(2.0 ** (mf - 2)), ALU.is_ge, ALU.mult)
    vec.scalar_tensor_tensor(b[4][:, :H], b[2][:, :H], 2.0 ** mf, b[3][:, :H], ALU.add, ALU.add)
    vec.tensor_scalar(b[3][:, :H], sq, 4.0, 2.0 ** (-mf), ALU.is_ge, ALU.mult)
    vec.tensor_scalar(b[5][:, :H], sq, 16.0, 2.0 ** (1 - mf), ALU.is_ge, ALU.mult)
    vec.scalar_tensor_tensor(b[1][:, :H], b[3][:, :H], 2.0 ** (-mf), b[5][:, :H], ALU.add, ALU.add)
    i = vec.tensor_tensor(b[2][:, :H], y, b[4][:, :H], ALU.mult)
    p.tick("v", i)
    p.wait("s", "v")
    sca.activation(ib[:, :H], b[2][:, :H], ACT.Copy, bias=64.5)
    i = sca.copy(b[0][:, :H], ib[:, :H])
    p.tick("s", i)
    p.wait("v", "s")
    vec.scalar_tensor_tensor(b[3][:, :H], b[2][:, :H], 64.5, b[0][:, :H], ALU.add, ALU.is_lt)
    vec.tensor_tensor(b[4][:, :H], b[0][:, :H], b[3][:, :H], ALU.subtract)
    vec.scalar_tensor_tensor(b[5][:, :H], b[4][:, :H], 64.0, b[1][:, :H], ALU.subtract, ALU.mult)
    vec.tensor_scalar(b[2][:, :H], b[5][:, :H], max_norm, -max_norm, ALU.min, ALU.max)
    i = vec.tensor_tensor(dst_bf, b[2][:, :H], bcast(t["scale"], 128, nb, 32), ALU.mult)
    p.tick("v", i)
    return i


def qmx_scratch2(nc, es, W, prefix="q"):
    nb = W // 32
    t = {}
    for i in range(6):
        t[f"b{i}"] = es.enter_context(nc.sbuf_tensor(f"{prefix}b{i}", [128, W], F32))
    t["ib"] = es.enter_context(nc.sbuf_tensor(prefix + "ib", [128, W], I32))
    for n in ("amax", "lg", "tpos", "ff", "gg", "s160", "iscale", "scale"):
        t[n] = es.enter_context(nc.sbuf_tensor(prefix + n, [128, nb], F32))
    t["ii"] = es.enter_context(nc.sbuf_tensor(prefix + "ii", [128, nb], I32))
    return t


def dma(p, key, dst, src, slow=False):
    eng = p.nc.sync if key == "q" else p.nc.scalar
    i = eng.dma_start(dst, src, allow_slow_non_contiguous=True) if slow else eng.dma_start(dst, src)
    p.cnt[key] += 16
    i.then_inc(p.sem[key], 16)
    return p.cnt[key]


def barrier(p):
    nc = p.nc
    for k in ("q", "a"):
        if p.cnt[k]:
            nc.sync.wait_ge(p.sem[k], p.cnt[k])
    for k in ("v", "s", "t", "g"):
        if p.cnt[k]:
            nc.sync.wait_ge(p.sem[k], p.cnt[k])
    nc.all_engine_barrier()


def emit_transpose_batch(p, src_fn, n, psum_bf, dst_ap, ident, wait_q=None):
    nc = p.nc
    if not isinstance(ident, bass.AP):
        ident = ident[:, :]
    for j in range(n):
        i = nc.tensor.transpose(psum_bf[:, j * 128:(j + 1) * 128], src_fn(j), ident)
    p.tick("t", i)
    p.wait("s", "t")
    if wait_q is not None:
        p.wait("s", "q", wait_q)
    i = nc.scalar.copy(dst_ap, psum_bf[:, :n * 128])
    p.tick("s", i)


def emit_ln(p, x_ap, W, xn_out, lnt, sq_scr):
    nc = p.nc
    vec, sca = nc.vector, nc.scalar
    sums, mcol, m2, vcol, lnv, rcol, mr = (lnt[:, i:i + 1] for i in range(7))
    i = vec.tensor_reduce(sums, x_ap, AX.X, ALU.add)
    p.tick("v", i)
    p.wait("s", "v")
    i = sca.activation(sq_scr, x_ap, ACT.Square, accum_out=lnt[:, 7:8])
    p.tick("s", i)
    p.vsync()
    vec.tensor_scalar(mcol, sums, 1.0 / W, None, ALU.mult)
    p.vsync()
    vec.tensor_tensor(m2, mcol, mcol, ALU.mult)
    p.wait("v", "s")
    p.vsync()
    i = vec.scalar_tensor_tensor(vcol, lnt[:, 7:8], 1.0 / W, m2, ALU.mult, ALU.subtract)
    p.tick("v", i)
    p.wait("s", "v")
    sca.activation(lnv, vcol, ACT.Ln, bias=1e-5)
    i = sca.activation(rcol, lnv, ACT.Exp, scale=-0.5)
    p.tick("s", i)
    p.wait("v", "s")
    vec.tensor_tensor(mr, mcol, rcol, ALU.mult)
    p.vsync()
    i = vec.tensor_scalar(xn_out, x_ap, rcol, mr, ALU.mult, ALU.subtract)
    p.tick("v", i)


def emit_ffn_weight_prep(p, w_in, IT, H, wqT_out, qt, ident, out_scale=1.0, dbg=None):
    nc = p.nc
    es = contextlib.ExitStack()
    U = _u(p)
    wsb = [es.enter_context(nc.sbuf_tensor(f"wp_wsb{i}" + U, [128, H], F32)) for i in range(2)]
    wq = [es.enter_context(nc.sbuf_tensor(f"wp_wq{i}" + U, [128, H], BF16)) for i in range(2)]
    stg = es.enter_context(nc.sbuf_tensor("wp_stg" + U, [128, 2048], BF16))
    pt = es.enter_context(nc.psum_tensor("wp_pt" + U, [128, 2048], BF16))
    NG = H // 2048 if H >= 2048 else 1
    CW = min(H, 2048)
    qdone = [0, 0]
    tdone = [0, 0]
    evac_s = 0
    outq = 0
    with es:
        dma(p, "q", wsb[0][:, :], bass.AP(w_in, 0, [[H, 128], [1, H]]))
        ld = [p.cnt["q"], 0]
        for it in range(IT):
            b = it % 2
            if it + 1 < IT:
                if it >= 1:
                    p.wait("q", "v", qdone[(it + 1) % 2])
                dma(p, "q", wsb[(it + 1) % 2][:, :],
                    bass.AP(w_in, (it + 1) * 128 * H, [[H, 128], [1, H]]))
                ld[(it + 1) % 2] = p.cnt["q"]
            p.wait("v", "q", ld[b])
            p.wait("s", "q", ld[b])
            if it >= 2:
                p.wait("v", "t", tdone[b])
            for c0 in range(0, H, CW):
                emit_qmx2(p, wsb[b][:, c0:c0 + CW], CW, wq[b][:, c0:c0 + CW], 1.0, qt,
                          out_scale=out_scale)
            qdone[b] = p.cnt["v"]
            for g0 in range(0, H // 128, 16):
                ng = min(16, H // 128 - g0)
                p.wait("t", "v", qdone[b])
                p.wait("t", "s", evac_s)
                emit_transpose_batch(
                    p, lambda j: wq[b][:, (g0 + j) * 128:(g0 + j + 1) * 128], ng,
                    pt, stg[:, :ng * 128], ident, wait_q=outq)
                evac_s = p.cnt["s"]
                p.wait("q", "s", evac_s)
                W2 = IT * 128
                dma(p, "q", bass.AP(wqT_out, (g0 * 128) * W2 + it * 128,
                                    [[W2, 128], [W2 * 128, ng], [1, 128]]),
                    stg[:, :ng * 128].rearrange("p (j c) -> p j c", j=ng))
                outq = p.cnt["q"]
            tdone[b] = p.cnt["t"]


def emit_plain_weight_prep(p, w_in, IT, H, wT_out, ident, tiled=False):
    nc = p.nc
    es = contextlib.ExitStack()
    U = _u(p)
    wsb = [es.enter_context(nc.sbuf_tensor(f"pp_wsb{i}" + U, [128, H], F32)) for i in range(2)]
    wq = [es.enter_context(nc.sbuf_tensor(f"pp_wq{i}" + U, [128, H], BF16)) for i in range(2)]
    stg = es.enter_context(nc.sbuf_tensor("pp_stg" + U, [128, 2048], BF16))
    pt = es.enter_context(nc.psum_tensor("pp_pt" + U, [128, 2048], BF16))
    NJ = H // 128
    W2 = IT * 128
    qdone = [0, 0]
    tdone = [0, 0]
    evac_s = 0
    outq = 0
    with es:
        dma(p, "q", wsb[0][:, :], bass.AP(w_in, 0, [[H, 128], [1, H]]))
        ld = [p.cnt["q"], 0]
        for it in range(IT):
            b = it % 2
            if it + 1 < IT:
                if it >= 1:
                    p.wait("q", "v", qdone[(it + 1) % 2])
                dma(p, "q", wsb[(it + 1) % 2][:, :],
                    bass.AP(w_in, (it + 1) * 128 * H, [[H, 128], [1, H]]))
                ld[(it + 1) % 2] = p.cnt["q"]
            p.wait("v", "q", ld[b])
            if it >= 2:
                p.wait("v", "t", tdone[b])
            i = nc.vector.tensor_copy(wq[b][:, :], wsb[b][:, :])
            p.tick("v", i)
            qdone[b] = p.cnt["v"]
            for g0 in range(0, NJ, 16):
                ng = min(16, NJ - g0)
                p.wait("t", "v", qdone[b])
                p.wait("t", "s", evac_s)
                emit_transpose_batch(
                    p, lambda j: wq[b][:, (g0 + j) * 128:(g0 + j + 1) * 128], ng,
                    pt, stg[:, :ng * 128], ident, wait_q=outq)
                evac_s = p.cnt["s"]
                p.wait("q", "s", evac_s)
                if tiled:
                    dma(p, "q", bass.AP(wT_out, (it * NJ + g0) * 16384,
                                        [[128, 128], [16384, ng], [1, 128]]),
                        stg[:, :ng * 128].rearrange("p (j c) -> p j c", j=ng))
                else:
                    dma(p, "q", bass.AP(wT_out, (g0 * 128) * W2 + it * 128,
                                        [[W2, 128], [W2 * 128, ng], [1, 128]]),
                        stg[:, :ng * 128].rearrange("p (j c) -> p j c", j=ng))
                outq = p.cnt["q"]
            tdone[b] = p.cnt["t"]


def _u(p):
    Prog._UID[0] += 1
    return f"_{Prog._UID[0]}"


def emit_ffn_stage(p, src, dst, wgT, wuT, wdT, interT, ident):
    nc = p.nc
    import contextlib
    T, H, I = 4096, 2048, 8192
    NT = T // 128
    TCH = 2048
    NTC = TCH // 128
    for tch in range(T // TCH):
        es = contextlib.ExitStack()
        U = _u(p)
        qt = qmx_scratch2(nc, es, 1024, prefix="ffq" + U)
        xn_qT = es.enter_context(nc.sbuf_tensor("ff_xnqT" + U, [128, 16 * TCH], BF16))
        with es:
            es1b = contextlib.ExitStack()
            xb = [es1b.enter_context(nc.sbuf_tensor(f"ff_xb{i}{U}", [128, H], F32)) for i in range(2)]
            xn = es1b.enter_context(nc.sbuf_tensor("ff_xn" + U, [128, H], F32))
            xq = es1b.enter_context(nc.sbuf_tensor("ff_xq" + U, [128, H], BF16))
            lnt = es1b.enter_context(nc.sbuf_tensor("ff_lnt" + U, [128, 8], F32))
            ptr = es1b.enter_context(nc.psum_tensor("ff_ptr" + U, [128, 2048], BF16))
            _noop = es1b
            with es1b:
                vdone = [0] * NTC
                for t in range(NTC):
                    tt = tch * NTC + t
                    if t >= 2:
                        p.wait("q", "v", vdone[t - 2])
                    dma(p, "q", xb[t % 2][:, :], bass.AP(src, tt * 128 * H, [[H, 128], [1, H]]))
                    p.wait("v", "q", p.cnt["q"])
                    p.wait("s", "q", p.cnt["q"])
                    if t >= 1:
                        p.wait("v", "t")
                    emit_ln(p, xb[t % 2][:, :], H, xn[:, :], lnt, xn[:, :])
                    for c0 in (0, 1024):
                        emit_qmx2(p, xn[:, c0:c0 + 1024], 1024, xq[:, c0:c0 + 1024], 3.0, qt)
                    vdone[t] = p.cnt["v"]
                    p.wait("t", "v")
                    if t > 0:
                        p.wait("t", "s")
                    emit_transpose_batch(
                        p, lambda j: xq[:, j * 128:(j + 1) * 128], 16, ptr,
                        bass.AP(xn_qT, t * 128, [[16 * TCH, 128], [TCH, 16], [1, 128]]),
                        ident)
                barrier(p)
            es2 = contextlib.ExitStack()
            U2 = _u(p)
            GW = 512
            NJ = GW // 128
            wg_sb = [es2.enter_context(nc.sbuf_tensor(f"ff_wg{i}{U2}", [128, 16 * GW], BF16)) for i in range(2)]
            wu_sb = [es2.enter_context(nc.sbuf_tensor(f"ff_wu{i}{U2}", [128, 16 * GW], BF16)) for i in range(2)]
            gbuf = es2.enter_context(nc.sbuf_tensor("ff_gbuf" + U2, [128, GW], F32))
            ubuf = es2.enter_context(nc.sbuf_tensor("ff_ubuf" + U2, [128, GW], F32))
            sgbuf = es2.enter_context(nc.sbuf_tensor("ff_sg" + U2, [128, GW], F32))
            e1b = es2.enter_context(nc.sbuf_tensor("ff_e1" + U2, [128, GW], F32))
            inb = es2.enter_context(nc.sbuf_tensor("ff_in" + U2, [128, GW], F32))
            iq = [es2.enter_context(nc.sbuf_tensor(f"ff_iq{i}{U2}", [128, GW], BF16)) for i in range(2)]
            itq = [es2.enter_context(nc.sbuf_tensor(f"ff_itq{i}{U2}", [128, GW], BF16)) for i in range(2)]
            psg = es2.enter_context(nc.psum_tensor("ff_psg" + U2, [128, 512], F32))
            psu = es2.enter_context(nc.psum_tensor("ff_psu" + U2, [128, 512], F32))
            ptr2 = es2.enter_context(nc.psum_tensor("ff_ptr2" + U2, [128, GW], BF16))
            NG = I // GW
            NK = NG * NTC

            def load_wslab(g, buf):
                for wT, sb in ((wgT, wg_sb[buf]), (wuT, wu_sb[buf])):
                    for j in range(16):
                        dma(p, "q", sb[:, j * GW:(j + 1) * GW],
                            bass.AP(wT, j * 128 * I + g * GW, [[I, 128], [1, GW]]))
                return p.cnt["q"]

            def emit_tr(k):
                p.wait("t", "v", qx_tick[k])
                if k >= 2:
                    p.wait("t", "s", ev_tick[k - 2])
                kk = k
                emit_transpose_batch(
                    p, lambda j: iq[kk % 2][:, j * 128:(j + 1) * 128], NJ, ptr2,
                    itq[kk % 2][:, :GW], ident,
                    wait_q=(qitq[k - 2] if k >= 2 else None))
                ev_tick[k] = p.cnt["s"]
                p.wait("q", "s", ev_tick[k])
                g_, t_ = divmod(k, NTC)
                tt_ = tch * NTC + t_
                dma(p, "q", bass.AP(interT, (tt_ * 64 + g_ * NJ) * 16384,
                                    [[128, 128], [16384, NJ], [1, 128]]),
                    itq[kk % 2][:, :GW].rearrange("p (j c) -> p j c", j=NJ))
                qitq[k] = p.cnt["q"]

            with es2:
                wready = [load_wslab(0, 0), 0]
                gdone = [0] * NG
                sc_tick = [0] * NK
                qx_tick = [0] * NK
                ev_tick = [0] * NK
                qitq = [0] * NK
                inb_tick = [0] * NK
                for g in range(NG):
                    if g + 1 < NG:
                        if g >= 1:
                            p.wait("q", "t", gdone[g - 1])
                        wready[(g + 1) % 2] = load_wslab(g + 1, (g + 1) % 2)
                    for t in range(NTC):
                        k = g * NTC + t
                        p.wait("t", "q", wready[g % 2])
                        p.wait("t", "s", sc_tick[k - 1] if k else 0)
                        for j in range(16):
                            nc.tensor.matmul(
                                psg[:, :],
                                bass.AP(xn_qT, j * TCH + t * 128, [[16 * TCH, 128], [1, 128]]),
                                wg_sb[g % 2][:, j * GW:j * GW + GW],
                                start=(j == 0), stop=(j == 15))
                        for j in range(16):
                            i = nc.tensor.matmul(
                                psu[:, :],
                                bass.AP(xn_qT, j * TCH + t * 128, [[16 * TCH, 128], [1, 128]]),
                                wu_sb[g % 2][:, j * GW:j * GW + GW],
                                start=(j == 0), stop=(j == 15))
                        p.tick("t", i)
                        mm_tick = p.cnt["t"]
                        gdone[g] = mm_tick
                        if k >= 1:
                            emit_tr(k - 1)
                        p.wait("s", "t", mm_tick)
                        if k >= 1:
                            p.wait("s", "v", inb_tick[k - 1])
                        nc.scalar.activation(sgbuf[:, :], psg[:, :], ACT.Silu)
                        i = nc.scalar.copy(ubuf[:, :], psu[:, :])
                        p.tick("s", i)
                        sc_tick[k] = p.cnt["s"]
                        p.wait("v", "s")
                        i = nc.vector.tensor_tensor(inb[:, :], sgbuf[:, :], ubuf[:, :], ALU.mult)
                        p.tick("v", i)
                        inb_tick[k] = p.cnt["v"]
                        if k >= 2:
                            p.wait("v", "t")
                        emit_qmx2(p, inb[:, :], GW, iq[k % 2][:, :], 3.0, qt)
                        qx_tick[k] = p.cnt["v"]
                emit_tr(NK - 1)
        barrier(p)
    es3 = contextlib.ExitStack()
    U3 = _u(p)
    wd_sb = es3.enter_context(nc.sbuf_tensor("ff_wd" + U3, [128, 64 * 1024], BF16))
    it_sb = [es3.enter_context(nc.sbuf_tensor(f"ff_it{i}{U3}", [128, 64 * 128], BF16)) for i in range(2)]
    xs = [es3.enter_context(nc.sbuf_tensor(f"ff_xs{i}{U3}", [128, 1024], F32)) for i in range(2)]
    outb = [es3.enter_context(nc.sbuf_tensor(f"ff_ob{i}{U3}", [128, 1024], F32)) for i in range(2)]
    pso = es3.enter_context(nc.psum_tensor("ff_pso" + U3, [128, 1024], F32))
    with es3:
        for half in range(2):
            h0 = half * 1024
            if half == 1:
                p.wait("q", "t")
            for j in range(64):
                dma(p, "q", wd_sb[:, j * 1024:(j + 1) * 1024],
                    bass.AP(wdT, j * 128 * H + h0, [[H, 128], [1, 1024]]))
            wdready = p.cnt["q"]
            tmm = [0] * NT
            vev = [0] * NT
            qout = [0] * NT
            for t in range(NT):
                bi = t % 2
                if t >= 2:
                    p.wait("q", "t", tmm[t - 2])
                    p.wait("q", "v", vev[t - 2])
                dma(p, "q", it_sb[bi][:, :],
                    bass.AP(interT, t * 64 * 16384, [[128, 128], [16384, 64], [1, 128]]))
                dma(p, "q", xs[bi][:, :],
                    bass.AP(src, t * 128 * H + h0, [[H, 128], [1, 1024]]))
                p.wait("t", "q", max(p.cnt["q"], wdready))
                if t >= 1:
                    p.wait("t", "v", vev[t - 1])
                for j in range(64):
                    nc.tensor.matmul(pso[:, 0:512], it_sb[bi][:, j * 128:(j + 1) * 128],
                                     wd_sb[:, j * 1024:j * 1024 + 512],
                                     start=(j == 0), stop=(j == 63))
                    i = nc.tensor.matmul(pso[:, 512:1024], it_sb[bi][:, j * 128:(j + 1) * 128],
                                         wd_sb[:, j * 1024 + 512:j * 1024 + 1024],
                                         start=(j == 0), stop=(j == 63))
                p.tick("t", i)
                tmm[t] = p.cnt["t"]
                p.wait("v", "t")
                if t >= 2:
                    p.wait("v", "q", qout[t - 2])
                i = nc.vector.tensor_tensor(outb[bi][:, :], pso[:, :], xs[bi][:, :], ALU.add)
                p.tick("v", i)
                vev[t] = p.cnt["v"]
                p.wait("q", "v", vev[t])
                dma(p, "q", bass.AP(dst, t * 128 * H + h0, [[H, 128], [1, 1024]]),
                    outb[bi][:, :])
                qout[t] = p.cnt["q"]
    barrier(p)


"""Mamba-2 SSD stage emitter for the macaron megakernel.

Stage B: dst = src + Mamba(LN(src)) per core (one sequence of 4096 tokens).
4096/CH chunks of CH tokens, carrying conv halo (3 cols) and SSD state across chunks.
"""

CH = 256
NTT = CH // 128
NBL = CH // 64
NOT = 67


def emit_sigma_psum(p, src_ap, dst, e1, e2, mul_with=None, bias_col=None):
    nc = p.nc
    if bias_col is None:
        i = nc.scalar.activation(e1, src_ap, ACT.Exp, scale=-1.0)
    else:
        i = nc.scalar.activation(e1, src_ap, ACT.Exp, scale=-1.0, bias=bias_col)
    p.tick("s", i)
    p.wait("v", "s")
    i = nc.vector.tensor_scalar(e2, e1, 1e18, None, ALU.min)
    p.tick("v", i)
    p.wait("s", "v")
    nc.scalar.activation(e1, e2, ACT.Ln, bias=1.0)
    i = nc.scalar.activation(e2, e1, ACT.Exp, scale=-1.0)
    p.tick("s", i)
    p.wait("v", "s")
    i = nc.vector.tensor_tensor(dst, mul_with, e2, ALU.mult)
    p.tick("v", i)
    return i


def emit_mamba_stage(p, src, dst, winT_t, woT, params, cm):
    nc = p.nc
    ident, identf = cm["ident"], cm["identf"]
    L1, M1m, ones1, onescol, Ieye64 = cm["L1"], cm["M1m"], cm["ones1"], cm["onescol"], cm["Ieye64"]
    onesf = cm["onesf"]
    es = contextlib.ExitStack()
    U = _u(p)
    T = lambda n, s, d=F32: es.enter_context(nc.sbuf_tensor(n + U, s, d))
    Sf = [T(f"mb_Sf{i}", [128, 4096]) for i in range(2)]
    Sbf = T("mb_Sbf", [128, 4096], BF16)
    convw = T("mb_convw", [128, 136])
    convb = T("mb_convb", [128, 34])
    convbn = T("mb_convbn", [128, 34])
    dtb = T("mb_dtb", [64, 1])
    acol = T("mb_acol", [64, 1])
    gcol = T("mb_gcol", [128, 32])
    halo = T("mb_halo", [128, 102])
    ddiag = T("mb_ddiag", [64, 4096], BF16)
    drow = T("mb_drow", [1, 64])
    drowb = T("mb_drowb", [1, 64], BF16)
    dmat_s = T("mb_dmat", [64, 64])
    with es:
        dma(p, "q", convw[:, :], bass.AP(params["conv_w"], 0, [[4, 128], [512, 34], [1, 4]]))
        dma(p, "q", convb[:, :34], bass.AP(params["conv_b"], 0, [[1, 128], [128, 34]]), slow=True)
        dma(p, "q", dtb[:, :], bass.AP(params["dt_bias"], 0, [[1, 64], [1, 1]]), slow=True)
        dma(p, "q", acol[:, :], bass.AP(params["A_log"], 0, [[1, 64], [1, 1]]), slow=True)
        dma(p, "q", drow[:, :], bass.AP(params["Dv"], 0, [[0, 1], [1, 64]]))
        dma(p, "q", gcol[:, :], bass.AP(params["norm_w"], 0, [[1, 128], [128, 32]]), slow=True)
        p.wait("v", "q", p.cnt["q"])
        p.wait("s", "q", p.cnt["q"])
        nc.vector.tensor_scalar(convbn[:, :34], convb[:, :34], -1.0, None, ALU.mult)
        nc.vector.memset(halo[:, :], 0.0)
        nc.vector.memset(Sf[0][:, :], 0.0)
        i = nc.vector.memset(Sbf[:, :], 0.0)
        p.tick("v", i)
        i = nc.scalar.activation(acol[:, :], acol[:, :], ACT.Exp)
        p.tick("s", i)
        p.wait("v", "s")
        p.vsync()
        i = nc.vector.tensor_scalar(acol[:, :], acol[:, :], -1.0, None, ALU.mult)
        p.tick("v", i)
        p.wait("s", "v")
        i = nc.scalar.copy(drowb[:, :], drow[:, :])
        p.tick("s", i)
        p.wait("t", "s")
        with nc.psum_tensor("mb_pdm" + U, [64, 64], F32) as pdm:
            i = nc.tensor.matmul(pdm[:, :], ones1[:1, :64], drowb[:, :], start=True, stop=True)
            p.tick("t", i)
            p.wait("v", "t")
            i = nc.vector.tensor_copy(dmat_s[:, :], pdm[:, :])
            p.tick("v", i)
            p.vsync()
            i = nc.vector.tensor_tensor(
                ddiag[:, :],
                bass.AP(dmat_s, 0, [[64, 64], [1, 64], [0, 64]]),
                bass.AP(Ieye64, 0, [[64, 64], [0, 64], [1, 64]]),
                ALU.mult)
            p.tick("v", i)
        barrier(p)

        for ch in range(4096 // CH):
            esC = contextlib.ExitStack()
            UC = _u(p)
            TC = lambda n, s, d=F32: esC.enter_context(nc.sbuf_tensor(n + UC, s, d))
            y3 = TC("mb_y3", [128, 32 * CH], BF16)
            rT = TC("mb_rT", [128, NTT])
            sz = TC("mb_sz", [128, 32 * CH], BF16)
            xbcs = TC("mb_xbcs", [128, 34 * CH], BF16)
            ext = TC("mb_ext", [128, CH + 3])
            accb = TC("mb_accb", [128, CH])
            e1 = TC("mb_e1", [128, CH])
            e2 = TC("mb_e2", [128, CH])
            dtT = TC("mb_dtT", [64, NBL * 64])
            adT = TC("mb_adT", [64, NBL * 64])
            BT = TC("mb_BT", [64, NBL * 128], BF16)
            with esC:
                esA = contextlib.ExitStack()
                TA = lambda n, s, d=F32: esA.enter_context(nc.sbuf_tensor(n + UC + "A", s, d))
                xc = [TA(f"mb_xc{i}", [128, 2048]) for i in range(2)]
                lnt = TA("mb_lnt", [128, 8])
                xnbf = TA("mb_xnbf", [128, 2048], BF16)
                xnT = TA("mb_xnT", [128, 16 * CH], BF16)
                wo_sb = [TA(f"mb_wo{i}", [128, 16 * 128], BF16) for i in range(2)]
                dtx = ext[0:64, 0:CH]
                dtv = accb[0:64, 0:CH]
                adv = e1[0:64, 0:CH]
                with esA:
                    vdone = [0, 0]
                    with nc.psum_tensor("mb_ptr" + UC, [128, 2048], BF16) as ptr:
                        for tt in range(NTT):
                            if tt >= 2:
                                p.wait("q", "v", vdone[tt % 2])
                            dma(p, "q", xc[tt % 2][:, :],
                                bass.AP(src, (ch * NTT + tt) * 128 * 2048, [[2048, 128], [1, 2048]]))
                            p.wait("v", "q", p.cnt["q"])
                            p.wait("s", "q", p.cnt["q"])
                            if tt > 0:
                                p.wait("v", "t")
                            emit_ln(p, xc[tt % 2][:, :], 2048, xnbf[:, :], lnt,
                                    y3[:, 0:2048])
                            vdone[tt % 2] = p.cnt["v"]
                            p.wait("t", "v")
                            if tt > 0:
                                p.wait("t", "s")
                            emit_transpose_batch(
                                p, lambda j: xnbf[:, j * 128:(j + 1) * 128], 16, ptr,
                                bass.AP(xnT, tt * 128, [[16 * CH, 128], [CH, 16], [1, 128]]),
                                ident)
                    def load_wot(ot, buf):
                        dma(p, "q", wo_sb[buf][:, :],
                            bass.AP(winT_t, ot * 16 * 16384, [[128, 128], [16384, 16], [1, 128]]))
                        return p.cnt["q"]
                    with (nc.psum_tensor("mb_po0" + UC, [128, CH], F32) as po0,
                          nc.psum_tensor("mb_po1" + UC, [128, CH], F32) as po1,
                          nc.psum_tensor("mb_pt2" + UC, [128, 64], F32) as pt2):
                        pos = [po0, po1]
                        p.wait("t", "s")
                        wready = [load_wot(0, 0), 0]
                        lastuse = [0, 0]
                        ev_v = [0, 0]
                        ev_s = [0, 0]
                        for ot in range(NOT):
                            po = pos[ot % 2]
                            if ot + 1 < NOT:
                                if ot >= 1:
                                    p.wait("q", "t", lastuse[(ot + 1) % 2])
                                wready[(ot + 1) % 2] = load_wot(ot + 1, (ot + 1) % 2)
                            p.wait("t", "q", wready[ot % 2])
                            if ot >= 2:
                                p.wait("t", "v", ev_v[ot % 2])
                                p.wait("t", "s", ev_s[ot % 2])
                            for j in range(16):
                                i = nc.tensor.matmul(
                                    po[:, :], wo_sb[ot % 2][:, j * 128:(j + 1) * 128],
                                    xnT[:, j * CH:j * CH + CH],
                                    start=(j == 0), stop=(j == 15))
                            p.tick("t", i)
                            lastuse[ot % 2] = p.cnt["t"]
                            p.wait("s", "t")
                            p.wait("v", "t")
                            if ot < 32:
                                i = nc.scalar.activation(sz[:, ot * CH:(ot + 1) * CH],
                                                         po[:, :], ACT.Silu)
                                p.tick("s", i)
                                ev_v[ot % 2] = p.cnt["v"]
                                ev_s[ot % 2] = p.cnt["s"]
                            elif ot < 66:
                                c = ot - 32
                                nc.vector.tensor_copy(ext[:, 0:3], halo[:, c * 3:c * 3 + 3])
                                i = nc.scalar.copy(ext[:, 3:CH + 3], po[:, :])
                                p.tick("s", i)
                                ev_s[ot % 2] = p.cnt["s"]
                                ev_v[ot % 2] = p.cnt["v"]
                                p.wait("v", "s")
                                p.vsync()
                                i = nc.vector.tensor_copy(halo[:, c * 3:c * 3 + 3], ext[:, CH:CH + 3])
                                p.tick("v", i)
                                p.vsync()
                                nc.vector.tensor_scalar(accb[:, :], ext[:, 0:CH],
                                                        convw[:, c * 4:c * 4 + 1], None, ALU.mult)
                                for k in (1, 2, 3):
                                    i = nc.vector.scalar_tensor_tensor(
                                        accb[:, :], ext[:, k:k + CH], convw[:, c * 4 + k:c * 4 + k + 1],
                                        accb[:, :], ALU.mult, ALU.add)
                                p.tick("v", i)
                                p.wait("s", "v")
                                i = nc.scalar.activation(xbcs[:, c * CH:(c + 1) * CH],
                                                         accb[:, :], ACT.Silu,
                                                         bias=convb[:, c:c + 1])
                                p.tick("s", i)
                            else:
                                i = nc.vector.tensor_scalar(dtx, po[0:64, :], dtb[:, 0:1],
                                                            None, ALU.add)
                                p.tick("v", i)
                                ev_v[ot % 2] = p.cnt["v"]
                                ev_s[ot % 2] = p.cnt["s"]
                                p.wait("s", "v")
                                nc.scalar.activation(e2[0:64, :], dtx, ACT.Abs)
                                nc.scalar.activation(e2[0:64, :], e2[0:64, :], ACT.Exp, scale=-1.0)
                                i = nc.scalar.activation(e2[0:64, :], e2[0:64, :], ACT.Ln, bias=1.0)
                                p.tick("s", i)
                                p.wait("v", "s")
                                nc.vector.scalar_tensor_tensor(dtv, dtx, 0.0,
                                                               e2[0:64, :], ALU.max, ALU.add)
                                p.vsync()
                                i = nc.vector.tensor_scalar(adv, dtv, acol[:, 0:1],
                                                            None, ALU.mult)
                                p.tick("v", i)
                                p.wait("t", "v")
                                for tt in range(NTT):
                                    i = nc.tensor.transpose(pt2[0:128, :], dtv[:, tt * 128:(tt + 1) * 128],
                                                            identf[0:64, 0:64])
                                    p.tick("t", i)
                                    p.wait("s", "t")
                                    nc.scalar.copy(dtT[:, (2 * tt) * 64:(2 * tt + 1) * 64], pt2[0:64, :])
                                    i = nc.scalar.copy(dtT[:, (2 * tt + 1) * 64:(2 * tt + 2) * 64], pt2[64:128, :])
                                    p.tick("s", i)
                                    p.wait("t", "s")
                                    i = nc.tensor.transpose(pt2[0:128, :], adv[:, tt * 128:(tt + 1) * 128],
                                                            identf[0:64, 0:64])
                                    p.tick("t", i)
                                    p.wait("s", "t")
                                    nc.scalar.copy(adT[:, (2 * tt) * 64:(2 * tt + 1) * 64], pt2[0:64, :])
                                    i = nc.scalar.copy(adT[:, (2 * tt + 1) * 64:(2 * tt + 2) * 64], pt2[64:128, :])
                                    p.tick("s", i)
                                    p.wait("t", "s")
                    with nc.psum_tensor("mb_ptb" + UC, [128, NTT * 128], BF16) as ptb:
                        p.wait("t", "v")
                        p.wait("t", "s")
                        for tt in range(NTT):
                            i = nc.tensor.transpose(ptb[:, tt * 128:(tt + 1) * 128],
                                                    xbcs[:, 32 * CH + tt * 128:32 * CH + (tt + 1) * 128],
                                                    ident[:, :])
                        p.tick("t", i)
                        p.wait("s", "t")
                        for tt in range(NTT):
                            nc.scalar.copy(BT[:, (2 * tt) * 128:(2 * tt + 1) * 128],
                                           ptb[0:64, tt * 128:(tt + 1) * 128])
                            i = nc.scalar.copy(BT[:, (2 * tt + 1) * 128:(2 * tt + 2) * 128],
                                               ptb[64:128, tt * 128:(tt + 1) * 128])
                        p.tick("s", i)
                    barrier(p)
                esB = contextlib.ExitStack()
                TB = lambda n, s, d=F32: esB.enter_context(nc.sbuf_tensor(n + UC + "S", s, d))
                XTb = TB("mb_XTb", [64, 4096], BF16)
                rhsD = TB("mb_rhsD", [64, 2048])
                rhsH = TB("mb_rhsH", [64, 2048], BF16)
                rhsL = TB("mb_rhsL", [64, 2048], BF16)
                Mh = TB("mb_Mh", [64, 4096], BF16)
                Gm = TB("mb_Gm", [64, 64], BF16)
                dt63 = TB("mb_dt63", [64, 64])
                en1 = TB("mb_en1", [64, 64])
                decdtT = TB("mb_decdt", [64, 64])
                expAcT = TB("mb_expAcT", [64, 64])
                lam = TB("mb_lam", [1, 64])
                lamlogm = TB("mb_lamlogm", [128, 64])
                lamm = TB("mb_lamm", [128, 64])
                Bd = TB("mb_Bd", [64, 2048], BF16)
                Yb = TB("mb_Yb", [64, 4096], BF16)
                y2b = TB("mb_y2", [128, 1024])
                sq16 = TB("mb_sq16", [128, 1024], BF16)
                with esB:
                    for b in range(NBL):
                        dtTb = dtT[0:64, b * 64:(b + 1) * 64]
                        adTb = adT[0:64, b * 64:(b + 1) * 64]
                        Cblk = xbcs[:, 33 * CH + b * 64:33 * CH + (b + 1) * 64]
                        Bblk = xbcs[:, 32 * CH + b * 64:32 * CH + (b + 1) * 64]
                        with nc.psum_tensor(f"mb_pxt{b}" + UC, [64, 2048], BF16) as pxt:
                            for half in range(2):
                                p.wait("t", "s")
                                p.wait("t", "v")
                                for cj in range(16):
                                    ct = half * 16 + cj
                                    i = nc.tensor.transpose(
                                        pxt[:, cj * 128:(cj + 1) * 128],
                                        xbcs[:, ct * CH + b * 64:ct * CH + (b + 1) * 64],
                                        ident[:, :])
                                p.tick("t", i)
                                p.wait("s", "t")
                                i = nc.scalar.copy(XTb[:, half * 2048:(half + 1) * 2048],
                                                   pxt[:, :])
                                p.tick("s", i)
                        with (nc.psum_tensor(f"mb_pG{b}" + UC, [64, 64], F32) as pG,
                              nc.psum_tensor(f"mb_pD{b}" + UC, [64, 2048], F32) as pD):
                            p.wait("t", "v")
                            p.wait("t", "s")
                            i = nc.tensor.matmul(pG[:, :], Bblk, Cblk, start=True, stop=True)
                            p.tick("t", i)
                            p.wait("v", "t")
                            i = nc.vector.tensor_tensor(Gm[:, :], pG[:, :], M1m[:, :], ALU.mult)
                            p.tick("v", i)
                            for hg in range(2):
                                p.wait("v", "s")
                                p.vsync()
                                nc.vector.tensor_tensor(
                                    rhsD[:, :],
                                    bass.AP(adT, b * 64 + hg * 32, [[NBL * 64, 64], [1, 32], [0, 64]]),
                                    bass.AP(M1m, 0, [[64, 64], [0, 32], [1, 64]]),
                                    ALU.mult)
                                nc.vector.tensor_copy(rhsH[:, :], rhsD[:, :])
                                i = nc.vector.tensor_tensor(rhsL[:, :], rhsD[:, :], rhsH[:, :], ALU.subtract)
                                p.tick("v", i)
                                p.wait("t", "v")
                                p.wait("t", "s")
                                for seg in range(4):
                                    nc.tensor.matmul(pD[:, seg * 512:(seg + 1) * 512],
                                                     L1[:, :], rhsH[:, seg * 512:(seg + 1) * 512],
                                                     start=True, stop=False)
                                    i = nc.tensor.matmul(pD[:, seg * 512:(seg + 1) * 512],
                                                         L1[:, :], rhsL[:, seg * 512:(seg + 1) * 512],
                                                         start=False, stop=True)
                                p.tick("t", i)
                                p.wait("s", "t")
                                i = nc.scalar.activation(Mh[:, hg * 2048:(hg + 1) * 2048],
                                                         pD[:, :], ACT.Exp)
                                p.tick("s", i)
                                p.wait("v", "s")
                                i = nc.vector.tensor_copy(
                                    dt63[:, hg * 32:hg * 32 + 32],
                                    bass.AP(pD, 63, [[2048, 64], [64, 32], [1, 1]]))
                                p.tick("v", i)
                                p.vsync()
                                nc.vector.tensor_tensor(
                                    Mh[:, hg * 2048:(hg + 1) * 2048],
                                    Mh[:, hg * 2048:(hg + 1) * 2048],
                                    bass.AP(Gm, 0, [[64, 64], [0, 32], [1, 64]]),
                                    ALU.mult)
                                i = nc.vector.tensor_tensor(
                                    Mh[:, hg * 2048:(hg + 1) * 2048],
                                    Mh[:, hg * 2048:(hg + 1) * 2048],
                                    bass.AP(dtT, b * 64 + hg * 32, [[NBL * 64, 64], [1, 32], [0, 64]]),
                                    ALU.mult)
                                p.tick("v", i)
                        p.vsync()
                        i = nc.vector.tensor_tensor(lam[0:1, :], dt63[0:1, :], adTb[0:1, :], ALU.add)
                        p.tick("v", i)
                        p.wait("s", "v")
                        i = nc.scalar.activation(decdtT[:, :], dt63[:, :], ACT.Exp)
                        p.tick("s", i)
                        p.wait("t", "v")
                        with nc.psum_tensor(f"mb_pL{b}" + UC, [128, 64], F32) as pL:
                            i = nc.tensor.matmul(pL[:, :], onesf[0:1, :], lam[0:1, :],
                                                 start=True, stop=True)
                            p.tick("t", i)
                            p.wait("v", "t")
                            i = nc.vector.tensor_copy(lamlogm[:, :], pL[:, :])
                            p.tick("v", i)
                        p.vsync()
                        nc.vector.tensor_tensor(en1[:, :], lamlogm[0:64, :], dt63[:, :], ALU.subtract)
                        i = nc.vector.tensor_tensor(decdtT[:, :], decdtT[:, :], dtTb, ALU.mult)
                        p.tick("v", i)
                        p.wait("s", "v")
                        nc.scalar.activation(expAcT[:, :], en1[:, :], ACT.Exp)
                        i = nc.scalar.activation(lamm[:, :], lamlogm[:, :], ACT.Exp)
                        p.tick("s", i)
                        p.wait("v", "s")
                        p.wait("t", "s")
                        for hg in range(4):
                            i = nc.vector.tensor_tensor(
                                Bd[:, :],
                                bass.AP(BT, b * 128, [[NBL * 128, 64], [0, 16], [1, 128]]),
                                bass.AP(decdtT, hg * 16, [[64, 64], [1, 16], [0, 128]]),
                                ALU.mult)
                            p.tick("v", i)
                            with (nc.psum_tensor(f"mb_pYd{b}_{hg}" + UC, [64, 1024], F32) as pYd,
                                  nc.psum_tensor(f"mb_pYo{b}_{hg}" + UC, [64, 1024], F32) as pYo,
                                  nc.psum_tensor(f"mb_pS{b}_{hg}" + UC, [128, 1024], F32) as pS):
                                p.wait("t", "v")
                                p.wait("t", "s")
                                for seg in range(2):
                                    nc.tensor.matmul(
                                        pYo[:, seg * 512:(seg + 1) * 512], Cblk,
                                        Sbf[:, hg * 1024 + seg * 512:hg * 1024 + (seg + 1) * 512],
                                        start=True, stop=True)
                                for hl in range(16):
                                    h = hg * 16 + hl
                                    xsl = XTb[:, h * 64:(h + 1) * 64]
                                    nc.tensor.matmul(pYd[:, hl * 64:(hl + 1) * 64],
                                                     Mh[:, h * 64:(h + 1) * 64], xsl,
                                                     start=True, stop=False)
                                    nc.tensor.matmul(pYd[:, hl * 64:(hl + 1) * 64],
                                                     ddiag[:, h * 64:(h + 1) * 64], xsl,
                                                     start=False, stop=True)
                                    i = nc.tensor.matmul(pS[:, hl * 64:(hl + 1) * 64],
                                                         Bd[:, hl * 128:(hl + 1) * 128], xsl,
                                                         start=True, stop=True)
                                p.tick("t", i)
                                p.wait("v", "t")
                                so = Sf[(ch * NBL + b) % 2]
                                sn = Sf[(ch * NBL + b + 1) % 2]
                                nc.vector.tensor_tensor(
                                    sn[:, hg * 1024:(hg + 1) * 1024],
                                    so[:, hg * 1024:(hg + 1) * 1024],
                                    bass.AP(lamm, hg * 16, [[64, 128], [1, 16], [0, 64]]),
                                    ALU.mult)
                                nc.vector.tensor_tensor(
                                    sn[:, hg * 1024:(hg + 1) * 1024],
                                    sn[:, hg * 1024:(hg + 1) * 1024],
                                    pS[:, :], ALU.add)
                                nc.vector.tensor_tensor(
                                    y2b[0:64, :], pYo[:, :],
                                    bass.AP(expAcT, hg * 16, [[64, 64], [1, 16], [0, 64]]),
                                    ALU.mult)
                                i = nc.vector.tensor_tensor(
                                    Yb[:, hg * 1024:(hg + 1) * 1024],
                                    y2b[0:64, :], pYd[:, :], ALU.add)
                                p.tick("v", i)
                        sn = Sf[(ch * NBL + b + 1) % 2]
                        i = nc.vector.tensor_copy(Sbf[:, :], sn[:, :])
                        p.tick("v", i)
                        with (nc.psum_tensor(f"mb_pyt{b}" + UC, [128, 1024], BF16) as pyt,
                              nc.psum_tensor(f"mb_prm{b}" + UC, [1, 64], F32) as prm,
                              nc.psum_tensor(f"mb_prt{b}" + UC, [64, 1], F32) as prt):
                            for half in range(2):
                                p.wait("t", "v")
                                for cj in range(16):
                                    ct = half * 16 + cj
                                    i = nc.tensor.transpose(
                                        pyt[:, cj * 64:(cj + 1) * 64],
                                        Yb[:, ct * 128:(ct + 1) * 128],
                                        ident[0:64, 0:64])
                                p.tick("t", i)
                                p.wait("v", "t")
                                nc.vector.tensor_tensor(
                                    y2b[:, :], pyt[:, :],
                                    bass.AP(sz, half * 16 * CH + b * 64,
                                            [[32 * CH, 128], [CH, 16], [1, 64]]),
                                    ALU.mult)
                                i = nc.vector.tensor_tensor(
                                    bass.AP(y3, half * 16 * CH + b * 64,
                                            [[32 * CH, 128], [CH, 16], [1, 64]]),
                                    y2b[:, :],
                                    bass.AP(gcol, half * 16, [[32, 128], [1, 16], [0, 64]]),
                                    ALU.mult)
                                p.tick("v", i)
                                p.wait("s", "v")
                                i = nc.scalar.activation(sq16[:, :], y2b[:, :], ACT.Square)
                                p.tick("s", i)
                                p.wait("t", "s")
                                for cj in range(16):
                                    i = nc.tensor.matmul(prm[:, :], onescol[:, 0:1],
                                                         sq16[:, cj * 64:(cj + 1) * 64],
                                                         start=(half == 0 and cj == 0),
                                                         stop=(half == 1 and cj == 15))
                                p.tick("t", i)
                            p.wait("s", "t")
                            nc.scalar.activation(e1[0:1, 0:64], prm[:, :], ACT.Ln,
                                                 scale=1.0 / 4096.0, bias=1e-5)
                            i = nc.scalar.activation(lam[0:1, :], e1[0:1, 0:64], ACT.Exp,
                                                     scale=-0.5)
                            p.tick("s", i)
                            p.wait("t", "s")
                            i = nc.tensor.matmul(prt[:, :], lam[0:1, :], onesf[0:1, 0:1],
                                                 start=True, stop=True)
                            p.tick("t", i)
                            p.wait("v", "t")
                            tt0 = b // 2
                            pl = (b % 2) * 64
                            i = nc.vector.tensor_copy(rT[pl:pl + 64, tt0:tt0 + 1], prt[:, :])
                            p.tick("v", i)
                    barrier(p)
                esD = contextlib.ExitStack()
                TD = lambda n, s, d=F32: esD.enter_context(nc.sbuf_tensor(n + UC + "D", s, d))
                wos = [TD(f"mb_wos{i}", [128, 2048], BF16) for i in range(2)]
                xs2 = [TD(f"mb_xs{i}", [128, 2048]) for i in range(2)]
                ob = [TD(f"mb_ob{i}", [128, 2048]) for i in range(2)]
                with esD:
                    with (nc.psum_tensor("mb_pout0" + UC, [128, 2048], F32) as pout0,
                          nc.psum_tensor("mb_pout1" + UC, [128, 2048], F32) as pout1):
                        pouts = [pout0, pout1]
                        for tt in range(NTT):
                            dma(p, "q", xs2[tt][:, :],
                                bass.AP(src, (ch * NTT + tt) * 128 * 2048, [[2048, 128], [1, 2048]]))
                        def load_wos(cj, buf):
                            dma(p, "q", wos[buf][:, :],
                                bass.AP(woT, cj * 128 * 2048, [[2048, 128], [1, 2048]]))
                            return p.cnt["q"]
                        wready = [load_wos(0, 0), 0]
                        lastw = [0, 0]
                        for cj in range(32):
                            if cj + 1 < 32:
                                if cj >= 1:
                                    p.wait("q", "t", lastw[(cj + 1) % 2])
                                wready[(cj + 1) % 2] = load_wos(cj + 1, (cj + 1) % 2)
                            p.wait("t", "q", wready[cj % 2])
                            if cj == 0:
                                p.wait("t", "v")
                            for tt in range(NTT):
                                for dch in range(4):
                                    i = nc.tensor.matmul(
                                        pouts[tt][:, dch * 512:(dch + 1) * 512],
                                        y3[:, cj * CH + tt * 128:cj * CH + (tt + 1) * 128],
                                        wos[cj % 2][:, dch * 512:(dch + 1) * 512],
                                        start=(cj == 0), stop=(cj == 31))
                            p.tick("t", i)
                            lastw[cj % 2] = p.cnt["t"]
                        p.wait("v", "t")
                        for tt in range(NTT):
                            i = nc.vector.scalar_tensor_tensor(
                                ob[tt][:, :], pouts[tt][:, :], rT[:, tt:tt + 1],
                                xs2[tt][:, :], ALU.mult, ALU.add)
                            p.tick("v", i)
                            p.wait("q", "v")
                            dma(p, "q", bass.AP(dst, (ch * NTT + tt) * 128 * 2048,
                                                [[2048, 128], [1, 2048]]),
                                ob[tt][:, :])
    barrier(p)
    barrier(p)


import contextlib
from concourse.bass_utils import run_bass_kernel_spmd


def _build_program():
    nc = bass.Bass("TRN2", target_bir_lowering=False, detect_race_conditions=False)
    p = Prog(nc)
    T, H, I = 4096, 2048, 8192
    x_in = nc.dram_tensor("x", [T, H], F32, kind="ExternalInput")
    wg_in = nc.dram_tensor("w_gate", [I, H], F32, kind="ExternalInput")
    wu_in = nc.dram_tensor("w_up", [I, H], F32, kind="ExternalInput")
    wd_in = nc.dram_tensor("w_down", [H, I], F32, kind="ExternalInput")
    wip = nc.dram_tensor("in_proj_pad", [8576, H], F32, kind="ExternalInput")
    wop = nc.dram_tensor("out_proj", [H, 4096], F32, kind="ExternalInput")
    cw_in = nc.dram_tensor("conv_w", [4352, 4], F32, kind="ExternalInput")
    cb_in = nc.dram_tensor("conv_b", [4352], F32, kind="ExternalInput")
    al_in = nc.dram_tensor("A_log", [64], F32, kind="ExternalInput")
    dv_in = nc.dram_tensor("Dv", [64], F32, kind="ExternalInput")
    dtb_in = nc.dram_tensor("dt_bias", [64], F32, kind="ExternalInput")
    nw_in = nc.dram_tensor("norm_w", [4096], F32, kind="ExternalInput")
    c_in = nc.dram_tensor("consts", [128, 64], F32, kind="ExternalInput")
    id_in = nc.dram_tensor("ident", [128, 128], BF16, kind="ExternalInput")
    idf_in = nc.dram_tensor("identf", [128, 128], F32, kind="ExternalInput")
    L1_in = nc.dram_tensor("L1", [64, 64], BF16, kind="ExternalInput")
    M1_in = nc.dram_tensor("M1m", [64, 64], BF16, kind="ExternalInput")
    o1_in = nc.dram_tensor("ones1", [1, 128], BF16, kind="ExternalInput")
    oc_in = nc.dram_tensor("onescol", [128, 1], BF16, kind="ExternalInput")
    ie_in = nc.dram_tensor("Ieye64", [64, 64], BF16, kind="ExternalInput")
    of_in = nc.dram_tensor("onesf", [1, 128], F32, kind="ExternalInput")
    out = nc.dram_tensor("out", [T, H], F32, kind="ExternalOutput")
    wgT = nc.dram_tensor("wgT", [H, I], BF16)
    wuT = nc.dram_tensor("wuT", [H, I], BF16)
    wdT = nc.dram_tensor("wdT", [I, H], BF16)
    interT = nc.dram_tensor("interT", [64 * 32 * 16384], BF16)
    winT_t = nc.dram_tensor("winT_t", [67 * 16 * 16384], BF16)
    woT = nc.dram_tensor("woT", [4096, 2048], BF16)
    resid1 = nc.dram_tensor("resid1", [T, H], F32)
    resid2 = nc.dram_tensor("resid2", [T, H], F32)

    es0 = contextlib.ExitStack()
    p.alloc_consts(es0)
    p.alloc_sems(es0)
    identt = es0.enter_context(nc.sbuf_tensor("identt", [128, 128], BF16))
    identf = es0.enter_context(nc.sbuf_tensor("identft", [128, 128], F32))
    L1t = es0.enter_context(nc.sbuf_tensor("L1t", [64, 64], BF16))
    M1t = es0.enter_context(nc.sbuf_tensor("M1t", [64, 64], BF16))
    o1t = es0.enter_context(nc.sbuf_tensor("o1t", [1, 128], BF16))
    oct_ = es0.enter_context(nc.sbuf_tensor("oct", [128, 1], BF16))
    iet = es0.enter_context(nc.sbuf_tensor("iet", [64, 64], BF16))
    oft = es0.enter_context(nc.sbuf_tensor("oft", [1, 128], F32))
    with es0:
        register_qmx_consts(p)
        for v in (1.0, -1.0, 0.5, -0.5, 1e18, 1e-5, 1.0 / H, 1.0 / 4096.0,
                  float(-162.0 * LN2 + np.log(0.5))):
            p.C(v)
        p.eng.update(v=nc.vector, s=nc.scalar, t=nc.tensor, g=nc.gpsimd,
                     q=nc.sync, a=nc.scalar)
        for dst_t, src_t in ((p.ctab, c_in), (identt, id_in), (identf, idf_in),
                             (L1t, L1_in), (M1t, M1_in), (o1t, o1_in),
                             (oct_, oc_in), (iet, ie_in), (oft, of_in)):
            dma(p, "q", dst_t[:, :], src_t.ap())
        nc.sync.wait_ge(p.sem["q"], p.cnt["q"])
        nc.all_engine_barrier()
        es1 = contextlib.ExitStack()
        qt = qmx_scratch2(nc, es1, 2048, prefix="pq")
        with es1:
            emit_ffn_weight_prep(p, wg_in, 64, H, wgT, qt, identt)
            emit_ffn_weight_prep(p, wu_in, 64, H, wuT, qt, identt)
            emit_ffn_weight_prep(p, wd_in, 16, I, wdT, qt, identt, out_scale=0.5)
            emit_plain_weight_prep(p, wip, 67, H, winT_t, identt, tiled=True)
            emit_plain_weight_prep(p, wop, 16, 4096, woT, identt)
            barrier(p)
        params = dict(conv_w=cw_in, conv_b=cb_in, A_log=al_in, Dv=dv_in,
                      dt_bias=dtb_in, norm_w=nw_in)
        cm = dict(ident=identt, identf=identf, L1=L1t, M1m=M1t, ones1=o1t,
                  onescol=oct_, Ieye64=iet, onesf=oft)
        emit_ffn_stage(p, x_in, resid1, wgT, wuT, wdT, interT, identt)
        emit_mamba_stage(p, resid1, resid2, winT_t, woT, params, cm)
        emit_ffn_stage(p, resid2, out, wgT, wuT, wdT, interT, identt)
    return nc, p


_CACHED = {}


def kernel(L0_attn_x, L0_attn_in_proj_w, L0_attn_conv_w, L0_attn_conv_b,
           L0_attn_A_log, L0_attn_D, L0_attn_dt_bias, L0_attn_norm_w,
           L0_attn_out_proj_w, L0_ffn_w_gate, L0_ffn_w_up, L0_ffn_w_down):
    import ml_dtypes
    if "nc" not in _CACHED:
        _CACHED["nc"], _CACHED["p"] = _build_program()
    nc, p = _CACHED["nc"], _CACHED["p"]
    x = np.ascontiguousarray(np.asarray(L0_attn_x, dtype=np.float32))
    in_proj_pad = np.zeros((8576, 2048), np.float32)
    in_proj_pad[:8512] = np.asarray(L0_attn_in_proj_w, np.float32)
    kk, ll = np.meshgrid(np.arange(64), np.arange(64), indexing="ij")
    common = {
        "w_gate": np.ascontiguousarray(np.asarray(L0_ffn_w_gate, np.float32)),
        "w_up": np.ascontiguousarray(np.asarray(L0_ffn_w_up, np.float32)),
        "w_down": np.ascontiguousarray(np.asarray(L0_ffn_w_down, np.float32)),
        "in_proj_pad": in_proj_pad,
        "out_proj": np.ascontiguousarray(np.asarray(L0_attn_out_proj_w, np.float32)),
        "conv_w": np.ascontiguousarray(np.asarray(L0_attn_conv_w, np.float32).reshape(4352, 4)),
        "conv_b": np.ascontiguousarray(np.asarray(L0_attn_conv_b, np.float32)),
        "A_log": np.ascontiguousarray(np.asarray(L0_attn_A_log, np.float32)),
        "Dv": np.ascontiguousarray(np.asarray(L0_attn_D, np.float32)),
        "dt_bias": np.ascontiguousarray(np.asarray(L0_attn_dt_bias, np.float32)),
        "norm_w": np.ascontiguousarray(np.asarray(L0_attn_norm_w, np.float32)),
        "consts": p.consts_array(),
        "ident": np.eye(128).astype(ml_dtypes.bfloat16),
        "identf": np.eye(128).astype(np.float32),
        "L1": (kk > ll).astype(ml_dtypes.bfloat16),
        "M1m": (kk <= ll).astype(ml_dtypes.bfloat16),
        "ones1": np.ones((1, 128), ml_dtypes.bfloat16),
        "onescol": np.ones((128, 1), ml_dtypes.bfloat16),
        "Ieye64": np.eye(64).astype(ml_dtypes.bfloat16),
        "onesf": np.ones((1, 128), np.float32),
    }
    in_maps = [dict(common, x=np.ascontiguousarray(x[b])) for b in range(8)]
    res = run_bass_kernel_spmd(nc, in_maps, core_ids=list(range(8)))
    return np.stack([np.asarray(res.results[b]["out"]) for b in range(8)], axis=0)


# version 07
# speedup vs baseline: 1.0196x; 1.0006x over previous
import contextlib
import numpy as np
import ml_dtypes
import concourse.bass as bass
import concourse.mybir as mybir

F32 = mybir.dt.float32
BF16 = mybir.dt.bfloat16
I32 = mybir.dt.int32
ALU = mybir.AluOpType
ACT = mybir.ActivationFunctionType
AX = mybir.AxisListType
LN2 = float(np.log(2.0))


def bcast(t, parts, nb, rep, col0=0):
    return bass.AP(t, col0, [[t.shape[1], parts], [1, nb], [0, rep]])


def bcast_h(t, parts, nh, nrep, col0=0, hstride=1):
    return bass.AP(t, col0, [[t.shape[1], parts], [hstride, nh], [0, nrep]])


class Prog:
    ENGS = ("v", "s", "t", "g", "q", "a")

    _UID = [0]

    def __init__(self, nc):
        self.nc = nc
        self.const_vals = []
        self.const_cols = {}
        self.ctab = None
        self.sem = {}
        self.cnt = {e: 0 for e in self.ENGS}
        self.eng = {}

    def alloc_consts(self, es, max_consts=64):
        self.ctab = es.enter_context(self.nc.sbuf_tensor("ctab", [128, max_consts], F32))
        self.max_consts = max_consts

    def alloc_sems(self, es):
        for e in self.ENGS:
            self.sem[e] = es.enter_context(self.nc.semaphore("sem_" + e))

    def C(self, val):
        val = float(val)
        if val in self.const_cols:
            return
        col = len(self.const_vals)
        assert col < self.max_consts, "const table full"
        self.const_cols[val] = col
        self.const_vals.append(val)
        self.nc.const_aps.register(F32, val, self.ctab[:, col:col + 1])

    def consts_array(self):
        arr = np.zeros((128, self.max_consts), np.float32)
        arr[:, :len(self.const_vals)] = np.array(self.const_vals, np.float32)[None, :]
        return arr

    def register_common_consts(self):
        for v in (1e-30, 162.0 * LN2, -162.0 * LN2, 1.0, 0.5, -1.0, 2.0, -0.5, 0.0,
                  1e-5, 1.0 / 4096.0, -0.5 * 1.0):
            self.C(v)

    def tick(self, key, inst, n=1):
        self.cnt[key] += n
        inst.then_inc(self.sem[key], n)
        return self.cnt[key]

    def vsync(self):
        i = self.nc.vector.engine_nop()
        self.tick("v", i)
        self.nc.vector.wait_ge(self.sem["v"], self.cnt["v"])

    def wait(self, from_key, on_key, value=None):
        if value is None:
            value = self.cnt[on_key]
        self.eng[from_key].wait_ge(self.sem[on_key], value)


def emit_floor(p, out, in_ap, itmp, ftmp, gtmp, shift=0.0):
    v, sca = p.nc.vector, p.nc.scalar
    i = v.tensor_scalar(itmp, in_ap, shift, None, ALU.add)
    p.tick("v", i)
    p.wait("s", "v")
    i = sca.copy(ftmp, itmp)
    p.tick("s", i)
    p.wait("v", "s")
    v.scalar_tensor_tensor(gtmp, in_ap, shift, ftmp, ALU.add, ALU.is_lt)
    p.vsync()
    return v.tensor_tensor(out, ftmp, gtmp, ALU.subtract)


def emit_qmx(p, vec, sca, vk, sk, src_ap, H, dst_bf, mfrac, t):
    mf = float(mfrac)
    max_norm = 7.5 if mf == 3.0 else 6.0
    nb = H // 32
    i = vec.tensor_reduce(t["amax"][:, :nb],
                          bass.AP(src_ap.tensor, src_ap.offset,
                                  [list(src_ap.ap[0])] + [[32, nb], [1, 32]]),
                          AX.X, ALU.max, apply_absolute_value=True)
    p.tick(vk, i)
    p.wait(sk, vk)
    i = sca.activation(t["lg"][:, :nb], t["amax"][:, :nb], ACT.Ln, bias=1e-30)
    p.tick(sk, i)
    p.wait(vk, sk)
    vec.tensor_scalar(t["tpos"][:, :nb], t["lg"][:, :nb], 1.0 / LN2, 160.0, ALU.mult, ALU.add)
    p.vsync()
    i = emit_floor(p, t["s160"][:, :nb], t["tpos"][:, :nb], t["ii"][:, :nb],
                   t["ff"][:, :nb], t["gg"][:, :nb])
    p.tick(vk, i)
    p.wait(sk, vk)
    i = sca.activation(t["iscale"][:, :nb], t["s160"][:, :nb], ACT.Exp,
                       bias=162.0 * LN2, scale=-LN2)
    i = sca.activation(t["scale"][:, :nb], t["s160"][:, :nb], ACT.Exp,
                       bias=-162.0 * LN2, scale=LN2)
    p.tick(sk, i)
    p.wait(vk, sk)
    y = t["y"][:, :H]
    sq = t["sq"][:, :H]
    vec.tensor_tensor(y, src_ap, bcast(t["iscale"], 128, nb, 32), ALU.mult)
    vec.tensor_tensor(sq, y, y, ALU.mult)
    vec.tensor_scalar(t["p1"][:, :H], sq, 4.0, -(2.0 ** (mf - 1)), ALU.is_ge, ALU.mult)
    vec.tensor_scalar(t["p2"][:, :H], sq, 16.0, -(2.0 ** (mf - 2)), ALU.is_ge, ALU.mult)
    vec.scalar_tensor_tensor(t["iqs"][:, :H], t["p1"][:, :H], 2.0 ** mf, t["p2"][:, :H], ALU.add, ALU.add)
    vec.tensor_scalar(t["q1"][:, :H], sq, 4.0, 2.0 ** (-mf), ALU.is_ge, ALU.mult)
    vec.tensor_scalar(t["q2"][:, :H], sq, 16.0, 2.0 ** (1 - mf), ALU.is_ge, ALU.mult)
    vec.scalar_tensor_tensor(t["qs"][:, :H], t["q1"][:, :H], 2.0 ** (-mf), t["q2"][:, :H], ALU.add, ALU.add)
    v_ = t["v"][:, :H]
    vec.tensor_tensor(v_, y, t["iqs"][:, :H], ALU.mult)
    i = vec.tensor_scalar(t["ri"][:, :H], v_, 64.5, None, ALU.add)
    p.tick("v", i)
    p.wait("s", "v")
    i = sca.copy(t["rf"][:, :H], t["ri"][:, :H])
    p.tick("s", i)
    p.wait("v", "s")
    vec.scalar_tensor_tensor(t["rg"][:, :H], v_, 64.5, t["rf"][:, :H], ALU.add, ALU.is_lt)
    vec.tensor_tensor(t["ru"][:, :H], t["rf"][:, :H], t["rg"][:, :H], ALU.subtract)
    vec.scalar_tensor_tensor(t["q0"][:, :H], t["ru"][:, :H], 64.0, t["qs"][:, :H], ALU.subtract, ALU.mult)
    vec.tensor_scalar(t["qc"][:, :H], t["q0"][:, :H], max_norm, -max_norm, ALU.min, ALU.max)
    i = vec.tensor_tensor(dst_bf, t["qc"][:, :H], bcast(t["scale"], 128, nb, 32), ALU.mult)
    return i


def qmx_scratch(nc, es, H, prefix=""):
    nb = H // 32
    t = {}
    for n in ("y", "sq", "p1", "p2", "iqs", "q1", "q2", "qs", "v", "rf", "rg", "ru", "q0", "qc"):
        t[n] = es.enter_context(nc.sbuf_tensor(prefix + n, [128, H], F32))
    t["ri"] = es.enter_context(nc.sbuf_tensor(prefix + "ri", [128, H], I32))
    for n in ("amax", "lg", "tpos", "ff", "gg", "s160", "iscale", "scale"):
        t[n] = es.enter_context(nc.sbuf_tensor(prefix + n, [128, nb], F32))
    t["ii"] = es.enter_context(nc.sbuf_tensor(prefix + "ii", [128, nb], I32))
    return t


def register_qmx_consts(p):
    for v in (1e-30, 162.0 * LN2, -162.0 * LN2, 64.5):
        p.C(v)


def emit_sigma(p, vec, sca, vk, sk, src_ap, dst, e1, e2):
    i = sca.activation(e1, src_ap, ACT.Exp, scale=-1.0)
    p.tick(sk, i)
    p.wait(vk, sk)
    i = vec.tensor_scalar(e2, e1, 1e18, None, ALU.min)
    p.tick(vk, i)
    p.wait(sk, vk)
    i = sca.activation(e1, e2, ACT.Ln, bias=1.0)
    i = sca.activation(dst, e1, ACT.Exp, scale=-1.0)
    return i


def np_quantize_mx(x, ebits, mfrac, emax_elem, max_norm, block=32):
    f = np.float32
    shp = x.shape
    xb = x.reshape(shp[:-1] + (shp[-1] // block, block)).astype(f)
    amax = np.max(np.abs(xb), axis=-1, keepdims=True)
    sexp = np.floor(np.log2(np.where(amax == 0, f(1.0), amax))).astype(f) - f(emax_elem)
    sexp = np.clip(sexp, -127.0, 127.0)
    scale = np.exp2(sexp).astype(f)
    y = (xb / scale).astype(f)
    ax = np.abs(y)
    priv = np.floor(np.log2(np.where(ax == 0, f(1.0), ax))).astype(f)
    priv = np.maximum(priv, f(-2 ** (ebits - 1) + 2))
    qs = np.exp2(priv - f(mfrac)).astype(f)
    q = (np.round(y / qs) * qs).astype(f)
    q = np.clip(q, -max_norm, max_norm)
    return (q * scale).astype(f).reshape(shp)



def emit_qmx_chunked(p, src_ap_fn, W, dst_ap_fn, mfrac, t, chunk=2048, out_scale=1.0):
    for c0 in range(0, W, chunk):
        w = min(chunk, W - c0)
        emit_qmx2(p, src_ap_fn(c0, w), w, dst_ap_fn(c0, w), mfrac, t, out_scale=out_scale)


def emit_qmx2(p, src_ap, H, dst_bf, mfrac, t, out_scale=1.0):
    vec, sca = p.nc.vector, p.nc.scalar
    mf = float(mfrac)
    max_norm = 7.5 if mf == 3.0 else 6.0
    nb = H // 32
    b = [t[f"b{i}"] for i in range(6)]
    ib = t["ib"]
    i = vec.tensor_reduce(t["amax"][:, :nb],
                          bass.AP(src_ap.tensor, src_ap.offset,
                                  [list(src_ap.ap[0])] + [[32, nb], [1, 32]]),
                          AX.X, ALU.max, apply_absolute_value=True)
    p.tick("v", i)
    p.wait("s", "v")
    i = sca.activation(t["lg"][:, :nb], t["amax"][:, :nb], ACT.Ln, bias=1e-30)
    p.tick("s", i)
    p.wait("v", "s")
    vec.tensor_scalar(t["tpos"][:, :nb], t["lg"][:, :nb], 1.0 / LN2, 160.0, ALU.mult, ALU.add)
    p.vsync()
    i = emit_floor(p, t["s160"][:, :nb], t["tpos"][:, :nb], t["ii"][:, :nb],
                   t["ff"][:, :nb], t["gg"][:, :nb])
    p.tick("v", i)
    p.wait("s", "v")
    i = sca.activation(t["iscale"][:, :nb], t["s160"][:, :nb], ACT.Exp,
                       bias=162.0 * LN2, scale=-LN2)
    i = sca.activation(t["scale"][:, :nb], t["s160"][:, :nb], ACT.Exp,
                       bias=float(-162.0 * LN2 + np.log(out_scale)), scale=LN2)
    p.tick("s", i)
    p.wait("v", "s")
    y, sq = b[0][:, :H], b[1][:, :H]
    vec.tensor_tensor(y, src_ap, bcast(t["iscale"], 128, nb, 32), ALU.mult)
    vec.tensor_tensor(sq, y, y, ALU.mult)
    vec.tensor_scalar(b[2][:, :H], sq, 4.0, -(2.0 ** (mf - 1)), ALU.is_ge, ALU.mult)
    vec.tensor_scalar(b[3][:, :H], sq, 16.0, -(2.0 ** (mf - 2)), ALU.is_ge, ALU.mult)
    vec.scalar_tensor_tensor(b[4][:, :H], b[2][:, :H], 2.0 ** mf, b[3][:, :H], ALU.add, ALU.add)
    vec.tensor_scalar(b[3][:, :H], sq, 4.0, 2.0 ** (-mf), ALU.is_ge, ALU.mult)
    vec.tensor_scalar(b[5][:, :H], sq, 16.0, 2.0 ** (1 - mf), ALU.is_ge, ALU.mult)
    vec.scalar_tensor_tensor(b[1][:, :H], b[3][:, :H], 2.0 ** (-mf), b[5][:, :H], ALU.add, ALU.add)
    i = vec.tensor_tensor(b[2][:, :H], y, b[4][:, :H], ALU.mult)
    p.tick("v", i)
    p.wait("s", "v")
    sca.activation(ib[:, :H], b[2][:, :H], ACT.Copy, bias=64.5)
    i = sca.copy(b[0][:, :H], ib[:, :H])
    p.tick("s", i)
    p.wait("v", "s")
    vec.scalar_tensor_tensor(b[3][:, :H], b[2][:, :H], 64.5, b[0][:, :H], ALU.add, ALU.is_lt)
    vec.tensor_tensor(b[4][:, :H], b[0][:, :H], b[3][:, :H], ALU.subtract)
    vec.scalar_tensor_tensor(b[5][:, :H], b[4][:, :H], 64.0, b[1][:, :H], ALU.subtract, ALU.mult)
    vec.tensor_scalar(b[2][:, :H], b[5][:, :H], max_norm, -max_norm, ALU.min, ALU.max)
    i = vec.tensor_tensor(dst_bf, b[2][:, :H], bcast(t["scale"], 128, nb, 32), ALU.mult)
    p.tick("v", i)
    return i


def qmx_scratch2(nc, es, W, prefix="q"):
    nb = W // 32
    t = {}
    for i in range(6):
        t[f"b{i}"] = es.enter_context(nc.sbuf_tensor(f"{prefix}b{i}", [128, W], F32))
    t["ib"] = es.enter_context(nc.sbuf_tensor(prefix + "ib", [128, W], I32))
    for n in ("amax", "lg", "tpos", "ff", "gg", "s160", "iscale", "scale"):
        t[n] = es.enter_context(nc.sbuf_tensor(prefix + n, [128, nb], F32))
    t["ii"] = es.enter_context(nc.sbuf_tensor(prefix + "ii", [128, nb], I32))
    return t


def dma(p, key, dst, src, slow=False):
    eng = p.nc.sync if key == "q" else p.nc.scalar
    i = eng.dma_start(dst, src, allow_slow_non_contiguous=True) if slow else eng.dma_start(dst, src)
    p.cnt[key] += 16
    i.then_inc(p.sem[key], 16)
    return p.cnt[key]


def barrier(p):
    nc = p.nc
    for k in ("q", "a"):
        if p.cnt[k]:
            nc.sync.wait_ge(p.sem[k], p.cnt[k])
    for k in ("v", "s", "t", "g"):
        if p.cnt[k]:
            nc.sync.wait_ge(p.sem[k], p.cnt[k])
    nc.all_engine_barrier()


def emit_transpose_batch(p, src_fn, n, psum_bf, dst_ap, ident, wait_q=None):
    nc = p.nc
    if not isinstance(ident, bass.AP):
        ident = ident[:, :]
    for j in range(n):
        i = nc.tensor.transpose(psum_bf[:, j * 128:(j + 1) * 128], src_fn(j), ident)
    p.tick("t", i)
    p.wait("s", "t")
    if wait_q is not None:
        p.wait("s", "q", wait_q)
    i = nc.scalar.copy(dst_ap, psum_bf[:, :n * 128])
    p.tick("s", i)


def emit_ln(p, x_ap, W, xn_out, lnt, sq_scr):
    nc = p.nc
    vec, sca = nc.vector, nc.scalar
    sums, mcol, m2, vcol, lnv, rcol, mr = (lnt[:, i:i + 1] for i in range(7))
    i = vec.tensor_reduce(sums, x_ap, AX.X, ALU.add)
    p.tick("v", i)
    p.wait("s", "v")
    i = sca.activation(sq_scr, x_ap, ACT.Square, accum_out=lnt[:, 7:8])
    p.tick("s", i)
    p.vsync()
    vec.tensor_scalar(mcol, sums, 1.0 / W, None, ALU.mult)
    p.vsync()
    vec.tensor_tensor(m2, mcol, mcol, ALU.mult)
    p.wait("v", "s")
    p.vsync()
    i = vec.scalar_tensor_tensor(vcol, lnt[:, 7:8], 1.0 / W, m2, ALU.mult, ALU.subtract)
    p.tick("v", i)
    p.wait("s", "v")
    sca.activation(lnv, vcol, ACT.Ln, bias=1e-5)
    i = sca.activation(rcol, lnv, ACT.Exp, scale=-0.5)
    p.tick("s", i)
    p.wait("v", "s")
    vec.tensor_tensor(mr, mcol, rcol, ALU.mult)
    p.vsync()
    i = vec.tensor_scalar(xn_out, x_ap, rcol, mr, ALU.mult, ALU.subtract)
    p.tick("v", i)


def emit_ffn_weight_prep(p, w_in, IT, H, wqT_out, qt, ident, out_scale=1.0, dbg=None):
    nc = p.nc
    es = contextlib.ExitStack()
    U = _u(p)
    wsb = [es.enter_context(nc.sbuf_tensor(f"wp_wsb{i}" + U, [128, H], F32)) for i in range(2)]
    wq = [es.enter_context(nc.sbuf_tensor(f"wp_wq{i}" + U, [128, H], BF16)) for i in range(2)]
    stg = es.enter_context(nc.sbuf_tensor("wp_stg" + U, [128, 2048], BF16))
    pt = es.enter_context(nc.psum_tensor("wp_pt" + U, [128, 2048], BF16))
    NG = H // 2048 if H >= 2048 else 1
    CW = min(H, 2048)
    qdone = [0, 0]
    tdone = [0, 0]
    evac_s = 0
    outq = 0
    with es:
        dma(p, "q", wsb[0][:, :], bass.AP(w_in, 0, [[H, 128], [1, H]]))
        ld = [p.cnt["q"], 0]
        for it in range(IT):
            b = it % 2
            if it + 1 < IT:
                if it >= 1:
                    p.wait("q", "v", qdone[(it + 1) % 2])
                dma(p, "q", wsb[(it + 1) % 2][:, :],
                    bass.AP(w_in, (it + 1) * 128 * H, [[H, 128], [1, H]]))
                ld[(it + 1) % 2] = p.cnt["q"]
            p.wait("v", "q", ld[b])
            p.wait("s", "q", ld[b])
            if it >= 2:
                p.wait("v", "t", tdone[b])
            for c0 in range(0, H, CW):
                emit_qmx2(p, wsb[b][:, c0:c0 + CW], CW, wq[b][:, c0:c0 + CW], 1.0, qt,
                          out_scale=out_scale)
            qdone[b] = p.cnt["v"]
            for g0 in range(0, H // 128, 16):
                ng = min(16, H // 128 - g0)
                p.wait("t", "v", qdone[b])
                p.wait("t", "s", evac_s)
                emit_transpose_batch(
                    p, lambda j: wq[b][:, (g0 + j) * 128:(g0 + j + 1) * 128], ng,
                    pt, stg[:, :ng * 128], ident, wait_q=outq)
                evac_s = p.cnt["s"]
                p.wait("q", "s", evac_s)
                W2 = IT * 128
                dma(p, "q", bass.AP(wqT_out, (g0 * 128) * W2 + it * 128,
                                    [[W2, 128], [W2 * 128, ng], [1, 128]]),
                    stg[:, :ng * 128].rearrange("p (j c) -> p j c", j=ng))
                outq = p.cnt["q"]
            tdone[b] = p.cnt["t"]


def emit_plain_weight_prep(p, w_in, IT, H, wT_out, ident, tiled=False):
    nc = p.nc
    es = contextlib.ExitStack()
    U = _u(p)
    wsb = [es.enter_context(nc.sbuf_tensor(f"pp_wsb{i}" + U, [128, H], F32)) for i in range(2)]
    wq = [es.enter_context(nc.sbuf_tensor(f"pp_wq{i}" + U, [128, H], BF16)) for i in range(2)]
    stg = es.enter_context(nc.sbuf_tensor("pp_stg" + U, [128, 2048], BF16))
    pt = es.enter_context(nc.psum_tensor("pp_pt" + U, [128, 2048], BF16))
    NJ = H // 128
    W2 = IT * 128
    qdone = [0, 0]
    tdone = [0, 0]
    evac_s = 0
    outq = 0
    with es:
        dma(p, "q", wsb[0][:, :], bass.AP(w_in, 0, [[H, 128], [1, H]]))
        ld = [p.cnt["q"], 0]
        for it in range(IT):
            b = it % 2
            if it + 1 < IT:
                if it >= 1:
                    p.wait("q", "v", qdone[(it + 1) % 2])
                dma(p, "q", wsb[(it + 1) % 2][:, :],
                    bass.AP(w_in, (it + 1) * 128 * H, [[H, 128], [1, H]]))
                ld[(it + 1) % 2] = p.cnt["q"]
            p.wait("v", "q", ld[b])
            if it >= 2:
                p.wait("v", "t", tdone[b])
            i = nc.vector.tensor_copy(wq[b][:, :], wsb[b][:, :])
            p.tick("v", i)
            qdone[b] = p.cnt["v"]
            for g0 in range(0, NJ, 16):
                ng = min(16, NJ - g0)
                p.wait("t", "v", qdone[b])
                p.wait("t", "s", evac_s)
                emit_transpose_batch(
                    p, lambda j: wq[b][:, (g0 + j) * 128:(g0 + j + 1) * 128], ng,
                    pt, stg[:, :ng * 128], ident, wait_q=outq)
                evac_s = p.cnt["s"]
                p.wait("q", "s", evac_s)
                if tiled:
                    dma(p, "q", bass.AP(wT_out, (it * NJ + g0) * 16384,
                                        [[128, 128], [16384, ng], [1, 128]]),
                        stg[:, :ng * 128].rearrange("p (j c) -> p j c", j=ng))
                else:
                    dma(p, "q", bass.AP(wT_out, (g0 * 128) * W2 + it * 128,
                                        [[W2, 128], [W2 * 128, ng], [1, 128]]),
                        stg[:, :ng * 128].rearrange("p (j c) -> p j c", j=ng))
                outq = p.cnt["q"]
            tdone[b] = p.cnt["t"]


def _u(p):
    Prog._UID[0] += 1
    return f"_{Prog._UID[0]}"


def emit_ffn_stage(p, src, dst, wgT, wuT, wdT, interT, ident):
    nc = p.nc
    import contextlib
    T, H, I = 4096, 2048, 8192
    NT = T // 128
    TCH = 2048
    NTC = TCH // 128
    for tch in range(T // TCH):
        es = contextlib.ExitStack()
        U = _u(p)
        qt = qmx_scratch2(nc, es, 1024, prefix="ffq" + U)
        xn_qT = es.enter_context(nc.sbuf_tensor("ff_xnqT" + U, [128, 16 * TCH], BF16))
        with es:
            es1b = contextlib.ExitStack()
            xb = [es1b.enter_context(nc.sbuf_tensor(f"ff_xb{i}{U}", [128, H], F32)) for i in range(2)]
            xn = es1b.enter_context(nc.sbuf_tensor("ff_xn" + U, [128, H], F32))
            xq = es1b.enter_context(nc.sbuf_tensor("ff_xq" + U, [128, H], BF16))
            lnt = es1b.enter_context(nc.sbuf_tensor("ff_lnt" + U, [128, 8], F32))
            ptr = es1b.enter_context(nc.psum_tensor("ff_ptr" + U, [128, 2048], BF16))
            _noop = es1b
            with es1b:
                vdone = [0] * NTC
                for t in range(NTC):
                    tt = tch * NTC + t
                    if t >= 2:
                        p.wait("q", "v", vdone[t - 2])
                    dma(p, "q", xb[t % 2][:, :], bass.AP(src, tt * 128 * H, [[H, 128], [1, H]]))
                    p.wait("v", "q", p.cnt["q"])
                    p.wait("s", "q", p.cnt["q"])
                    if t >= 1:
                        p.wait("v", "t")
                    emit_ln(p, xb[t % 2][:, :], H, xn[:, :], lnt, xn[:, :])
                    for c0 in (0, 1024):
                        emit_qmx2(p, xn[:, c0:c0 + 1024], 1024, xq[:, c0:c0 + 1024], 3.0, qt)
                    vdone[t] = p.cnt["v"]
                    p.wait("t", "v")
                    if t > 0:
                        p.wait("t", "s")
                    emit_transpose_batch(
                        p, lambda j: xq[:, j * 128:(j + 1) * 128], 16, ptr,
                        bass.AP(xn_qT, t * 128, [[16 * TCH, 128], [TCH, 16], [1, 128]]),
                        ident)
                barrier(p)
            es2 = contextlib.ExitStack()
            U2 = _u(p)
            GW = 512
            NJ = GW // 128
            wg_sb = [es2.enter_context(nc.sbuf_tensor(f"ff_wg{i}{U2}", [128, 16 * GW], BF16)) for i in range(2)]
            wu_sb = [es2.enter_context(nc.sbuf_tensor(f"ff_wu{i}{U2}", [128, 16 * GW], BF16)) for i in range(2)]
            ubuf = [es2.enter_context(nc.sbuf_tensor(f"ff_ubuf{i}{U2}", [128, GW], F32)) for i in range(2)]
            sgbuf = [es2.enter_context(nc.sbuf_tensor(f"ff_sg{i}{U2}", [128, GW], F32)) for i in range(2)]
            inb = es2.enter_context(nc.sbuf_tensor("ff_in" + U2, [128, GW], F32))
            iq = [es2.enter_context(nc.sbuf_tensor(f"ff_iq{i}{U2}", [128, GW], BF16)) for i in range(2)]
            itq = [es2.enter_context(nc.sbuf_tensor(f"ff_itq{i}{U2}", [128, GW], BF16)) for i in range(2)]
            psg = [es2.enter_context(nc.psum_tensor(f"ff_psg{i}{U2}", [128, 512], F32)) for i in range(2)]
            psu = [es2.enter_context(nc.psum_tensor(f"ff_psu{i}{U2}", [128, 512], F32)) for i in range(2)]
            ptr2 = es2.enter_context(nc.psum_tensor("ff_ptr2" + U2, [128, GW], BF16))
            NG = I // GW
            NK = NG * NTC

            def load_wslab(g, buf):
                for wT, sb in ((wgT, wg_sb[buf]), (wuT, wu_sb[buf])):
                    for j in range(16):
                        dma(p, "q", sb[:, j * GW:(j + 1) * GW],
                            bass.AP(wT, j * 128 * I + g * GW, [[I, 128], [1, GW]]))
                return p.cnt["q"]

            def emit_tr(k):
                p.wait("t", "v", qx_tick[k])
                if k >= 2:
                    p.wait("t", "s", ev_tick[k - 2])
                kk = k
                emit_transpose_batch(
                    p, lambda j: iq[kk % 2][:, j * 128:(j + 1) * 128], NJ, ptr2,
                    itq[kk % 2][:, :GW], ident,
                    wait_q=(qitq[k - 2] if k >= 2 else None))
                ev_tick[k] = p.cnt["s"]
                p.wait("q", "s", ev_tick[k])
                g_, t_ = divmod(k, NTC)
                tt_ = tch * NTC + t_
                dma(p, "q", bass.AP(interT, (tt_ * 64 + g_ * NJ) * 16384,
                                    [[128, 128], [16384, NJ], [1, 128]]),
                    itq[kk % 2][:, :GW].rearrange("p (j c) -> p j c", j=NJ))
                qitq[k] = p.cnt["q"]

            with es2:
                wready = [load_wslab(0, 0), 0]
                gdone = [0] * NG
                mm_tick = [0] * NK
                sc_tick = [0] * NK
                qx_tick = [0] * NK
                ev_tick = [0] * NK
                qitq = [0] * NK
                inb_tick = [0] * NK
                for g in range(NG):
                    if g + 1 < NG:
                        if g >= 1:
                            p.wait("q", "t", gdone[g - 1])
                        wready[(g + 1) % 2] = load_wslab(g + 1, (g + 1) % 2)
                    for t in range(NTC):
                        k = g * NTC + t
                        kb = k % 2
                        p.wait("t", "q", wready[g % 2])
                        if k >= 2:
                            p.wait("t", "s", sc_tick[k - 2])
                        for j in range(16):
                            nc.tensor.matmul(
                                psg[kb][:, :],
                                bass.AP(xn_qT, j * TCH + t * 128, [[16 * TCH, 128], [1, 128]]),
                                wg_sb[g % 2][:, j * GW:j * GW + GW],
                                start=(j == 0), stop=(j == 15))
                        for j in range(16):
                            i = nc.tensor.matmul(
                                psu[kb][:, :],
                                bass.AP(xn_qT, j * TCH + t * 128, [[16 * TCH, 128], [1, 128]]),
                                wu_sb[g % 2][:, j * GW:j * GW + GW],
                                start=(j == 0), stop=(j == 15))
                        p.tick("t", i)
                        mm_tick[k] = p.cnt["t"]
                        gdone[g] = mm_tick[k]
                        if k >= 1:
                            emit_tr(k - 1)
                        p.wait("s", "t", mm_tick[k])
                        if k >= 2:
                            p.wait("s", "v", inb_tick[k - 2])
                        nc.scalar.activation(sgbuf[kb][:, :], psg[kb][:, :], ACT.Silu)
                        i = nc.scalar.copy(ubuf[kb][:, :], psu[kb][:, :])
                        p.tick("s", i)
                        sc_tick[k] = p.cnt["s"]
                        p.wait("v", "s", sc_tick[k])
                        i = nc.vector.tensor_tensor(inb[:, :], sgbuf[kb][:, :], ubuf[kb][:, :], ALU.mult)
                        p.tick("v", i)
                        inb_tick[k] = p.cnt["v"]
                        if k >= 2:
                            p.wait("v", "t")
                        emit_qmx2(p, inb[:, :], GW, iq[k % 2][:, :], 3.0, qt)
                        qx_tick[k] = p.cnt["v"]
                emit_tr(NK - 1)
        barrier(p)
    es3 = contextlib.ExitStack()
    U3 = _u(p)
    wd_sb = es3.enter_context(nc.sbuf_tensor("ff_wd" + U3, [128, 64 * 1024], BF16))
    it_sb = [es3.enter_context(nc.sbuf_tensor(f"ff_it{i}{U3}", [128, 64 * 128], BF16)) for i in range(2)]
    xs = [es3.enter_context(nc.sbuf_tensor(f"ff_xs{i}{U3}", [128, 1024], F32)) for i in range(2)]
    outb = [es3.enter_context(nc.sbuf_tensor(f"ff_ob{i}{U3}", [128, 1024], F32)) for i in range(2)]
    pso = es3.enter_context(nc.psum_tensor("ff_pso" + U3, [128, 1024], F32))
    with es3:
        for half in range(2):
            h0 = half * 1024
            if half == 1:
                p.wait("q", "t")
            for j in range(64):
                dma(p, "q", wd_sb[:, j * 1024:(j + 1) * 1024],
                    bass.AP(wdT, j * 128 * H + h0, [[H, 128], [1, 1024]]))
            wdready = p.cnt["q"]
            tmm = [0] * NT
            vev = [0] * NT
            qout = [0] * NT
            for t in range(NT):
                bi = t % 2
                if t >= 2:
                    p.wait("q", "t", tmm[t - 2])
                    p.wait("q", "v", vev[t - 2])
                dma(p, "q", it_sb[bi][:, :],
                    bass.AP(interT, t * 64 * 16384, [[128, 128], [16384, 64], [1, 128]]))
                dma(p, "q", xs[bi][:, :],
                    bass.AP(src, t * 128 * H + h0, [[H, 128], [1, 1024]]))
                p.wait("t", "q", max(p.cnt["q"], wdready))
                if t >= 1:
                    p.wait("t", "v", vev[t - 1])
                for j in range(64):
                    nc.tensor.matmul(pso[:, 0:512], it_sb[bi][:, j * 128:(j + 1) * 128],
                                     wd_sb[:, j * 1024:j * 1024 + 512],
                                     start=(j == 0), stop=(j == 63))
                    i = nc.tensor.matmul(pso[:, 512:1024], it_sb[bi][:, j * 128:(j + 1) * 128],
                                         wd_sb[:, j * 1024 + 512:j * 1024 + 1024],
                                         start=(j == 0), stop=(j == 63))
                p.tick("t", i)
                tmm[t] = p.cnt["t"]
                p.wait("v", "t")
                if t >= 2:
                    p.wait("v", "q", qout[t - 2])
                i = nc.vector.tensor_tensor(outb[bi][:, :], pso[:, :], xs[bi][:, :], ALU.add)
                p.tick("v", i)
                vev[t] = p.cnt["v"]
                p.wait("q", "v", vev[t])
                dma(p, "q", bass.AP(dst, t * 128 * H + h0, [[H, 128], [1, 1024]]),
                    outb[bi][:, :])
                qout[t] = p.cnt["q"]
    barrier(p)


"""Mamba-2 SSD stage emitter for the macaron megakernel.

Stage B: dst = src + Mamba(LN(src)) per core (one sequence of 4096 tokens).
4096/CH chunks of CH tokens, carrying conv halo (3 cols) and SSD state across chunks.
"""

CH = 256
NTT = CH // 128
NBL = CH // 64
NOT = 67


def emit_sigma_psum(p, src_ap, dst, e1, e2, mul_with=None, bias_col=None):
    nc = p.nc
    if bias_col is None:
        i = nc.scalar.activation(e1, src_ap, ACT.Exp, scale=-1.0)
    else:
        i = nc.scalar.activation(e1, src_ap, ACT.Exp, scale=-1.0, bias=bias_col)
    p.tick("s", i)
    p.wait("v", "s")
    i = nc.vector.tensor_scalar(e2, e1, 1e18, None, ALU.min)
    p.tick("v", i)
    p.wait("s", "v")
    nc.scalar.activation(e1, e2, ACT.Ln, bias=1.0)
    i = nc.scalar.activation(e2, e1, ACT.Exp, scale=-1.0)
    p.tick("s", i)
    p.wait("v", "s")
    i = nc.vector.tensor_tensor(dst, mul_with, e2, ALU.mult)
    p.tick("v", i)
    return i


def emit_mamba_stage(p, src, dst, winT_t, woT, params, cm):
    nc = p.nc
    ident, identf = cm["ident"], cm["identf"]
    L1, M1m, ones1, onescol, Ieye64 = cm["L1"], cm["M1m"], cm["ones1"], cm["onescol"], cm["Ieye64"]
    onesf = cm["onesf"]
    es = contextlib.ExitStack()
    U = _u(p)
    T = lambda n, s, d=F32: es.enter_context(nc.sbuf_tensor(n + U, s, d))
    Sf = [T(f"mb_Sf{i}", [128, 4096]) for i in range(2)]
    Sbf = T("mb_Sbf", [128, 4096], BF16)
    convw = T("mb_convw", [128, 136])
    convb = T("mb_convb", [128, 34])
    convbn = T("mb_convbn", [128, 34])
    dtb = T("mb_dtb", [64, 1])
    acol = T("mb_acol", [64, 1])
    gcol = T("mb_gcol", [128, 32])
    halo = T("mb_halo", [128, 102])
    ddiag = T("mb_ddiag", [64, 4096], BF16)
    drow = T("mb_drow", [1, 64])
    drowb = T("mb_drowb", [1, 64], BF16)
    dmat_s = T("mb_dmat", [64, 64])
    with es:
        dma(p, "q", convw[:, :], bass.AP(params["conv_w"], 0, [[4, 128], [512, 34], [1, 4]]))
        dma(p, "q", convb[:, :34], bass.AP(params["conv_b"], 0, [[1, 128], [128, 34]]), slow=True)
        dma(p, "q", dtb[:, :], bass.AP(params["dt_bias"], 0, [[1, 64], [1, 1]]), slow=True)
        dma(p, "q", acol[:, :], bass.AP(params["A_log"], 0, [[1, 64], [1, 1]]), slow=True)
        dma(p, "q", drow[:, :], bass.AP(params["Dv"], 0, [[0, 1], [1, 64]]))
        dma(p, "q", gcol[:, :], bass.AP(params["norm_w"], 0, [[1, 128], [128, 32]]), slow=True)
        p.wait("v", "q", p.cnt["q"])
        p.wait("s", "q", p.cnt["q"])
        nc.vector.tensor_scalar(convbn[:, :34], convb[:, :34], -1.0, None, ALU.mult)
        nc.vector.memset(halo[:, :], 0.0)
        nc.vector.memset(Sf[0][:, :], 0.0)
        i = nc.vector.memset(Sbf[:, :], 0.0)
        p.tick("v", i)
        i = nc.scalar.activation(acol[:, :], acol[:, :], ACT.Exp)
        p.tick("s", i)
        p.wait("v", "s")
        p.vsync()
        i = nc.vector.tensor_scalar(acol[:, :], acol[:, :], -1.0, None, ALU.mult)
        p.tick("v", i)
        p.wait("s", "v")
        i = nc.scalar.copy(drowb[:, :], drow[:, :])
        p.tick("s", i)
        p.wait("t", "s")
        with nc.psum_tensor("mb_pdm" + U, [64, 64], F32) as pdm:
            i = nc.tensor.matmul(pdm[:, :], ones1[:1, :64], drowb[:, :], start=True, stop=True)
            p.tick("t", i)
            p.wait("v", "t")
            i = nc.vector.tensor_copy(dmat_s[:, :], pdm[:, :])
            p.tick("v", i)
            p.vsync()
            i = nc.vector.tensor_tensor(
                ddiag[:, :],
                bass.AP(dmat_s, 0, [[64, 64], [1, 64], [0, 64]]),
                bass.AP(Ieye64, 0, [[64, 64], [0, 64], [1, 64]]),
                ALU.mult)
            p.tick("v", i)
        barrier(p)

        for ch in range(4096 // CH):
            esC = contextlib.ExitStack()
            UC = _u(p)
            TC = lambda n, s, d=F32: esC.enter_context(nc.sbuf_tensor(n + UC, s, d))
            y3 = TC("mb_y3", [128, 32 * CH], BF16)
            rT = TC("mb_rT", [128, NTT])
            sz = TC("mb_sz", [128, 32 * CH], BF16)
            xbcs = TC("mb_xbcs", [128, 34 * CH], BF16)
            ext = TC("mb_ext", [128, CH + 3])
            accb = TC("mb_accb", [128, CH])
            e1 = TC("mb_e1", [128, CH])
            e2 = TC("mb_e2", [128, CH])
            dtT = TC("mb_dtT", [64, NBL * 64])
            adT = TC("mb_adT", [64, NBL * 64])
            BT = TC("mb_BT", [64, NBL * 128], BF16)
            with esC:
                esA = contextlib.ExitStack()
                TA = lambda n, s, d=F32: esA.enter_context(nc.sbuf_tensor(n + UC + "A", s, d))
                xc = [TA(f"mb_xc{i}", [128, 2048]) for i in range(2)]
                lnt = TA("mb_lnt", [128, 8])
                xnbf = TA("mb_xnbf", [128, 2048], BF16)
                xnT = TA("mb_xnT", [128, 16 * CH], BF16)
                wo_sb = [TA(f"mb_wo{i}", [128, 16 * 128], BF16) for i in range(2)]
                dtx = ext[0:64, 0:CH]
                dtv = accb[0:64, 0:CH]
                adv = e1[0:64, 0:CH]
                with esA:
                    vdone = [0, 0]
                    with nc.psum_tensor("mb_ptr" + UC, [128, 2048], BF16) as ptr:
                        for tt in range(NTT):
                            if tt >= 2:
                                p.wait("q", "v", vdone[tt % 2])
                            dma(p, "q", xc[tt % 2][:, :],
                                bass.AP(src, (ch * NTT + tt) * 128 * 2048, [[2048, 128], [1, 2048]]))
                            p.wait("v", "q", p.cnt["q"])
                            p.wait("s", "q", p.cnt["q"])
                            if tt > 0:
                                p.wait("v", "t")
                            emit_ln(p, xc[tt % 2][:, :], 2048, xnbf[:, :], lnt,
                                    y3[:, 0:2048])
                            vdone[tt % 2] = p.cnt["v"]
                            p.wait("t", "v")
                            if tt > 0:
                                p.wait("t", "s")
                            emit_transpose_batch(
                                p, lambda j: xnbf[:, j * 128:(j + 1) * 128], 16, ptr,
                                bass.AP(xnT, tt * 128, [[16 * CH, 128], [CH, 16], [1, 128]]),
                                ident)
                    def load_wot(ot, buf):
                        dma(p, "q", wo_sb[buf][:, :],
                            bass.AP(winT_t, ot * 16 * 16384, [[128, 128], [16384, 16], [1, 128]]))
                        return p.cnt["q"]
                    with (nc.psum_tensor("mb_po0" + UC, [128, CH], F32) as po0,
                          nc.psum_tensor("mb_po1" + UC, [128, CH], F32) as po1,
                          nc.psum_tensor("mb_pt2" + UC, [128, 64], F32) as pt2):
                        pos = [po0, po1]
                        p.wait("t", "s")
                        wready = [load_wot(0, 0), 0]
                        lastuse = [0, 0]
                        ev_v = [0, 0]
                        ev_s = [0, 0]
                        for ot in range(NOT):
                            po = pos[ot % 2]
                            if ot + 1 < NOT:
                                if ot >= 1:
                                    p.wait("q", "t", lastuse[(ot + 1) % 2])
                                wready[(ot + 1) % 2] = load_wot(ot + 1, (ot + 1) % 2)
                            p.wait("t", "q", wready[ot % 2])
                            if ot >= 2:
                                p.wait("t", "v", ev_v[ot % 2])
                                p.wait("t", "s", ev_s[ot % 2])
                            for j in range(16):
                                i = nc.tensor.matmul(
                                    po[:, :], wo_sb[ot % 2][:, j * 128:(j + 1) * 128],
                                    xnT[:, j * CH:j * CH + CH],
                                    start=(j == 0), stop=(j == 15))
                            p.tick("t", i)
                            lastuse[ot % 2] = p.cnt["t"]
                            p.wait("s", "t")
                            p.wait("v", "t")
                            if ot < 32:
                                i = nc.scalar.activation(sz[:, ot * CH:(ot + 1) * CH],
                                                         po[:, :], ACT.Silu)
                                p.tick("s", i)
                                ev_v[ot % 2] = p.cnt["v"]
                                ev_s[ot % 2] = p.cnt["s"]
                            elif ot < 66:
                                c = ot - 32
                                nc.vector.tensor_copy(ext[:, 0:3], halo[:, c * 3:c * 3 + 3])
                                i = nc.scalar.copy(ext[:, 3:CH + 3], po[:, :])
                                p.tick("s", i)
                                ev_s[ot % 2] = p.cnt["s"]
                                ev_v[ot % 2] = p.cnt["v"]
                                p.wait("v", "s")
                                p.vsync()
                                i = nc.vector.tensor_copy(halo[:, c * 3:c * 3 + 3], ext[:, CH:CH + 3])
                                p.tick("v", i)
                                p.vsync()
                                nc.vector.tensor_scalar(accb[:, :], ext[:, 0:CH],
                                                        convw[:, c * 4:c * 4 + 1], None, ALU.mult)
                                for k in (1, 2, 3):
                                    i = nc.vector.scalar_tensor_tensor(
                                        accb[:, :], ext[:, k:k + CH], convw[:, c * 4 + k:c * 4 + k + 1],
                                        accb[:, :], ALU.mult, ALU.add)
                                p.tick("v", i)
                                p.wait("s", "v")
                                i = nc.scalar.activation(xbcs[:, c * CH:(c + 1) * CH],
                                                         accb[:, :], ACT.Silu,
                                                         bias=convb[:, c:c + 1])
                                p.tick("s", i)
                            else:
                                i = nc.vector.tensor_scalar(dtx, po[0:64, :], dtb[:, 0:1],
                                                            None, ALU.add)
                                p.tick("v", i)
                                ev_v[ot % 2] = p.cnt["v"]
                                ev_s[ot % 2] = p.cnt["s"]
                                p.wait("s", "v")
                                nc.scalar.activation(e2[0:64, :], dtx, ACT.Abs)
                                nc.scalar.activation(e2[0:64, :], e2[0:64, :], ACT.Exp, scale=-1.0)
                                i = nc.scalar.activation(e2[0:64, :], e2[0:64, :], ACT.Ln, bias=1.0)
                                p.tick("s", i)
                                p.wait("v", "s")
                                nc.vector.scalar_tensor_tensor(dtv, dtx, 0.0,
                                                               e2[0:64, :], ALU.max, ALU.add)
                                p.vsync()
                                i = nc.vector.tensor_scalar(adv, dtv, acol[:, 0:1],
                                                            None, ALU.mult)
                                p.tick("v", i)
                                p.wait("t", "v")
                                for tt in range(NTT):
                                    i = nc.tensor.transpose(pt2[0:128, :], dtv[:, tt * 128:(tt + 1) * 128],
                                                            identf[0:64, 0:64])
                                    p.tick("t", i)
                                    p.wait("s", "t")
                                    nc.scalar.copy(dtT[:, (2 * tt) * 64:(2 * tt + 1) * 64], pt2[0:64, :])
                                    i = nc.scalar.copy(dtT[:, (2 * tt + 1) * 64:(2 * tt + 2) * 64], pt2[64:128, :])
                                    p.tick("s", i)
                                    p.wait("t", "s")
                                    i = nc.tensor.transpose(pt2[0:128, :], adv[:, tt * 128:(tt + 1) * 128],
                                                            identf[0:64, 0:64])
                                    p.tick("t", i)
                                    p.wait("s", "t")
                                    nc.scalar.copy(adT[:, (2 * tt) * 64:(2 * tt + 1) * 64], pt2[0:64, :])
                                    i = nc.scalar.copy(adT[:, (2 * tt + 1) * 64:(2 * tt + 2) * 64], pt2[64:128, :])
                                    p.tick("s", i)
                                    p.wait("t", "s")
                    with nc.psum_tensor("mb_ptb" + UC, [128, NTT * 128], BF16) as ptb:
                        p.wait("t", "v")
                        p.wait("t", "s")
                        for tt in range(NTT):
                            i = nc.tensor.transpose(ptb[:, tt * 128:(tt + 1) * 128],
                                                    xbcs[:, 32 * CH + tt * 128:32 * CH + (tt + 1) * 128],
                                                    ident[:, :])
                        p.tick("t", i)
                        p.wait("s", "t")
                        for tt in range(NTT):
                            nc.scalar.copy(BT[:, (2 * tt) * 128:(2 * tt + 1) * 128],
                                           ptb[0:64, tt * 128:(tt + 1) * 128])
                            i = nc.scalar.copy(BT[:, (2 * tt + 1) * 128:(2 * tt + 2) * 128],
                                               ptb[64:128, tt * 128:(tt + 1) * 128])
                        p.tick("s", i)
                    barrier(p)
                esB = contextlib.ExitStack()
                TB = lambda n, s, d=F32: esB.enter_context(nc.sbuf_tensor(n + UC + "S", s, d))
                XTb = TB("mb_XTb", [64, 4096], BF16)
                rhsD = TB("mb_rhsD", [64, 2048])
                rhsH = TB("mb_rhsH", [64, 2048], BF16)
                rhsL = TB("mb_rhsL", [64, 2048], BF16)
                Mh = TB("mb_Mh", [64, 4096], BF16)
                Gm = TB("mb_Gm", [64, 64], BF16)
                dt63 = TB("mb_dt63", [64, 64])
                en1 = TB("mb_en1", [64, 64])
                decdtT = TB("mb_decdt", [64, 64])
                expAcT = TB("mb_expAcT", [64, 64])
                lam = TB("mb_lam", [1, 64])
                lamlogm = TB("mb_lamlogm", [128, 64])
                lamm = TB("mb_lamm", [128, 64])
                Bd = TB("mb_Bd", [64, 2048], BF16)
                Yb = TB("mb_Yb", [64, 4096], BF16)
                y2b = TB("mb_y2", [128, 1024])
                sq16 = TB("mb_sq16", [128, 1024], BF16)
                with esB:
                    for b in range(NBL):
                        dtTb = dtT[0:64, b * 64:(b + 1) * 64]
                        adTb = adT[0:64, b * 64:(b + 1) * 64]
                        Cblk = xbcs[:, 33 * CH + b * 64:33 * CH + (b + 1) * 64]
                        Bblk = xbcs[:, 32 * CH + b * 64:32 * CH + (b + 1) * 64]
                        with nc.psum_tensor(f"mb_pxt{b}" + UC, [64, 2048], BF16) as pxt:
                            for half in range(2):
                                p.wait("t", "s")
                                p.wait("t", "v")
                                for cj in range(16):
                                    ct = half * 16 + cj
                                    i = nc.tensor.transpose(
                                        pxt[:, cj * 128:(cj + 1) * 128],
                                        xbcs[:, ct * CH + b * 64:ct * CH + (b + 1) * 64],
                                        ident[:, :])
                                p.tick("t", i)
                                p.wait("s", "t")
                                i = nc.scalar.copy(XTb[:, half * 2048:(half + 1) * 2048],
                                                   pxt[:, :])
                                p.tick("s", i)
                        with (nc.psum_tensor(f"mb_pG{b}" + UC, [64, 64], F32) as pG,
                              nc.psum_tensor(f"mb_pD{b}" + UC, [64, 2048], F32) as pD):
                            p.wait("t", "v")
                            p.wait("t", "s")
                            i = nc.tensor.matmul(pG[:, :], Bblk, Cblk, start=True, stop=True)
                            p.tick("t", i)
                            p.wait("v", "t")
                            i = nc.vector.tensor_tensor(Gm[:, :], pG[:, :], M1m[:, :], ALU.mult)
                            p.tick("v", i)
                            for hg in range(2):
                                p.wait("v", "s")
                                p.vsync()
                                nc.vector.tensor_tensor(
                                    rhsD[:, :],
                                    bass.AP(adT, b * 64 + hg * 32, [[NBL * 64, 64], [1, 32], [0, 64]]),
                                    bass.AP(M1m, 0, [[64, 64], [0, 32], [1, 64]]),
                                    ALU.mult)
                                nc.vector.tensor_copy(rhsH[:, :], rhsD[:, :])
                                i = nc.vector.tensor_tensor(rhsL[:, :], rhsD[:, :], rhsH[:, :], ALU.subtract)
                                p.tick("v", i)
                                p.wait("t", "v")
                                p.wait("t", "s")
                                for seg in range(4):
                                    nc.tensor.matmul(pD[:, seg * 512:(seg + 1) * 512],
                                                     L1[:, :], rhsH[:, seg * 512:(seg + 1) * 512],
                                                     start=True, stop=False)
                                    i = nc.tensor.matmul(pD[:, seg * 512:(seg + 1) * 512],
                                                         L1[:, :], rhsL[:, seg * 512:(seg + 1) * 512],
                                                         start=False, stop=True)
                                p.tick("t", i)
                                p.wait("s", "t")
                                i = nc.scalar.activation(Mh[:, hg * 2048:(hg + 1) * 2048],
                                                         pD[:, :], ACT.Exp)
                                p.tick("s", i)
                                p.wait("v", "s")
                                i = nc.vector.tensor_copy(
                                    dt63[:, hg * 32:hg * 32 + 32],
                                    bass.AP(pD, 63, [[2048, 64], [64, 32], [1, 1]]))
                                p.tick("v", i)
                                p.vsync()
                                nc.vector.tensor_tensor(
                                    Mh[:, hg * 2048:(hg + 1) * 2048],
                                    Mh[:, hg * 2048:(hg + 1) * 2048],
                                    bass.AP(Gm, 0, [[64, 64], [0, 32], [1, 64]]),
                                    ALU.mult)
                                i = nc.vector.tensor_tensor(
                                    Mh[:, hg * 2048:(hg + 1) * 2048],
                                    Mh[:, hg * 2048:(hg + 1) * 2048],
                                    bass.AP(dtT, b * 64 + hg * 32, [[NBL * 64, 64], [1, 32], [0, 64]]),
                                    ALU.mult)
                                p.tick("v", i)
                        p.vsync()
                        i = nc.vector.tensor_tensor(lam[0:1, :], dt63[0:1, :], adTb[0:1, :], ALU.add)
                        p.tick("v", i)
                        p.wait("s", "v")
                        i = nc.scalar.activation(decdtT[:, :], dt63[:, :], ACT.Exp)
                        p.tick("s", i)
                        p.wait("t", "v")
                        with nc.psum_tensor(f"mb_pL{b}" + UC, [128, 64], F32) as pL:
                            i = nc.tensor.matmul(pL[:, :], onesf[0:1, :], lam[0:1, :],
                                                 start=True, stop=True)
                            p.tick("t", i)
                            p.wait("v", "t")
                            i = nc.vector.tensor_copy(lamlogm[:, :], pL[:, :])
                            p.tick("v", i)
                        p.vsync()
                        nc.vector.tensor_tensor(en1[:, :], lamlogm[0:64, :], dt63[:, :], ALU.subtract)
                        i = nc.vector.tensor_tensor(decdtT[:, :], decdtT[:, :], dtTb, ALU.mult)
                        p.tick("v", i)
                        p.wait("s", "v")
                        nc.scalar.activation(expAcT[:, :], en1[:, :], ACT.Exp)
                        i = nc.scalar.activation(lamm[:, :], lamlogm[:, :], ACT.Exp)
                        p.tick("s", i)
                        p.wait("v", "s")
                        p.wait("t", "s")
                        for hg in range(4):
                            i = nc.vector.tensor_tensor(
                                Bd[:, :],
                                bass.AP(BT, b * 128, [[NBL * 128, 64], [0, 16], [1, 128]]),
                                bass.AP(decdtT, hg * 16, [[64, 64], [1, 16], [0, 128]]),
                                ALU.mult)
                            p.tick("v", i)
                            with (nc.psum_tensor(f"mb_pYd{b}_{hg}" + UC, [64, 1024], F32) as pYd,
                                  nc.psum_tensor(f"mb_pYo{b}_{hg}" + UC, [64, 1024], F32) as pYo,
                                  nc.psum_tensor(f"mb_pS{b}_{hg}" + UC, [128, 1024], F32) as pS):
                                p.wait("t", "v")
                                p.wait("t", "s")
                                for seg in range(2):
                                    nc.tensor.matmul(
                                        pYo[:, seg * 512:(seg + 1) * 512], Cblk,
                                        Sbf[:, hg * 1024 + seg * 512:hg * 1024 + (seg + 1) * 512],
                                        start=True, stop=True)
                                for hl in range(16):
                                    h = hg * 16 + hl
                                    xsl = XTb[:, h * 64:(h + 1) * 64]
                                    nc.tensor.matmul(pYd[:, hl * 64:(hl + 1) * 64],
                                                     Mh[:, h * 64:(h + 1) * 64], xsl,
                                                     start=True, stop=False)
                                    nc.tensor.matmul(pYd[:, hl * 64:(hl + 1) * 64],
                                                     ddiag[:, h * 64:(h + 1) * 64], xsl,
                                                     start=False, stop=True)
                                    i = nc.tensor.matmul(pS[:, hl * 64:(hl + 1) * 64],
                                                         Bd[:, hl * 128:(hl + 1) * 128], xsl,
                                                         start=True, stop=True)
                                p.tick("t", i)
                                p.wait("v", "t")
                                so = Sf[(ch * NBL + b) % 2]
                                sn = Sf[(ch * NBL + b + 1) % 2]
                                nc.vector.tensor_tensor(
                                    sn[:, hg * 1024:(hg + 1) * 1024],
                                    so[:, hg * 1024:(hg + 1) * 1024],
                                    bass.AP(lamm, hg * 16, [[64, 128], [1, 16], [0, 64]]),
                                    ALU.mult)
                                nc.vector.tensor_tensor(
                                    sn[:, hg * 1024:(hg + 1) * 1024],
                                    sn[:, hg * 1024:(hg + 1) * 1024],
                                    pS[:, :], ALU.add)
                                nc.vector.tensor_tensor(
                                    y2b[0:64, :], pYo[:, :],
                                    bass.AP(expAcT, hg * 16, [[64, 64], [1, 16], [0, 64]]),
                                    ALU.mult)
                                i = nc.vector.tensor_tensor(
                                    Yb[:, hg * 1024:(hg + 1) * 1024],
                                    y2b[0:64, :], pYd[:, :], ALU.add)
                                p.tick("v", i)
                        sn = Sf[(ch * NBL + b + 1) % 2]
                        i = nc.vector.tensor_copy(Sbf[:, :], sn[:, :])
                        p.tick("v", i)
                        with (nc.psum_tensor(f"mb_pyt{b}" + UC, [128, 1024], BF16) as pyt,
                              nc.psum_tensor(f"mb_prm{b}" + UC, [1, 64], F32) as prm,
                              nc.psum_tensor(f"mb_prt{b}" + UC, [64, 1], F32) as prt):
                            for half in range(2):
                                p.wait("t", "v")
                                for cj in range(16):
                                    ct = half * 16 + cj
                                    i = nc.tensor.transpose(
                                        pyt[:, cj * 64:(cj + 1) * 64],
                                        Yb[:, ct * 128:(ct + 1) * 128],
                                        ident[0:64, 0:64])
                                p.tick("t", i)
                                p.wait("v", "t")
                                nc.vector.tensor_tensor(
                                    y2b[:, :], pyt[:, :],
                                    bass.AP(sz, half * 16 * CH + b * 64,
                                            [[32 * CH, 128], [CH, 16], [1, 64]]),
                                    ALU.mult)
                                i = nc.vector.tensor_tensor(
                                    bass.AP(y3, half * 16 * CH + b * 64,
                                            [[32 * CH, 128], [CH, 16], [1, 64]]),
                                    y2b[:, :],
                                    bass.AP(gcol, half * 16, [[32, 128], [1, 16], [0, 64]]),
                                    ALU.mult)
                                p.tick("v", i)
                                p.wait("s", "v")
                                i = nc.scalar.activation(sq16[:, :], y2b[:, :], ACT.Square)
                                p.tick("s", i)
                                p.wait("t", "s")
                                for cj in range(16):
                                    i = nc.tensor.matmul(prm[:, :], onescol[:, 0:1],
                                                         sq16[:, cj * 64:(cj + 1) * 64],
                                                         start=(half == 0 and cj == 0),
                                                         stop=(half == 1 and cj == 15))
                                p.tick("t", i)
                            p.wait("s", "t")
                            nc.scalar.activation(e1[0:1, 0:64], prm[:, :], ACT.Ln,
                                                 scale=1.0 / 4096.0, bias=1e-5)
                            i = nc.scalar.activation(lam[0:1, :], e1[0:1, 0:64], ACT.Exp,
                                                     scale=-0.5)
                            p.tick("s", i)
                            p.wait("t", "s")
                            i = nc.tensor.matmul(prt[:, :], lam[0:1, :], onesf[0:1, 0:1],
                                                 start=True, stop=True)
                            p.tick("t", i)
                            p.wait("v", "t")
                            tt0 = b // 2
                            pl = (b % 2) * 64
                            i = nc.vector.tensor_copy(rT[pl:pl + 64, tt0:tt0 + 1], prt[:, :])
                            p.tick("v", i)
                    barrier(p)
                esD = contextlib.ExitStack()
                TD = lambda n, s, d=F32: esD.enter_context(nc.sbuf_tensor(n + UC + "D", s, d))
                wos = [TD(f"mb_wos{i}", [128, 2048], BF16) for i in range(2)]
                xs2 = [TD(f"mb_xs{i}", [128, 2048]) for i in range(2)]
                ob = [TD(f"mb_ob{i}", [128, 2048]) for i in range(2)]
                with esD:
                    with (nc.psum_tensor("mb_pout0" + UC, [128, 2048], F32) as pout0,
                          nc.psum_tensor("mb_pout1" + UC, [128, 2048], F32) as pout1):
                        pouts = [pout0, pout1]
                        for tt in range(NTT):
                            dma(p, "q", xs2[tt][:, :],
                                bass.AP(src, (ch * NTT + tt) * 128 * 2048, [[2048, 128], [1, 2048]]))
                        def load_wos(cj, buf):
                            dma(p, "q", wos[buf][:, :],
                                bass.AP(woT, cj * 128 * 2048, [[2048, 128], [1, 2048]]))
                            return p.cnt["q"]
                        wready = [load_wos(0, 0), 0]
                        lastw = [0, 0]
                        for cj in range(32):
                            if cj + 1 < 32:
                                if cj >= 1:
                                    p.wait("q", "t", lastw[(cj + 1) % 2])
                                wready[(cj + 1) % 2] = load_wos(cj + 1, (cj + 1) % 2)
                            p.wait("t", "q", wready[cj % 2])
                            if cj == 0:
                                p.wait("t", "v")
                            for tt in range(NTT):
                                for dch in range(4):
                                    i = nc.tensor.matmul(
                                        pouts[tt][:, dch * 512:(dch + 1) * 512],
                                        y3[:, cj * CH + tt * 128:cj * CH + (tt + 1) * 128],
                                        wos[cj % 2][:, dch * 512:(dch + 1) * 512],
                                        start=(cj == 0), stop=(cj == 31))
                            p.tick("t", i)
                            lastw[cj % 2] = p.cnt["t"]
                        p.wait("v", "t")
                        for tt in range(NTT):
                            i = nc.vector.scalar_tensor_tensor(
                                ob[tt][:, :], pouts[tt][:, :], rT[:, tt:tt + 1],
                                xs2[tt][:, :], ALU.mult, ALU.add)
                            p.tick("v", i)
                            p.wait("q", "v")
                            dma(p, "q", bass.AP(dst, (ch * NTT + tt) * 128 * 2048,
                                                [[2048, 128], [1, 2048]]),
                                ob[tt][:, :])
    barrier(p)
    barrier(p)


import contextlib
from concourse.bass_utils import run_bass_kernel_spmd


def _build_program():
    nc = bass.Bass("TRN2", target_bir_lowering=False, detect_race_conditions=False)
    p = Prog(nc)
    T, H, I = 4096, 2048, 8192
    x_in = nc.dram_tensor("x", [T, H], F32, kind="ExternalInput")
    wg_in = nc.dram_tensor("w_gate", [I, H], F32, kind="ExternalInput")
    wu_in = nc.dram_tensor("w_up", [I, H], F32, kind="ExternalInput")
    wd_in = nc.dram_tensor("w_down", [H, I], F32, kind="ExternalInput")
    wip = nc.dram_tensor("in_proj_pad", [8576, H], F32, kind="ExternalInput")
    wop = nc.dram_tensor("out_proj", [H, 4096], F32, kind="ExternalInput")
    cw_in = nc.dram_tensor("conv_w", [4352, 4], F32, kind="ExternalInput")
    cb_in = nc.dram_tensor("conv_b", [4352], F32, kind="ExternalInput")
    al_in = nc.dram_tensor("A_log", [64], F32, kind="ExternalInput")
    dv_in = nc.dram_tensor("Dv", [64], F32, kind="ExternalInput")
    dtb_in = nc.dram_tensor("dt_bias", [64], F32, kind="ExternalInput")
    nw_in = nc.dram_tensor("norm_w", [4096], F32, kind="ExternalInput")
    c_in = nc.dram_tensor("consts", [128, 64], F32, kind="ExternalInput")
    id_in = nc.dram_tensor("ident", [128, 128], BF16, kind="ExternalInput")
    idf_in = nc.dram_tensor("identf", [128, 128], F32, kind="ExternalInput")
    L1_in = nc.dram_tensor("L1", [64, 64], BF16, kind="ExternalInput")
    M1_in = nc.dram_tensor("M1m", [64, 64], BF16, kind="ExternalInput")
    o1_in = nc.dram_tensor("ones1", [1, 128], BF16, kind="ExternalInput")
    oc_in = nc.dram_tensor("onescol", [128, 1], BF16, kind="ExternalInput")
    ie_in = nc.dram_tensor("Ieye64", [64, 64], BF16, kind="ExternalInput")
    of_in = nc.dram_tensor("onesf", [1, 128], F32, kind="ExternalInput")
    out = nc.dram_tensor("out", [T, H], F32, kind="ExternalOutput")
    wgT = nc.dram_tensor("wgT", [H, I], BF16)
    wuT = nc.dram_tensor("wuT", [H, I], BF16)
    wdT = nc.dram_tensor("wdT", [I, H], BF16)
    interT = nc.dram_tensor("interT", [64 * 32 * 16384], BF16)
    winT_t = nc.dram_tensor("winT_t", [67 * 16 * 16384], BF16)
    woT = nc.dram_tensor("woT", [4096, 2048], BF16)
    resid1 = nc.dram_tensor("resid1", [T, H], F32)
    resid2 = nc.dram_tensor("resid2", [T, H], F32)

    es0 = contextlib.ExitStack()
    p.alloc_consts(es0)
    p.alloc_sems(es0)
    identt = es0.enter_context(nc.sbuf_tensor("identt", [128, 128], BF16))
    identf = es0.enter_context(nc.sbuf_tensor("identft", [128, 128], F32))
    L1t = es0.enter_context(nc.sbuf_tensor("L1t", [64, 64], BF16))
    M1t = es0.enter_context(nc.sbuf_tensor("M1t", [64, 64], BF16))
    o1t = es0.enter_context(nc.sbuf_tensor("o1t", [1, 128], BF16))
    oct_ = es0.enter_context(nc.sbuf_tensor("oct", [128, 1], BF16))
    iet = es0.enter_context(nc.sbuf_tensor("iet", [64, 64], BF16))
    oft = es0.enter_context(nc.sbuf_tensor("oft", [1, 128], F32))
    with es0:
        register_qmx_consts(p)
        for v in (1.0, -1.0, 0.5, -0.5, 1e18, 1e-5, 1.0 / H, 1.0 / 4096.0,
                  float(-162.0 * LN2 + np.log(0.5))):
            p.C(v)
        p.eng.update(v=nc.vector, s=nc.scalar, t=nc.tensor, g=nc.gpsimd,
                     q=nc.sync, a=nc.scalar)
        for dst_t, src_t in ((p.ctab, c_in), (identt, id_in), (identf, idf_in),
                             (L1t, L1_in), (M1t, M1_in), (o1t, o1_in),
                             (oct_, oc_in), (iet, ie_in), (oft, of_in)):
            dma(p, "q", dst_t[:, :], src_t.ap())
        nc.sync.wait_ge(p.sem["q"], p.cnt["q"])
        nc.all_engine_barrier()
        es1 = contextlib.ExitStack()
        qt = qmx_scratch2(nc, es1, 2048, prefix="pq")
        with es1:
            emit_ffn_weight_prep(p, wg_in, 64, H, wgT, qt, identt)
            emit_ffn_weight_prep(p, wu_in, 64, H, wuT, qt, identt)
            emit_ffn_weight_prep(p, wd_in, 16, I, wdT, qt, identt, out_scale=0.5)
            emit_plain_weight_prep(p, wip, 67, H, winT_t, identt, tiled=True)
            emit_plain_weight_prep(p, wop, 16, 4096, woT, identt)
            barrier(p)
        params = dict(conv_w=cw_in, conv_b=cb_in, A_log=al_in, Dv=dv_in,
                      dt_bias=dtb_in, norm_w=nw_in)
        cm = dict(ident=identt, identf=identf, L1=L1t, M1m=M1t, ones1=o1t,
                  onescol=oct_, Ieye64=iet, onesf=oft)
        emit_ffn_stage(p, x_in, resid1, wgT, wuT, wdT, interT, identt)
        emit_mamba_stage(p, resid1, resid2, winT_t, woT, params, cm)
        emit_ffn_stage(p, resid2, out, wgT, wuT, wdT, interT, identt)
    return nc, p


_CACHED = {}


def kernel(L0_attn_x, L0_attn_in_proj_w, L0_attn_conv_w, L0_attn_conv_b,
           L0_attn_A_log, L0_attn_D, L0_attn_dt_bias, L0_attn_norm_w,
           L0_attn_out_proj_w, L0_ffn_w_gate, L0_ffn_w_up, L0_ffn_w_down):
    import ml_dtypes
    if "nc" not in _CACHED:
        _CACHED["nc"], _CACHED["p"] = _build_program()
    nc, p = _CACHED["nc"], _CACHED["p"]
    x = np.ascontiguousarray(np.asarray(L0_attn_x, dtype=np.float32))
    in_proj_pad = np.zeros((8576, 2048), np.float32)
    in_proj_pad[:8512] = np.asarray(L0_attn_in_proj_w, np.float32)
    kk, ll = np.meshgrid(np.arange(64), np.arange(64), indexing="ij")
    common = {
        "w_gate": np.ascontiguousarray(np.asarray(L0_ffn_w_gate, np.float32)),
        "w_up": np.ascontiguousarray(np.asarray(L0_ffn_w_up, np.float32)),
        "w_down": np.ascontiguousarray(np.asarray(L0_ffn_w_down, np.float32)),
        "in_proj_pad": in_proj_pad,
        "out_proj": np.ascontiguousarray(np.asarray(L0_attn_out_proj_w, np.float32)),
        "conv_w": np.ascontiguousarray(np.asarray(L0_attn_conv_w, np.float32).reshape(4352, 4)),
        "conv_b": np.ascontiguousarray(np.asarray(L0_attn_conv_b, np.float32)),
        "A_log": np.ascontiguousarray(np.asarray(L0_attn_A_log, np.float32)),
        "Dv": np.ascontiguousarray(np.asarray(L0_attn_D, np.float32)),
        "dt_bias": np.ascontiguousarray(np.asarray(L0_attn_dt_bias, np.float32)),
        "norm_w": np.ascontiguousarray(np.asarray(L0_attn_norm_w, np.float32)),
        "consts": p.consts_array(),
        "ident": np.eye(128).astype(ml_dtypes.bfloat16),
        "identf": np.eye(128).astype(np.float32),
        "L1": (kk > ll).astype(ml_dtypes.bfloat16),
        "M1m": (kk <= ll).astype(ml_dtypes.bfloat16),
        "ones1": np.ones((1, 128), ml_dtypes.bfloat16),
        "onescol": np.ones((128, 1), ml_dtypes.bfloat16),
        "Ieye64": np.eye(64).astype(ml_dtypes.bfloat16),
        "onesf": np.ones((1, 128), np.float32),
    }
    in_maps = [dict(common, x=np.ascontiguousarray(x[b])) for b in range(8)]
    res = run_bass_kernel_spmd(nc, in_maps, core_ids=list(range(8)))
    return np.stack([np.asarray(res.results[b]["out"]) for b in range(8)], axis=0)


# version 10 (final)
# speedup vs baseline: 1.1751x; 1.1525x over previous
import contextlib
import numpy as np
import ml_dtypes
import concourse.bass as bass
import concourse.mybir as mybir

F32 = mybir.dt.float32
BF16 = mybir.dt.bfloat16
I32 = mybir.dt.int32
ALU = mybir.AluOpType
ACT = mybir.ActivationFunctionType
AX = mybir.AxisListType
LN2 = float(np.log(2.0))


def bcast(t, parts, nb, rep, col0=0):
    return bass.AP(t, col0, [[t.shape[1], parts], [1, nb], [0, rep]])


def bcast_h(t, parts, nh, nrep, col0=0, hstride=1):
    return bass.AP(t, col0, [[t.shape[1], parts], [hstride, nh], [0, nrep]])


class Prog:
    ENGS = ("v", "s", "t", "g", "q", "a")

    _UID = [0]

    def __init__(self, nc):
        self.nc = nc
        self.const_vals = []
        self.const_cols = {}
        self.ctab = None
        self.sem = {}
        self.cnt = {e: 0 for e in self.ENGS}
        self.eng = {}

    def alloc_consts(self, es, max_consts=64):
        self.ctab = es.enter_context(self.nc.sbuf_tensor("ctab", [128, max_consts], F32))
        self.max_consts = max_consts

    def alloc_sems(self, es):
        for e in self.ENGS:
            self.sem[e] = es.enter_context(self.nc.semaphore("sem_" + e))

    def C(self, val):
        val = float(val)
        if val in self.const_cols:
            return
        col = len(self.const_vals)
        assert col < self.max_consts, "const table full"
        self.const_cols[val] = col
        self.const_vals.append(val)
        self.nc.const_aps.register(F32, val, self.ctab[:, col:col + 1])

    def consts_array(self):
        arr = np.zeros((128, self.max_consts), np.float32)
        arr[:, :len(self.const_vals)] = np.array(self.const_vals, np.float32)[None, :]
        return arr

    def register_common_consts(self):
        for v in (1e-30, 162.0 * LN2, -162.0 * LN2, 1.0, 0.5, -1.0, 2.0, -0.5, 0.0,
                  1e-5, 1.0 / 4096.0, -0.5 * 1.0):
            self.C(v)

    def tick(self, key, inst, n=1):
        self.cnt[key] += n
        inst.then_inc(self.sem[key], n)
        return self.cnt[key]

    def vsync(self):
        i = self.nc.vector.engine_nop()
        self.tick("v", i)
        self.nc.vector.wait_ge(self.sem["v"], self.cnt["v"])

    def wait(self, from_key, on_key, value=None):
        if value is None:
            value = self.cnt[on_key]
        self.eng[from_key].wait_ge(self.sem[on_key], value)


def emit_floor(p, out, in_ap, itmp, ftmp, gtmp, shift=0.0):
    v, sca = p.nc.vector, p.nc.scalar
    i = v.tensor_scalar(itmp, in_ap, shift, None, ALU.add)
    p.tick("v", i)
    p.wait("s", "v")
    i = sca.copy(ftmp, itmp)
    p.tick("s", i)
    p.wait("v", "s")
    v.scalar_tensor_tensor(gtmp, in_ap, shift, ftmp, ALU.add, ALU.is_lt)
    p.vsync()
    return v.tensor_tensor(out, ftmp, gtmp, ALU.subtract)


def emit_qmx(p, vec, sca, vk, sk, src_ap, H, dst_bf, mfrac, t):
    mf = float(mfrac)
    max_norm = 7.5 if mf == 3.0 else 6.0
    nb = H // 32
    i = vec.tensor_reduce(t["amax"][:, :nb],
                          bass.AP(src_ap.tensor, src_ap.offset,
                                  [list(src_ap.ap[0])] + [[32, nb], [1, 32]]),
                          AX.X, ALU.max, apply_absolute_value=True)
    p.tick(vk, i)
    p.wait(sk, vk)
    i = sca.activation(t["lg"][:, :nb], t["amax"][:, :nb], ACT.Ln, bias=1e-30)
    p.tick(sk, i)
    p.wait(vk, sk)
    vec.tensor_scalar(t["tpos"][:, :nb], t["lg"][:, :nb], 1.0 / LN2, 160.0, ALU.mult, ALU.add)
    p.vsync()
    i = emit_floor(p, t["s160"][:, :nb], t["tpos"][:, :nb], t["ii"][:, :nb],
                   t["ff"][:, :nb], t["gg"][:, :nb])
    p.tick(vk, i)
    p.wait(sk, vk)
    i = sca.activation(t["iscale"][:, :nb], t["s160"][:, :nb], ACT.Exp,
                       bias=162.0 * LN2, scale=-LN2)
    i = sca.activation(t["scale"][:, :nb], t["s160"][:, :nb], ACT.Exp,
                       bias=-162.0 * LN2, scale=LN2)
    p.tick(sk, i)
    p.wait(vk, sk)
    y = t["y"][:, :H]
    sq = t["sq"][:, :H]
    vec.tensor_tensor(y, src_ap, bcast(t["iscale"], 128, nb, 32), ALU.mult)
    vec.tensor_tensor(sq, y, y, ALU.mult)
    vec.tensor_scalar(t["p1"][:, :H], sq, 4.0, -(2.0 ** (mf - 1)), ALU.is_ge, ALU.mult)
    vec.tensor_scalar(t["p2"][:, :H], sq, 16.0, -(2.0 ** (mf - 2)), ALU.is_ge, ALU.mult)
    vec.scalar_tensor_tensor(t["iqs"][:, :H], t["p1"][:, :H], 2.0 ** mf, t["p2"][:, :H], ALU.add, ALU.add)
    vec.tensor_scalar(t["q1"][:, :H], sq, 4.0, 2.0 ** (-mf), ALU.is_ge, ALU.mult)
    vec.tensor_scalar(t["q2"][:, :H], sq, 16.0, 2.0 ** (1 - mf), ALU.is_ge, ALU.mult)
    vec.scalar_tensor_tensor(t["qs"][:, :H], t["q1"][:, :H], 2.0 ** (-mf), t["q2"][:, :H], ALU.add, ALU.add)
    v_ = t["v"][:, :H]
    vec.tensor_tensor(v_, y, t["iqs"][:, :H], ALU.mult)
    i = vec.tensor_scalar(t["ri"][:, :H], v_, 64.5, None, ALU.add)
    p.tick("v", i)
    p.wait("s", "v")
    i = sca.copy(t["rf"][:, :H], t["ri"][:, :H])
    p.tick("s", i)
    p.wait("v", "s")
    vec.scalar_tensor_tensor(t["rg"][:, :H], v_, 64.5, t["rf"][:, :H], ALU.add, ALU.is_lt)
    vec.tensor_tensor(t["ru"][:, :H], t["rf"][:, :H], t["rg"][:, :H], ALU.subtract)
    vec.scalar_tensor_tensor(t["q0"][:, :H], t["ru"][:, :H], 64.0, t["qs"][:, :H], ALU.subtract, ALU.mult)
    vec.tensor_scalar(t["qc"][:, :H], t["q0"][:, :H], max_norm, -max_norm, ALU.min, ALU.max)
    i = vec.tensor_tensor(dst_bf, t["qc"][:, :H], bcast(t["scale"], 128, nb, 32), ALU.mult)
    return i


def qmx_scratch(nc, es, H, prefix=""):
    nb = H // 32
    t = {}
    for n in ("y", "sq", "p1", "p2", "iqs", "q1", "q2", "qs", "v", "rf", "rg", "ru", "q0", "qc"):
        t[n] = es.enter_context(nc.sbuf_tensor(prefix + n, [128, H], F32))
    t["ri"] = es.enter_context(nc.sbuf_tensor(prefix + "ri", [128, H], I32))
    for n in ("amax", "lg", "tpos", "ff", "gg", "s160", "iscale", "scale"):
        t[n] = es.enter_context(nc.sbuf_tensor(prefix + n, [128, nb], F32))
    t["ii"] = es.enter_context(nc.sbuf_tensor(prefix + "ii", [128, nb], I32))
    return t


def register_qmx_consts(p):
    for v in (1e-30, 162.0 * LN2, -162.0 * LN2, 64.5):
        p.C(v)


def emit_sigma(p, vec, sca, vk, sk, src_ap, dst, e1, e2):
    i = sca.activation(e1, src_ap, ACT.Exp, scale=-1.0)
    p.tick(sk, i)
    p.wait(vk, sk)
    i = vec.tensor_scalar(e2, e1, 1e18, None, ALU.min)
    p.tick(vk, i)
    p.wait(sk, vk)
    i = sca.activation(e1, e2, ACT.Ln, bias=1.0)
    i = sca.activation(dst, e1, ACT.Exp, scale=-1.0)
    return i


def np_quantize_mx(x, ebits, mfrac, emax_elem, max_norm, block=32):
    f = np.float32
    shp = x.shape
    xb = x.reshape(shp[:-1] + (shp[-1] // block, block)).astype(f)
    amax = np.max(np.abs(xb), axis=-1, keepdims=True)
    sexp = np.floor(np.log2(np.where(amax == 0, f(1.0), amax))).astype(f) - f(emax_elem)
    sexp = np.clip(sexp, -127.0, 127.0)
    scale = np.exp2(sexp).astype(f)
    y = (xb / scale).astype(f)
    ax = np.abs(y)
    priv = np.floor(np.log2(np.where(ax == 0, f(1.0), ax))).astype(f)
    priv = np.maximum(priv, f(-2 ** (ebits - 1) + 2))
    qs = np.exp2(priv - f(mfrac)).astype(f)
    q = (np.round(y / qs) * qs).astype(f)
    q = np.clip(q, -max_norm, max_norm)
    return (q * scale).astype(f).reshape(shp)



def emit_qmx_chunked(p, src_ap_fn, W, dst_ap_fn, mfrac, t, chunk=2048, out_scale=1.0):
    for c0 in range(0, W, chunk):
        w = min(chunk, W - c0)
        emit_qmx2(p, src_ap_fn(c0, w), w, dst_ap_fn(c0, w), mfrac, t, out_scale=out_scale)


def emit_qmx2(p, src_ap, H, dst_bf, mfrac, t, out_scale=1.0):
    vec, sca = p.nc.vector, p.nc.scalar
    mf = float(mfrac)
    max_norm = 7.5 if mf == 3.0 else 6.0
    nb = H // 32
    b = [t[f"b{i}"] for i in range(6)]
    ib = t["ib"]
    i = vec.tensor_reduce(t["amax"][:, :nb],
                          bass.AP(src_ap.tensor, src_ap.offset,
                                  [list(src_ap.ap[0])] + [[32, nb], [1, 32]]),
                          AX.X, ALU.max, apply_absolute_value=True)
    p.tick("v", i)
    p.wait("s", "v")
    i = sca.activation(t["lg"][:, :nb], t["amax"][:, :nb], ACT.Ln, bias=1e-30)
    p.tick("s", i)
    p.wait("v", "s")
    vec.tensor_scalar(t["tpos"][:, :nb], t["lg"][:, :nb], 1.0 / LN2, 160.0, ALU.mult, ALU.add)
    p.vsync()
    i = emit_floor(p, t["s160"][:, :nb], t["tpos"][:, :nb], t["ii"][:, :nb],
                   t["ff"][:, :nb], t["gg"][:, :nb])
    p.tick("v", i)
    p.wait("s", "v")
    i = sca.activation(t["iscale"][:, :nb], t["s160"][:, :nb], ACT.Exp,
                       bias=162.0 * LN2, scale=-LN2)
    i = sca.activation(t["scale"][:, :nb], t["s160"][:, :nb], ACT.Exp,
                       bias=float(-162.0 * LN2 + np.log(out_scale)), scale=LN2)
    p.tick("s", i)
    p.wait("v", "s")
    y, sq = b[0][:, :H], b[1][:, :H]
    vec.tensor_tensor(y, src_ap, bcast(t["iscale"], 128, nb, 32), ALU.mult)
    vec.tensor_tensor(sq, y, y, ALU.mult)
    vec.tensor_scalar(b[2][:, :H], sq, 4.0, -(2.0 ** (mf - 1)), ALU.is_ge, ALU.mult)
    vec.tensor_scalar(b[3][:, :H], sq, 16.0, -(2.0 ** (mf - 2)), ALU.is_ge, ALU.mult)
    vec.scalar_tensor_tensor(b[4][:, :H], b[2][:, :H], 2.0 ** mf, b[3][:, :H], ALU.add, ALU.add)
    i = vec.tensor_tensor(b[2][:, :H], y, b[4][:, :H], ALU.mult)
    p.tick("v", i)
    p.wait("s", "v")
    sca.activation(ib[:, :H], b[2][:, :H], ACT.Copy, bias=64.5)
    i = sca.copy(b[0][:, :H], ib[:, :H])
    p.tick("s", i)
    vec.tensor_scalar(b[3][:, :H], sq, 4.0, 2.0 ** (-mf), ALU.is_ge, ALU.mult)
    vec.tensor_scalar(b[5][:, :H], sq, 16.0, 2.0 ** (1 - mf), ALU.is_ge, ALU.mult)
    vec.scalar_tensor_tensor(b[1][:, :H], b[3][:, :H], 2.0 ** (-mf), b[5][:, :H], ALU.add, ALU.add)
    p.wait("v", "s")
    vec.scalar_tensor_tensor(b[3][:, :H], b[2][:, :H], 64.5, b[0][:, :H], ALU.add, ALU.is_lt)
    vec.tensor_tensor(b[4][:, :H], b[0][:, :H], b[3][:, :H], ALU.subtract)
    vec.scalar_tensor_tensor(b[5][:, :H], b[4][:, :H], 64.0, b[1][:, :H], ALU.subtract, ALU.mult)
    vec.tensor_scalar(b[2][:, :H], b[5][:, :H], max_norm, -max_norm, ALU.min, ALU.max)
    i = vec.tensor_tensor(dst_bf, b[2][:, :H], bcast(t["scale"], 128, nb, 32), ALU.mult)
    p.tick("v", i)
    return i


def qmx_scratch2(nc, es, W, prefix="q"):
    nb = W // 32
    t = {}
    for i in range(6):
        t[f"b{i}"] = es.enter_context(nc.sbuf_tensor(f"{prefix}b{i}", [128, W], F32))
    t["ib"] = es.enter_context(nc.sbuf_tensor(prefix + "ib", [128, W], I32))
    for n in ("amax", "lg", "tpos", "ff", "gg", "s160", "iscale", "scale"):
        t[n] = es.enter_context(nc.sbuf_tensor(prefix + n, [128, nb], F32))
    t["ii"] = es.enter_context(nc.sbuf_tensor(prefix + "ii", [128, nb], I32))
    return t


def dma(p, key, dst, src, slow=False):
    eng = p.nc.sync if key == "q" else p.nc.scalar
    i = eng.dma_start(dst, src, allow_slow_non_contiguous=True) if slow else eng.dma_start(dst, src)
    p.cnt[key] += 16
    i.then_inc(p.sem[key], 16)
    return p.cnt[key]


def barrier(p):
    nc = p.nc
    for k in ("q", "a"):
        if p.cnt[k]:
            nc.sync.wait_ge(p.sem[k], p.cnt[k])
    for k in ("v", "s", "t", "g"):
        if p.cnt[k]:
            nc.sync.wait_ge(p.sem[k], p.cnt[k])
    nc.all_engine_barrier()


def emit_transpose_batch(p, src_fn, n, psum_bf, dst_ap, ident, wait_q=None):
    nc = p.nc
    if not isinstance(ident, bass.AP):
        ident = ident[:, :]
    for j in range(n):
        i = nc.tensor.transpose(psum_bf[:, j * 128:(j + 1) * 128], src_fn(j), ident)
    p.tick("t", i)
    p.wait("s", "t")
    if wait_q is not None:
        p.wait("s", "q", wait_q)
    i = nc.scalar.copy(dst_ap, psum_bf[:, :n * 128])
    p.tick("s", i)


def emit_ln(p, x_ap, W, xn_out, lnt, sq_scr):
    nc = p.nc
    vec, sca = nc.vector, nc.scalar
    sums, mcol, m2, vcol, lnv, rcol, mr = (lnt[:, i:i + 1] for i in range(7))
    i = vec.tensor_reduce(sums, x_ap, AX.X, ALU.add)
    p.tick("v", i)
    p.wait("s", "v")
    i = sca.activation(sq_scr, x_ap, ACT.Square, accum_out=lnt[:, 7:8])
    p.tick("s", i)
    p.vsync()
    vec.tensor_scalar(mcol, sums, 1.0 / W, None, ALU.mult)
    p.vsync()
    vec.tensor_tensor(m2, mcol, mcol, ALU.mult)
    p.wait("v", "s")
    p.vsync()
    i = vec.scalar_tensor_tensor(vcol, lnt[:, 7:8], 1.0 / W, m2, ALU.mult, ALU.subtract)
    p.tick("v", i)
    p.wait("s", "v")
    sca.activation(lnv, vcol, ACT.Ln, bias=1e-5)
    i = sca.activation(rcol, lnv, ACT.Exp, scale=-0.5)
    p.tick("s", i)
    p.wait("v", "s")
    vec.tensor_tensor(mr, mcol, rcol, ALU.mult)
    p.vsync()
    i = vec.tensor_scalar(xn_out, x_ap, rcol, mr, ALU.mult, ALU.subtract)
    p.tick("v", i)


def emit_ffn_weight_prep(p, w_in, IT, H, wqT_out, qt, ident, out_scale=1.0, dbg=None):
    nc = p.nc
    es = contextlib.ExitStack()
    U = _u(p)
    wsb = [es.enter_context(nc.sbuf_tensor(f"wp_wsb{i}" + U, [128, H], F32)) for i in range(2)]
    wq = [es.enter_context(nc.sbuf_tensor(f"wp_wq{i}" + U, [128, H], BF16)) for i in range(2)]
    stg = es.enter_context(nc.sbuf_tensor("wp_stg" + U, [128, 2048], BF16))
    pt = es.enter_context(nc.psum_tensor("wp_pt" + U, [128, 2048], BF16))
    NG = H // 2048 if H >= 2048 else 1
    CW = min(H, 2048)
    qdone = [0, 0]
    tdone = [0, 0]
    evac_s = 0
    outq = 0
    with es:
        dma(p, "q", wsb[0][:, :], bass.AP(w_in, 0, [[H, 128], [1, H]]))
        ld = [p.cnt["q"], 0]
        for it in range(IT):
            b = it % 2
            if it + 1 < IT:
                if it >= 1:
                    p.wait("q", "v", qdone[(it + 1) % 2])
                dma(p, "q", wsb[(it + 1) % 2][:, :],
                    bass.AP(w_in, (it + 1) * 128 * H, [[H, 128], [1, H]]))
                ld[(it + 1) % 2] = p.cnt["q"]
            p.wait("v", "q", ld[b])
            p.wait("s", "q", ld[b])
            if it >= 2:
                p.wait("v", "t", tdone[b])
            for c0 in range(0, H, CW):
                emit_qmx2(p, wsb[b][:, c0:c0 + CW], CW, wq[b][:, c0:c0 + CW], 1.0, qt,
                          out_scale=out_scale)
            qdone[b] = p.cnt["v"]
            for g0 in range(0, H // 128, 16):
                ng = min(16, H // 128 - g0)
                p.wait("t", "v", qdone[b])
                p.wait("t", "s", evac_s)
                emit_transpose_batch(
                    p, lambda j: wq[b][:, (g0 + j) * 128:(g0 + j + 1) * 128], ng,
                    pt, stg[:, :ng * 128], ident, wait_q=outq)
                evac_s = p.cnt["s"]
                p.wait("q", "s", evac_s)
                W2 = IT * 128
                dma(p, "q", bass.AP(wqT_out, (g0 * 128) * W2 + it * 128,
                                    [[W2, 128], [W2 * 128, ng], [1, 128]]),
                    stg[:, :ng * 128].rearrange("p (j c) -> p j c", j=ng))
                outq = p.cnt["q"]
            tdone[b] = p.cnt["t"]


def emit_plain_weight_prep(p, w_in, IT, H, wT_out, ident, tiled=False):
    nc = p.nc
    es = contextlib.ExitStack()
    U = _u(p)
    wsb = [es.enter_context(nc.sbuf_tensor(f"pp_wsb{i}" + U, [128, H], F32)) for i in range(2)]
    wq = [es.enter_context(nc.sbuf_tensor(f"pp_wq{i}" + U, [128, H], BF16)) for i in range(2)]
    stg = es.enter_context(nc.sbuf_tensor("pp_stg" + U, [128, 2048], BF16))
    pt = es.enter_context(nc.psum_tensor("pp_pt" + U, [128, 2048], BF16))
    NJ = H // 128
    W2 = IT * 128
    qdone = [0, 0]
    tdone = [0, 0]
    evac_s = 0
    outq = 0
    with es:
        dma(p, "q", wsb[0][:, :], bass.AP(w_in, 0, [[H, 128], [1, H]]))
        ld = [p.cnt["q"], 0]
        for it in range(IT):
            b = it % 2
            if it + 1 < IT:
                if it >= 1:
                    p.wait("q", "v", qdone[(it + 1) % 2])
                dma(p, "q", wsb[(it + 1) % 2][:, :],
                    bass.AP(w_in, (it + 1) * 128 * H, [[H, 128], [1, H]]))
                ld[(it + 1) % 2] = p.cnt["q"]
            p.wait("v", "q", ld[b])
            if it >= 2:
                p.wait("v", "t", tdone[b])
            i = nc.vector.tensor_copy(wq[b][:, :], wsb[b][:, :])
            p.tick("v", i)
            qdone[b] = p.cnt["v"]
            for g0 in range(0, NJ, 16):
                ng = min(16, NJ - g0)
                p.wait("t", "v", qdone[b])
                p.wait("t", "s", evac_s)
                emit_transpose_batch(
                    p, lambda j: wq[b][:, (g0 + j) * 128:(g0 + j + 1) * 128], ng,
                    pt, stg[:, :ng * 128], ident, wait_q=outq)
                evac_s = p.cnt["s"]
                p.wait("q", "s", evac_s)
                if tiled:
                    dma(p, "q", bass.AP(wT_out, (it * NJ + g0) * 16384,
                                        [[128, 128], [16384, ng], [1, 128]]),
                        stg[:, :ng * 128].rearrange("p (j c) -> p j c", j=ng))
                else:
                    dma(p, "q", bass.AP(wT_out, (g0 * 128) * W2 + it * 128,
                                        [[W2, 128], [W2 * 128, ng], [1, 128]]),
                        stg[:, :ng * 128].rearrange("p (j c) -> p j c", j=ng))
                outq = p.cnt["q"]
            tdone[b] = p.cnt["t"]


def _u(p):
    Prog._UID[0] += 1
    return f"_{Prog._UID[0]}"


def emit_ffn_stage(p, src, dst, wgT, wuT, wdT, interT, ident):
    nc = p.nc
    import contextlib
    T, H, I = 4096, 2048, 8192
    NT = T // 128
    TCH = 2048
    NTC = TCH // 128
    for tch in range(T // TCH):
        es = contextlib.ExitStack()
        U = _u(p)
        qt = qmx_scratch2(nc, es, 1024, prefix="ffq" + U)
        xn_qT = es.enter_context(nc.sbuf_tensor("ff_xnqT" + U, [128, 16 * TCH], BF16))
        with es:
            es1b = contextlib.ExitStack()
            xb = [es1b.enter_context(nc.sbuf_tensor(f"ff_xb{i}{U}", [128, H], F32)) for i in range(2)]
            xn = es1b.enter_context(nc.sbuf_tensor("ff_xn" + U, [128, H], F32))
            xq = [es1b.enter_context(nc.sbuf_tensor(f"ff_xq{i}{U}", [128, H], BF16)) for i in range(2)]
            lnt = es1b.enter_context(nc.sbuf_tensor("ff_lnt" + U, [128, 8], F32))
            ptr = es1b.enter_context(nc.psum_tensor("ff_ptr" + U, [128, 2048], BF16))
            _noop = es1b
            with es1b:
                vdone = [0] * NTC
                trdone = [0] * NTC

                def f1_tr(t_):
                    p.wait("t", "v", vdone[t_])
                    if t_ > 0:
                        p.wait("t", "s")
                    emit_transpose_batch(
                        p, lambda j, tb=t_ % 2: xq[tb][:, j * 128:(j + 1) * 128], 16, ptr,
                        bass.AP(xn_qT, t_ * 128, [[16 * TCH, 128], [TCH, 16], [1, 128]]),
                        ident)
                    trdone[t_] = p.cnt["t"]

                dma(p, "q", xb[0][:, :], bass.AP(src, tch * NTC * 128 * H, [[H, 128], [1, H]]))
                qld = [p.cnt["q"], 0]
                for t in range(NTC):
                    tt = tch * NTC + t
                    if t + 1 < NTC:
                        if t >= 1:
                            p.wait("q", "v", vdone[t - 1])
                        dma(p, "q", xb[(t + 1) % 2][:, :],
                            bass.AP(src, (tt + 1) * 128 * H, [[H, 128], [1, H]]))
                        qld[(t + 1) % 2] = p.cnt["q"]
                    p.wait("v", "q", qld[t % 2])
                    p.wait("s", "q", qld[t % 2])
                    if t >= 2:
                        p.wait("v", "t", trdone[t - 2])
                    emit_ln(p, xb[t % 2][:, :], H, xn[:, :], lnt, xn[:, :])
                    for c0 in (0, 1024):
                        emit_qmx2(p, xn[:, c0:c0 + 1024], 1024, xq[t % 2][:, c0:c0 + 1024], 3.0, qt)
                    vdone[t] = p.cnt["v"]
                    if t >= 1:
                        f1_tr(t - 1)
                f1_tr(NTC - 1)
                barrier(p)
            es2 = contextlib.ExitStack()
            U2 = _u(p)
            GW = 512
            NJ = GW // 128
            wg_sb = [es2.enter_context(nc.sbuf_tensor(f"ff_wg{i}{U2}", [128, 16 * GW], BF16)) for i in range(2)]
            wu_sb = [es2.enter_context(nc.sbuf_tensor(f"ff_wu{i}{U2}", [128, 16 * GW], BF16)) for i in range(2)]
            ubuf = [es2.enter_context(nc.sbuf_tensor(f"ff_ubuf{i}{U2}", [128, GW], F32)) for i in range(2)]
            sgbuf = [es2.enter_context(nc.sbuf_tensor(f"ff_sg{i}{U2}", [128, GW], F32)) for i in range(2)]
            inb = es2.enter_context(nc.sbuf_tensor("ff_in" + U2, [128, GW], F32))
            iq = [es2.enter_context(nc.sbuf_tensor(f"ff_iq{i}{U2}", [128, GW], BF16)) for i in range(3)]
            itq = [es2.enter_context(nc.sbuf_tensor(f"ff_itq{i}{U2}", [128, GW], BF16)) for i in range(2)]
            psg = [es2.enter_context(nc.psum_tensor(f"ff_psg{i}{U2}", [128, 512], F32)) for i in range(2)]
            psu = [es2.enter_context(nc.psum_tensor(f"ff_psu{i}{U2}", [128, 512], F32)) for i in range(2)]
            ptr2 = es2.enter_context(nc.psum_tensor("ff_ptr2" + U2, [128, GW], BF16))
            NG = I // GW
            NK = NG * NTC

            def load_wslab(g, buf):
                for wT, sb in ((wgT, wg_sb[buf]), (wuT, wu_sb[buf])):
                    for j in range(16):
                        dma(p, "q", sb[:, j * GW:(j + 1) * GW],
                            bass.AP(wT, j * 128 * I + g * GW, [[I, 128], [1, GW]]))
                return p.cnt["q"]

            def emit_tr(k):
                p.wait("t", "v", qx_tick[k])
                if k >= 2:
                    p.wait("t", "s", ev_tick[k - 2])
                kk = k
                emit_transpose_batch(
                    p, lambda j: iq[kk % 3][:, j * 128:(j + 1) * 128], NJ, ptr2,
                    itq[kk % 2][:, :GW], ident,
                    wait_q=(qitq[k - 2] if k >= 2 else None))
                tr_tick[k] = p.cnt["t"]
                ev_tick[k] = p.cnt["s"]
                p.wait("q", "s", ev_tick[k])
                g_, t_ = divmod(k, NTC)
                tt_ = tch * NTC + t_
                dma(p, "q", bass.AP(interT, (tt_ * 64 + g_ * NJ) * 16384,
                                    [[128, 128], [16384, NJ], [1, 128]]),
                    itq[kk % 2][:, :GW].rearrange("p (j c) -> p j c", j=NJ))
                qitq[k] = p.cnt["q"]

            with es2:
                wready = [load_wslab(0, 0), 0]
                gdone = [0] * NG
                mm_tick = [0] * NK
                tr_tick = [0] * NK
                sc_tick = [0] * NK
                qx_tick = [0] * NK
                ev_tick = [0] * NK
                qitq = [0] * NK
                inb_tick = [0] * NK
                for g in range(NG):
                    if g + 1 < NG:
                        if g >= 1:
                            p.wait("q", "t", gdone[g - 1])
                        wready[(g + 1) % 2] = load_wslab(g + 1, (g + 1) % 2)
                    for t in range(NTC):
                        k = g * NTC + t
                        kb = k % 2
                        p.wait("t", "q", wready[g % 2])
                        if k >= 2:
                            p.wait("t", "s", sc_tick[k - 2])
                        for j in range(16):
                            nc.tensor.matmul(
                                psg[kb][:, :],
                                bass.AP(xn_qT, j * TCH + t * 128, [[16 * TCH, 128], [1, 128]]),
                                wg_sb[g % 2][:, j * GW:j * GW + GW],
                                start=(j == 0), stop=(j == 15))
                        for j in range(16):
                            i = nc.tensor.matmul(
                                psu[kb][:, :],
                                bass.AP(xn_qT, j * TCH + t * 128, [[16 * TCH, 128], [1, 128]]),
                                wu_sb[g % 2][:, j * GW:j * GW + GW],
                                start=(j == 0), stop=(j == 15))
                        p.tick("t", i)
                        mm_tick[k] = p.cnt["t"]
                        gdone[g] = mm_tick[k]
                        if k >= 2:
                            emit_tr(k - 2)
                        p.wait("s", "t", mm_tick[k])
                        if k >= 2:
                            p.wait("s", "v", inb_tick[k - 2])
                        nc.scalar.activation(sgbuf[kb][:, :], psg[kb][:, :], ACT.Silu)
                        i = nc.scalar.copy(ubuf[kb][:, :], psu[kb][:, :])
                        p.tick("s", i)
                        sc_tick[k] = p.cnt["s"]
                        p.wait("v", "s", sc_tick[k])
                        i = nc.vector.tensor_tensor(inb[:, :], sgbuf[kb][:, :], ubuf[kb][:, :], ALU.mult)
                        p.tick("v", i)
                        inb_tick[k] = p.cnt["v"]
                        if k >= 3:
                            p.wait("v", "t", tr_tick[k - 3])
                        emit_qmx2(p, inb[:, :], GW, iq[k % 3][:, :], 3.0, qt)
                        qx_tick[k] = p.cnt["v"]
                emit_tr(NK - 2)
                emit_tr(NK - 1)
        barrier(p)
    es3 = contextlib.ExitStack()
    U3 = _u(p)
    wd_sb = es3.enter_context(nc.sbuf_tensor("ff_wd" + U3, [128, 64 * 1024], BF16))
    it_sb = [es3.enter_context(nc.sbuf_tensor(f"ff_it{i}{U3}", [128, 64 * 128], BF16)) for i in range(2)]
    xs = [es3.enter_context(nc.sbuf_tensor(f"ff_xs{i}{U3}", [128, 1024], F32)) for i in range(2)]
    outb = [es3.enter_context(nc.sbuf_tensor(f"ff_ob{i}{U3}", [128, 1024], F32)) for i in range(2)]
    pso = [es3.enter_context(nc.psum_tensor(f"ff_pso{i}{U3}", [128, 1024], F32)) for i in range(2)]
    with es3:
        for half in range(2):
            h0 = half * 1024
            if half == 1:
                p.wait("q", "t")
            for j in range(64):
                dma(p, "q", wd_sb[:, j * 1024:(j + 1) * 1024],
                    bass.AP(wdT, j * 128 * H + h0, [[H, 128], [1, 1024]]))
            wdready = p.cnt["q"]
            tmm = [0] * NT
            vev = [0] * NT
            qout = [0] * NT
            for t in range(NT):
                bi = t % 2
                if t >= 2:
                    p.wait("q", "t", tmm[t - 2])
                    p.wait("q", "v", vev[t - 2])
                dma(p, "q", it_sb[bi][:, :],
                    bass.AP(interT, t * 64 * 16384, [[128, 128], [16384, 64], [1, 128]]))
                dma(p, "q", xs[bi][:, :],
                    bass.AP(src, t * 128 * H + h0, [[H, 128], [1, 1024]]))
                p.wait("t", "q", max(p.cnt["q"], wdready))
                if t >= 2:
                    p.wait("t", "v", vev[t - 2])
                for j in range(64):
                    nc.tensor.matmul(pso[bi][:, 0:512], it_sb[bi][:, j * 128:(j + 1) * 128],
                                     wd_sb[:, j * 1024:j * 1024 + 512],
                                     start=(j == 0), stop=(j == 63))
                    i = nc.tensor.matmul(pso[bi][:, 512:1024], it_sb[bi][:, j * 128:(j + 1) * 128],
                                         wd_sb[:, j * 1024 + 512:j * 1024 + 1024],
                                         start=(j == 0), stop=(j == 63))
                p.tick("t", i)
                tmm[t] = p.cnt["t"]
                p.wait("v", "t")
                if t >= 2:
                    p.wait("v", "q", qout[t - 2])
                i = nc.vector.tensor_tensor(outb[bi][:, :], pso[bi][:, :], xs[bi][:, :], ALU.add)
                p.tick("v", i)
                vev[t] = p.cnt["v"]
                p.wait("q", "v", vev[t])
                dma(p, "q", bass.AP(dst, t * 128 * H + h0, [[H, 128], [1, 1024]]),
                    outb[bi][:, :])
                qout[t] = p.cnt["q"]
    barrier(p)


"""Mamba-2 SSD stage emitter for the macaron megakernel.

Stage B: dst = src + Mamba(LN(src)) per core (one sequence of 4096 tokens).
4096/CH chunks of CH tokens, carrying conv halo (3 cols) and SSD state across chunks.
"""

CH = 256
NTT = CH // 128
NBL = CH // 64
NOT = 67


def emit_sigma_psum(p, src_ap, dst, e1, e2, mul_with=None, bias_col=None):
    nc = p.nc
    if bias_col is None:
        i = nc.scalar.activation(e1, src_ap, ACT.Exp, scale=-1.0)
    else:
        i = nc.scalar.activation(e1, src_ap, ACT.Exp, scale=-1.0, bias=bias_col)
    p.tick("s", i)
    p.wait("v", "s")
    i = nc.vector.tensor_scalar(e2, e1, 1e18, None, ALU.min)
    p.tick("v", i)
    p.wait("s", "v")
    nc.scalar.activation(e1, e2, ACT.Ln, bias=1.0)
    i = nc.scalar.activation(e2, e1, ACT.Exp, scale=-1.0)
    p.tick("s", i)
    p.wait("v", "s")
    i = nc.vector.tensor_tensor(dst, mul_with, e2, ALU.mult)
    p.tick("v", i)
    return i


def emit_mamba_stage(p, src, dst, winT_t, woT, params, cm):
    nc = p.nc
    ident, identf = cm["ident"], cm["identf"]
    L1, M1m, ones1, onescol, Ieye64 = cm["L1"], cm["M1m"], cm["ones1"], cm["onescol"], cm["Ieye64"]
    onesf = cm["onesf"]
    es = contextlib.ExitStack()
    U = _u(p)
    T = lambda n, s, d=F32: es.enter_context(nc.sbuf_tensor(n + U, s, d))
    Sf = [T(f"mb_Sf{i}", [128, 4096]) for i in range(2)]
    Sbf = T("mb_Sbf", [128, 4096], BF16)
    convw = T("mb_convw", [128, 136])
    convb = T("mb_convb", [128, 34])
    convbn = T("mb_convbn", [128, 34])
    dtb = T("mb_dtb", [64, 1])
    acol = T("mb_acol", [64, 1])
    gcol = T("mb_gcol", [128, 32])
    halo = T("mb_halo", [128, 102])
    ddiag = T("mb_ddiag", [64, 4096], BF16)
    drow = T("mb_drow", [1, 64])
    drowb = T("mb_drowb", [1, 64], BF16)
    dmat_s = T("mb_dmat", [64, 64])
    with es:
        dma(p, "q", convw[:, :], bass.AP(params["conv_w"], 0, [[4, 128], [512, 34], [1, 4]]))
        dma(p, "q", convb[:, :34], bass.AP(params["conv_b"], 0, [[1, 128], [128, 34]]), slow=True)
        dma(p, "q", dtb[:, :], bass.AP(params["dt_bias"], 0, [[1, 64], [1, 1]]), slow=True)
        dma(p, "q", acol[:, :], bass.AP(params["A_log"], 0, [[1, 64], [1, 1]]), slow=True)
        dma(p, "q", drow[:, :], bass.AP(params["Dv"], 0, [[0, 1], [1, 64]]))
        dma(p, "q", gcol[:, :], bass.AP(params["norm_w"], 0, [[1, 128], [128, 32]]), slow=True)
        p.wait("v", "q", p.cnt["q"])
        p.wait("s", "q", p.cnt["q"])
        nc.vector.tensor_scalar(convbn[:, :34], convb[:, :34], -1.0, None, ALU.mult)
        nc.vector.memset(halo[:, :], 0.0)
        nc.vector.memset(Sf[0][:, :], 0.0)
        i = nc.vector.memset(Sbf[:, :], 0.0)
        p.tick("v", i)
        i = nc.scalar.activation(acol[:, :], acol[:, :], ACT.Exp)
        p.tick("s", i)
        p.wait("v", "s")
        p.vsync()
        i = nc.vector.tensor_scalar(acol[:, :], acol[:, :], -1.0, None, ALU.mult)
        p.tick("v", i)
        p.wait("s", "v")
        i = nc.scalar.copy(drowb[:, :], drow[:, :])
        p.tick("s", i)
        p.wait("t", "s")
        with nc.psum_tensor("mb_pdm" + U, [64, 64], F32) as pdm:
            i = nc.tensor.matmul(pdm[:, :], ones1[:1, :64], drowb[:, :], start=True, stop=True)
            p.tick("t", i)
            p.wait("v", "t")
            i = nc.vector.tensor_copy(dmat_s[:, :], pdm[:, :])
            p.tick("v", i)
            p.vsync()
            i = nc.vector.tensor_tensor(
                ddiag[:, :],
                bass.AP(dmat_s, 0, [[64, 64], [1, 64], [0, 64]]),
                bass.AP(Ieye64, 0, [[64, 64], [0, 64], [1, 64]]),
                ALU.mult)
            p.tick("v", i)
        barrier(p)

        for ch in range(4096 // CH):
            esC = contextlib.ExitStack()
            UC = _u(p)
            TC = lambda n, s, d=F32: esC.enter_context(nc.sbuf_tensor(n + UC, s, d))
            y3 = TC("mb_y3", [128, 32 * CH], BF16)
            rT = TC("mb_rT", [128, NTT])
            sz = TC("mb_sz", [128, 32 * CH], BF16)
            xbcs = TC("mb_xbcs", [128, 34 * CH], BF16)
            ext = TC("mb_ext", [128, CH + 3])
            accb = TC("mb_accb", [128, CH])
            e1 = TC("mb_e1", [128, CH])
            e2 = TC("mb_e2", [128, CH])
            dtT = TC("mb_dtT", [64, NBL * 64])
            adT = TC("mb_adT", [64, NBL * 64])
            BT = TC("mb_BT", [64, NBL * 128], BF16)
            with esC:
                esA = contextlib.ExitStack()
                TA = lambda n, s, d=F32: esA.enter_context(nc.sbuf_tensor(n + UC + "A", s, d))
                xc = [TA(f"mb_xc{i}", [128, 2048]) for i in range(2)]
                lnt = TA("mb_lnt", [128, 8])
                xnbf = TA("mb_xnbf", [128, 2048], BF16)
                xnT = TA("mb_xnT", [128, 16 * CH], BF16)
                wo_sb = [TA(f"mb_wo{i}", [128, 16 * 128], BF16) for i in range(2)]
                dtx = ext[0:64, 0:CH]
                dtv = accb[0:64, 0:CH]
                adv = e1[0:64, 0:CH]
                with esA:
                    vdone = [0, 0]
                    with nc.psum_tensor("mb_ptr" + UC, [128, 2048], BF16) as ptr:
                        for tt in range(NTT):
                            if tt >= 2:
                                p.wait("q", "v", vdone[tt % 2])
                            dma(p, "q", xc[tt % 2][:, :],
                                bass.AP(src, (ch * NTT + tt) * 128 * 2048, [[2048, 128], [1, 2048]]))
                            p.wait("v", "q", p.cnt["q"])
                            p.wait("s", "q", p.cnt["q"])
                            if tt > 0:
                                p.wait("v", "t")
                            emit_ln(p, xc[tt % 2][:, :], 2048, xnbf[:, :], lnt,
                                    y3[:, 0:2048])
                            vdone[tt % 2] = p.cnt["v"]
                            p.wait("t", "v")
                            if tt > 0:
                                p.wait("t", "s")
                            emit_transpose_batch(
                                p, lambda j: xnbf[:, j * 128:(j + 1) * 128], 16, ptr,
                                bass.AP(xnT, tt * 128, [[16 * CH, 128], [CH, 16], [1, 128]]),
                                ident)
                    def load_wot(ot, buf):
                        dma(p, "q", wo_sb[buf][:, :],
                            bass.AP(winT_t, ot * 16 * 16384, [[128, 128], [16384, 16], [1, 128]]))
                        return p.cnt["q"]
                    with (nc.psum_tensor("mb_po0" + UC, [128, CH], F32) as po0,
                          nc.psum_tensor("mb_po1" + UC, [128, CH], F32) as po1,
                          nc.psum_tensor("mb_pt2" + UC, [128, 64], F32) as pt2):
                        pos = [po0, po1]
                        p.wait("t", "s")
                        wready = [load_wot(0, 0), 0]
                        lastuse = [0, 0]
                        ev_v = [0, 0]
                        ev_s = [0, 0]
                        for ot in range(NOT):
                            po = pos[ot % 2]
                            if ot + 1 < NOT:
                                if ot >= 1:
                                    p.wait("q", "t", lastuse[(ot + 1) % 2])
                                wready[(ot + 1) % 2] = load_wot(ot + 1, (ot + 1) % 2)
                            p.wait("t", "q", wready[ot % 2])
                            if ot >= 2:
                                p.wait("t", "v", ev_v[ot % 2])
                                p.wait("t", "s", ev_s[ot % 2])
                            for j in range(16):
                                i = nc.tensor.matmul(
                                    po[:, :], wo_sb[ot % 2][:, j * 128:(j + 1) * 128],
                                    xnT[:, j * CH:j * CH + CH],
                                    start=(j == 0), stop=(j == 15))
                            p.tick("t", i)
                            lastuse[ot % 2] = p.cnt["t"]
                            p.wait("s", "t")
                            p.wait("v", "t")
                            if ot < 32:
                                i = nc.scalar.activation(sz[:, ot * CH:(ot + 1) * CH],
                                                         po[:, :], ACT.Silu)
                                p.tick("s", i)
                                ev_v[ot % 2] = p.cnt["v"]
                                ev_s[ot % 2] = p.cnt["s"]
                            elif ot < 66:
                                c = ot - 32
                                nc.vector.tensor_copy(ext[:, 0:3], halo[:, c * 3:c * 3 + 3])
                                i = nc.scalar.copy(ext[:, 3:CH + 3], po[:, :])
                                p.tick("s", i)
                                ev_s[ot % 2] = p.cnt["s"]
                                ev_v[ot % 2] = p.cnt["v"]
                                p.wait("v", "s")
                                p.vsync()
                                i = nc.vector.tensor_copy(halo[:, c * 3:c * 3 + 3], ext[:, CH:CH + 3])
                                p.tick("v", i)
                                p.vsync()
                                nc.vector.tensor_scalar(accb[:, :], ext[:, 0:CH],
                                                        convw[:, c * 4:c * 4 + 1], None, ALU.mult)
                                for k in (1, 2, 3):
                                    i = nc.vector.scalar_tensor_tensor(
                                        accb[:, :], ext[:, k:k + CH], convw[:, c * 4 + k:c * 4 + k + 1],
                                        accb[:, :], ALU.mult, ALU.add)
                                p.tick("v", i)
                                p.wait("s", "v")
                                i = nc.scalar.activation(xbcs[:, c * CH:(c + 1) * CH],
                                                         accb[:, :], ACT.Silu,
                                                         bias=convb[:, c:c + 1])
                                p.tick("s", i)
                            else:
                                i = nc.vector.tensor_scalar(dtx, po[0:64, :], dtb[:, 0:1],
                                                            None, ALU.add)
                                p.tick("v", i)
                                ev_v[ot % 2] = p.cnt["v"]
                                ev_s[ot % 2] = p.cnt["s"]
                                p.wait("s", "v")
                                nc.scalar.activation(e2[0:64, :], dtx, ACT.Abs)
                                nc.scalar.activation(e2[0:64, :], e2[0:64, :], ACT.Exp, scale=-1.0)
                                i = nc.scalar.activation(e2[0:64, :], e2[0:64, :], ACT.Ln, bias=1.0)
                                p.tick("s", i)
                                p.wait("v", "s")
                                nc.vector.scalar_tensor_tensor(dtv, dtx, 0.0,
                                                               e2[0:64, :], ALU.max, ALU.add)
                                p.vsync()
                                i = nc.vector.tensor_scalar(adv, dtv, acol[:, 0:1],
                                                            None, ALU.mult)
                                p.tick("v", i)
                                p.wait("t", "v")
                                for tt in range(NTT):
                                    i = nc.tensor.transpose(pt2[0:128, :], dtv[:, tt * 128:(tt + 1) * 128],
                                                            identf[0:64, 0:64])
                                    p.tick("t", i)
                                    p.wait("s", "t")
                                    nc.scalar.copy(dtT[:, (2 * tt) * 64:(2 * tt + 1) * 64], pt2[0:64, :])
                                    i = nc.scalar.copy(dtT[:, (2 * tt + 1) * 64:(2 * tt + 2) * 64], pt2[64:128, :])
                                    p.tick("s", i)
                                    p.wait("t", "s")
                                    i = nc.tensor.transpose(pt2[0:128, :], adv[:, tt * 128:(tt + 1) * 128],
                                                            identf[0:64, 0:64])
                                    p.tick("t", i)
                                    p.wait("s", "t")
                                    nc.scalar.copy(adT[:, (2 * tt) * 64:(2 * tt + 1) * 64], pt2[0:64, :])
                                    i = nc.scalar.copy(adT[:, (2 * tt + 1) * 64:(2 * tt + 2) * 64], pt2[64:128, :])
                                    p.tick("s", i)
                                    p.wait("t", "s")
                    with nc.psum_tensor("mb_ptb" + UC, [128, NTT * 128], BF16) as ptb:
                        p.wait("t", "v")
                        p.wait("t", "s")
                        for tt in range(NTT):
                            i = nc.tensor.transpose(ptb[:, tt * 128:(tt + 1) * 128],
                                                    xbcs[:, 32 * CH + tt * 128:32 * CH + (tt + 1) * 128],
                                                    ident[:, :])
                        p.tick("t", i)
                        p.wait("s", "t")
                        for tt in range(NTT):
                            nc.scalar.copy(BT[:, (2 * tt) * 128:(2 * tt + 1) * 128],
                                           ptb[0:64, tt * 128:(tt + 1) * 128])
                            i = nc.scalar.copy(BT[:, (2 * tt + 1) * 128:(2 * tt + 2) * 128],
                                               ptb[64:128, tt * 128:(tt + 1) * 128])
                        p.tick("s", i)
                    barrier(p)
                esB = contextlib.ExitStack()
                TB = lambda n, s, d=F32: esB.enter_context(nc.sbuf_tensor(n + UC + "S", s, d))
                XTb = TB("mb_XTb", [64, 4096], BF16)
                rhsD = TB("mb_rhsD", [64, 2048])
                rhsH = TB("mb_rhsH", [64, 2048], BF16)
                rhsL = TB("mb_rhsL", [64, 2048], BF16)
                Mh = TB("mb_Mh", [64, 4096], BF16)
                Gm = TB("mb_Gm", [64, 64], BF16)
                dt63 = TB("mb_dt63", [64, 64])
                en1 = TB("mb_en1", [64, 64])
                decdtT = TB("mb_decdt", [64, 64])
                expAcT = TB("mb_expAcT", [64, 64])
                lam = TB("mb_lam", [1, 64])
                lamlogm = TB("mb_lamlogm", [128, 64])
                lamm = TB("mb_lamm", [128, 64])
                Bd = TB("mb_Bd", [64, 2048], BF16)
                Yb = TB("mb_Yb", [64, 4096], BF16)
                y2b = TB("mb_y2", [128, 1024])
                sq16 = TB("mb_sq16", [128, 1024], BF16)
                with esB:
                    for b in range(NBL):
                        dtTb = dtT[0:64, b * 64:(b + 1) * 64]
                        adTb = adT[0:64, b * 64:(b + 1) * 64]
                        Cblk = xbcs[:, 33 * CH + b * 64:33 * CH + (b + 1) * 64]
                        Bblk = xbcs[:, 32 * CH + b * 64:32 * CH + (b + 1) * 64]
                        with nc.psum_tensor(f"mb_pxt{b}" + UC, [64, 2048], BF16) as pxt:
                            for half in range(2):
                                p.wait("t", "s")
                                p.wait("t", "v")
                                for cj in range(16):
                                    ct = half * 16 + cj
                                    i = nc.tensor.transpose(
                                        pxt[:, cj * 128:(cj + 1) * 128],
                                        xbcs[:, ct * CH + b * 64:ct * CH + (b + 1) * 64],
                                        ident[:, :])
                                p.tick("t", i)
                                p.wait("s", "t")
                                i = nc.scalar.copy(XTb[:, half * 2048:(half + 1) * 2048],
                                                   pxt[:, :])
                                p.tick("s", i)
                        with (nc.psum_tensor(f"mb_pG{b}" + UC, [64, 64], F32) as pG,
                              nc.psum_tensor(f"mb_pD{b}" + UC, [64, 2048], F32) as pD):
                            p.wait("t", "v")
                            p.wait("t", "s")
                            i = nc.tensor.matmul(pG[:, :], Bblk, Cblk, start=True, stop=True)
                            p.tick("t", i)
                            p.wait("v", "t")
                            i = nc.vector.tensor_tensor(Gm[:, :], pG[:, :], M1m[:, :], ALU.mult)
                            p.tick("v", i)
                            for hg in range(2):
                                p.wait("v", "s")
                                p.vsync()
                                nc.vector.tensor_tensor(
                                    rhsD[:, :],
                                    bass.AP(adT, b * 64 + hg * 32, [[NBL * 64, 64], [1, 32], [0, 64]]),
                                    bass.AP(M1m, 0, [[64, 64], [0, 32], [1, 64]]),
                                    ALU.mult)
                                nc.vector.tensor_copy(rhsH[:, :], rhsD[:, :])
                                i = nc.vector.tensor_tensor(rhsL[:, :], rhsD[:, :], rhsH[:, :], ALU.subtract)
                                p.tick("v", i)
                                p.wait("t", "v")
                                p.wait("t", "s")
                                for seg in range(4):
                                    nc.tensor.matmul(pD[:, seg * 512:(seg + 1) * 512],
                                                     L1[:, :], rhsH[:, seg * 512:(seg + 1) * 512],
                                                     start=True, stop=False)
                                    i = nc.tensor.matmul(pD[:, seg * 512:(seg + 1) * 512],
                                                         L1[:, :], rhsL[:, seg * 512:(seg + 1) * 512],
                                                         start=False, stop=True)
                                p.tick("t", i)
                                p.wait("s", "t")
                                i = nc.scalar.activation(Mh[:, hg * 2048:(hg + 1) * 2048],
                                                         pD[:, :], ACT.Exp)
                                p.tick("s", i)
                                p.wait("v", "s")
                                i = nc.vector.tensor_copy(
                                    dt63[:, hg * 32:hg * 32 + 32],
                                    bass.AP(pD, 63, [[2048, 64], [64, 32], [1, 1]]))
                                p.tick("v", i)
                                p.vsync()
                                nc.vector.tensor_tensor(
                                    Mh[:, hg * 2048:(hg + 1) * 2048],
                                    Mh[:, hg * 2048:(hg + 1) * 2048],
                                    bass.AP(Gm, 0, [[64, 64], [0, 32], [1, 64]]),
                                    ALU.mult)
                                i = nc.vector.tensor_tensor(
                                    Mh[:, hg * 2048:(hg + 1) * 2048],
                                    Mh[:, hg * 2048:(hg + 1) * 2048],
                                    bass.AP(dtT, b * 64 + hg * 32, [[NBL * 64, 64], [1, 32], [0, 64]]),
                                    ALU.mult)
                                p.tick("v", i)
                        p.vsync()
                        i = nc.vector.tensor_tensor(lam[0:1, :], dt63[0:1, :], adTb[0:1, :], ALU.add)
                        p.tick("v", i)
                        p.wait("s", "v")
                        i = nc.scalar.activation(decdtT[:, :], dt63[:, :], ACT.Exp)
                        p.tick("s", i)
                        p.wait("t", "v")
                        with nc.psum_tensor(f"mb_pL{b}" + UC, [128, 64], F32) as pL:
                            i = nc.tensor.matmul(pL[:, :], onesf[0:1, :], lam[0:1, :],
                                                 start=True, stop=True)
                            p.tick("t", i)
                            p.wait("v", "t")
                            i = nc.vector.tensor_copy(lamlogm[:, :], pL[:, :])
                            p.tick("v", i)
                        p.vsync()
                        nc.vector.tensor_tensor(en1[:, :], lamlogm[0:64, :], dt63[:, :], ALU.subtract)
                        i = nc.vector.tensor_tensor(decdtT[:, :], decdtT[:, :], dtTb, ALU.mult)
                        p.tick("v", i)
                        p.wait("s", "v")
                        nc.scalar.activation(expAcT[:, :], en1[:, :], ACT.Exp)
                        i = nc.scalar.activation(lamm[:, :], lamlogm[:, :], ACT.Exp)
                        p.tick("s", i)
                        p.wait("v", "s")
                        p.wait("t", "s")
                        for hg in range(4):
                            i = nc.vector.tensor_tensor(
                                Bd[:, :],
                                bass.AP(BT, b * 128, [[NBL * 128, 64], [0, 16], [1, 128]]),
                                bass.AP(decdtT, hg * 16, [[64, 64], [1, 16], [0, 128]]),
                                ALU.mult)
                            p.tick("v", i)
                            with (nc.psum_tensor(f"mb_pYd{b}_{hg}" + UC, [64, 1024], F32) as pYd,
                                  nc.psum_tensor(f"mb_pYo{b}_{hg}" + UC, [64, 1024], F32) as pYo,
                                  nc.psum_tensor(f"mb_pS{b}_{hg}" + UC, [128, 1024], F32) as pS):
                                p.wait("t", "v")
                                p.wait("t", "s")
                                for seg in range(2):
                                    nc.tensor.matmul(
                                        pYo[:, seg * 512:(seg + 1) * 512], Cblk,
                                        Sbf[:, hg * 1024 + seg * 512:hg * 1024 + (seg + 1) * 512],
                                        start=True, stop=True)
                                for hl in range(16):
                                    h = hg * 16 + hl
                                    xsl = XTb[:, h * 64:(h + 1) * 64]
                                    nc.tensor.matmul(pYd[:, hl * 64:(hl + 1) * 64],
                                                     Mh[:, h * 64:(h + 1) * 64], xsl,
                                                     start=True, stop=False)
                                    nc.tensor.matmul(pYd[:, hl * 64:(hl + 1) * 64],
                                                     ddiag[:, h * 64:(h + 1) * 64], xsl,
                                                     start=False, stop=True)
                                    i = nc.tensor.matmul(pS[:, hl * 64:(hl + 1) * 64],
                                                         Bd[:, hl * 128:(hl + 1) * 128], xsl,
                                                         start=True, stop=True)
                                p.tick("t", i)
                                p.wait("v", "t")
                                so = Sf[(ch * NBL + b) % 2]
                                sn = Sf[(ch * NBL + b + 1) % 2]
                                nc.vector.tensor_tensor(
                                    sn[:, hg * 1024:(hg + 1) * 1024],
                                    so[:, hg * 1024:(hg + 1) * 1024],
                                    bass.AP(lamm, hg * 16, [[64, 128], [1, 16], [0, 64]]),
                                    ALU.mult)
                                nc.vector.tensor_tensor(
                                    sn[:, hg * 1024:(hg + 1) * 1024],
                                    sn[:, hg * 1024:(hg + 1) * 1024],
                                    pS[:, :], ALU.add)
                                nc.vector.tensor_tensor(
                                    y2b[0:64, :], pYo[:, :],
                                    bass.AP(expAcT, hg * 16, [[64, 64], [1, 16], [0, 64]]),
                                    ALU.mult)
                                i = nc.vector.tensor_tensor(
                                    Yb[:, hg * 1024:(hg + 1) * 1024],
                                    y2b[0:64, :], pYd[:, :], ALU.add)
                                p.tick("v", i)
                        sn = Sf[(ch * NBL + b + 1) % 2]
                        i = nc.vector.tensor_copy(Sbf[:, :], sn[:, :])
                        p.tick("v", i)
                        with (nc.psum_tensor(f"mb_pyt{b}" + UC, [128, 1024], BF16) as pyt,
                              nc.psum_tensor(f"mb_prm{b}" + UC, [1, 64], F32) as prm,
                              nc.psum_tensor(f"mb_prt{b}" + UC, [64, 1], F32) as prt):
                            for half in range(2):
                                p.wait("t", "v")
                                for cj in range(16):
                                    ct = half * 16 + cj
                                    i = nc.tensor.transpose(
                                        pyt[:, cj * 64:(cj + 1) * 64],
                                        Yb[:, ct * 128:(ct + 1) * 128],
                                        ident[0:64, 0:64])
                                p.tick("t", i)
                                p.wait("v", "t")
                                nc.vector.tensor_tensor(
                                    y2b[:, :], pyt[:, :],
                                    bass.AP(sz, half * 16 * CH + b * 64,
                                            [[32 * CH, 128], [CH, 16], [1, 64]]),
                                    ALU.mult)
                                i = nc.vector.tensor_tensor(
                                    bass.AP(y3, half * 16 * CH + b * 64,
                                            [[32 * CH, 128], [CH, 16], [1, 64]]),
                                    y2b[:, :],
                                    bass.AP(gcol, half * 16, [[32, 128], [1, 16], [0, 64]]),
                                    ALU.mult)
                                p.tick("v", i)
                                p.wait("s", "v")
                                i = nc.scalar.activation(sq16[:, :], y2b[:, :], ACT.Square)
                                p.tick("s", i)
                                p.wait("t", "s")
                                for cj in range(16):
                                    i = nc.tensor.matmul(prm[:, :], onescol[:, 0:1],
                                                         sq16[:, cj * 64:(cj + 1) * 64],
                                                         start=(half == 0 and cj == 0),
                                                         stop=(half == 1 and cj == 15))
                                p.tick("t", i)
                            p.wait("s", "t")
                            nc.scalar.activation(e1[0:1, 0:64], prm[:, :], ACT.Ln,
                                                 scale=1.0 / 4096.0, bias=1e-5)
                            i = nc.scalar.activation(lam[0:1, :], e1[0:1, 0:64], ACT.Exp,
                                                     scale=-0.5)
                            p.tick("s", i)
                            p.wait("t", "s")
                            i = nc.tensor.matmul(prt[:, :], lam[0:1, :], onesf[0:1, 0:1],
                                                 start=True, stop=True)
                            p.tick("t", i)
                            p.wait("v", "t")
                            tt0 = b // 2
                            pl = (b % 2) * 64
                            i = nc.vector.tensor_copy(rT[pl:pl + 64, tt0:tt0 + 1], prt[:, :])
                            p.tick("v", i)
                    barrier(p)
                esD = contextlib.ExitStack()
                TD = lambda n, s, d=F32: esD.enter_context(nc.sbuf_tensor(n + UC + "D", s, d))
                wos = [TD(f"mb_wos{i}", [128, 2048], BF16) for i in range(2)]
                xs2 = [TD(f"mb_xs{i}", [128, 2048]) for i in range(2)]
                ob = [TD(f"mb_ob{i}", [128, 2048]) for i in range(2)]
                with esD:
                    with (nc.psum_tensor("mb_pout0" + UC, [128, 2048], F32) as pout0,
                          nc.psum_tensor("mb_pout1" + UC, [128, 2048], F32) as pout1):
                        pouts = [pout0, pout1]
                        for tt in range(NTT):
                            dma(p, "q", xs2[tt][:, :],
                                bass.AP(src, (ch * NTT + tt) * 128 * 2048, [[2048, 128], [1, 2048]]))
                        def load_wos(cj, buf):
                            dma(p, "q", wos[buf][:, :],
                                bass.AP(woT, cj * 128 * 2048, [[2048, 128], [1, 2048]]))
                            return p.cnt["q"]
                        wready = [load_wos(0, 0), 0]
                        lastw = [0, 0]
                        for cj in range(32):
                            if cj + 1 < 32:
                                if cj >= 1:
                                    p.wait("q", "t", lastw[(cj + 1) % 2])
                                wready[(cj + 1) % 2] = load_wos(cj + 1, (cj + 1) % 2)
                            p.wait("t", "q", wready[cj % 2])
                            if cj == 0:
                                p.wait("t", "v")
                            for tt in range(NTT):
                                for dch in range(4):
                                    i = nc.tensor.matmul(
                                        pouts[tt][:, dch * 512:(dch + 1) * 512],
                                        y3[:, cj * CH + tt * 128:cj * CH + (tt + 1) * 128],
                                        wos[cj % 2][:, dch * 512:(dch + 1) * 512],
                                        start=(cj == 0), stop=(cj == 31))
                            p.tick("t", i)
                            lastw[cj % 2] = p.cnt["t"]
                        p.wait("v", "t")
                        for tt in range(NTT):
                            i = nc.vector.scalar_tensor_tensor(
                                ob[tt][:, :], pouts[tt][:, :], rT[:, tt:tt + 1],
                                xs2[tt][:, :], ALU.mult, ALU.add)
                            p.tick("v", i)
                            p.wait("q", "v")
                            dma(p, "q", bass.AP(dst, (ch * NTT + tt) * 128 * 2048,
                                                [[2048, 128], [1, 2048]]),
                                ob[tt][:, :])
    barrier(p)
    barrier(p)


import contextlib
from concourse.bass_utils import run_bass_kernel_spmd


def _build_program():
    nc = bass.Bass("TRN2", target_bir_lowering=False, detect_race_conditions=False)
    p = Prog(nc)
    T, H, I = 4096, 2048, 8192
    x_in = nc.dram_tensor("x", [T, H], F32, kind="ExternalInput")
    wg_in = nc.dram_tensor("w_gate", [I, H], F32, kind="ExternalInput")
    wu_in = nc.dram_tensor("w_up", [I, H], F32, kind="ExternalInput")
    wd_in = nc.dram_tensor("w_down", [H, I], F32, kind="ExternalInput")
    wip = nc.dram_tensor("in_proj_pad", [8576, H], F32, kind="ExternalInput")
    wop = nc.dram_tensor("out_proj", [H, 4096], F32, kind="ExternalInput")
    cw_in = nc.dram_tensor("conv_w", [4352, 4], F32, kind="ExternalInput")
    cb_in = nc.dram_tensor("conv_b", [4352], F32, kind="ExternalInput")
    al_in = nc.dram_tensor("A_log", [64], F32, kind="ExternalInput")
    dv_in = nc.dram_tensor("Dv", [64], F32, kind="ExternalInput")
    dtb_in = nc.dram_tensor("dt_bias", [64], F32, kind="ExternalInput")
    nw_in = nc.dram_tensor("norm_w", [4096], F32, kind="ExternalInput")
    c_in = nc.dram_tensor("consts", [128, 64], F32, kind="ExternalInput")
    id_in = nc.dram_tensor("ident", [128, 128], BF16, kind="ExternalInput")
    idf_in = nc.dram_tensor("identf", [128, 128], F32, kind="ExternalInput")
    L1_in = nc.dram_tensor("L1", [64, 64], BF16, kind="ExternalInput")
    M1_in = nc.dram_tensor("M1m", [64, 64], BF16, kind="ExternalInput")
    o1_in = nc.dram_tensor("ones1", [1, 128], BF16, kind="ExternalInput")
    oc_in = nc.dram_tensor("onescol", [128, 1], BF16, kind="ExternalInput")
    ie_in = nc.dram_tensor("Ieye64", [64, 64], BF16, kind="ExternalInput")
    of_in = nc.dram_tensor("onesf", [1, 128], F32, kind="ExternalInput")
    out = nc.dram_tensor("out", [T, H], F32, kind="ExternalOutput")
    wgT = nc.dram_tensor("wgT", [H, I], BF16)
    wuT = nc.dram_tensor("wuT", [H, I], BF16)
    wdT = nc.dram_tensor("wdT", [I, H], BF16)
    interT = nc.dram_tensor("interT", [64 * 32 * 16384], BF16)
    winT_t = nc.dram_tensor("winT_t", [67 * 16 * 16384], BF16)
    woT = nc.dram_tensor("woT", [4096, 2048], BF16)
    resid1 = nc.dram_tensor("resid1", [T, H], F32)
    resid2 = nc.dram_tensor("resid2", [T, H], F32)

    es0 = contextlib.ExitStack()
    p.alloc_consts(es0)
    p.alloc_sems(es0)
    identt = es0.enter_context(nc.sbuf_tensor("identt", [128, 128], BF16))
    identf = es0.enter_context(nc.sbuf_tensor("identft", [128, 128], F32))
    L1t = es0.enter_context(nc.sbuf_tensor("L1t", [64, 64], BF16))
    M1t = es0.enter_context(nc.sbuf_tensor("M1t", [64, 64], BF16))
    o1t = es0.enter_context(nc.sbuf_tensor("o1t", [1, 128], BF16))
    oct_ = es0.enter_context(nc.sbuf_tensor("oct", [128, 1], BF16))
    iet = es0.enter_context(nc.sbuf_tensor("iet", [64, 64], BF16))
    oft = es0.enter_context(nc.sbuf_tensor("oft", [1, 128], F32))
    with es0:
        register_qmx_consts(p)
        for v in (1.0, -1.0, 0.5, -0.5, 1e18, 1e-5, 1.0 / H, 1.0 / 4096.0,
                  float(-162.0 * LN2 + np.log(0.5))):
            p.C(v)
        p.eng.update(v=nc.vector, s=nc.scalar, t=nc.tensor, g=nc.gpsimd,
                     q=nc.sync, a=nc.scalar)
        for dst_t, src_t in ((p.ctab, c_in), (identt, id_in), (identf, idf_in),
                             (L1t, L1_in), (M1t, M1_in), (o1t, o1_in),
                             (oct_, oc_in), (iet, ie_in), (oft, of_in)):
            dma(p, "q", dst_t[:, :], src_t.ap())
        nc.sync.wait_ge(p.sem["q"], p.cnt["q"])
        nc.all_engine_barrier()
        es1 = contextlib.ExitStack()
        qt = qmx_scratch2(nc, es1, 2048, prefix="pq")
        with es1:
            emit_ffn_weight_prep(p, wg_in, 64, H, wgT, qt, identt)
            emit_ffn_weight_prep(p, wu_in, 64, H, wuT, qt, identt)
            emit_ffn_weight_prep(p, wd_in, 16, I, wdT, qt, identt, out_scale=0.5)
            emit_plain_weight_prep(p, wip, 67, H, winT_t, identt, tiled=True)
            emit_plain_weight_prep(p, wop, 16, 4096, woT, identt)
            barrier(p)
        params = dict(conv_w=cw_in, conv_b=cb_in, A_log=al_in, Dv=dv_in,
                      dt_bias=dtb_in, norm_w=nw_in)
        cm = dict(ident=identt, identf=identf, L1=L1t, M1m=M1t, ones1=o1t,
                  onescol=oct_, Ieye64=iet, onesf=oft)
        emit_ffn_stage(p, x_in, resid1, wgT, wuT, wdT, interT, identt)
        emit_mamba_stage(p, resid1, resid2, winT_t, woT, params, cm)
        emit_ffn_stage(p, resid2, out, wgT, wuT, wdT, interT, identt)
    return nc, p


_CACHED = {}


def kernel(L0_attn_x, L0_attn_in_proj_w, L0_attn_conv_w, L0_attn_conv_b,
           L0_attn_A_log, L0_attn_D, L0_attn_dt_bias, L0_attn_norm_w,
           L0_attn_out_proj_w, L0_ffn_w_gate, L0_ffn_w_up, L0_ffn_w_down):
    import ml_dtypes
    if "nc" not in _CACHED:
        _CACHED["nc"], _CACHED["p"] = _build_program()
    nc, p = _CACHED["nc"], _CACHED["p"]
    x = np.ascontiguousarray(np.asarray(L0_attn_x, dtype=np.float32))
    in_proj_pad = np.zeros((8576, 2048), np.float32)
    in_proj_pad[:8512] = np.asarray(L0_attn_in_proj_w, np.float32)
    kk, ll = np.meshgrid(np.arange(64), np.arange(64), indexing="ij")
    common = {
        "w_gate": np.ascontiguousarray(np.asarray(L0_ffn_w_gate, np.float32)),
        "w_up": np.ascontiguousarray(np.asarray(L0_ffn_w_up, np.float32)),
        "w_down": np.ascontiguousarray(np.asarray(L0_ffn_w_down, np.float32)),
        "in_proj_pad": in_proj_pad,
        "out_proj": np.ascontiguousarray(np.asarray(L0_attn_out_proj_w, np.float32)),
        "conv_w": np.ascontiguousarray(np.asarray(L0_attn_conv_w, np.float32).reshape(4352, 4)),
        "conv_b": np.ascontiguousarray(np.asarray(L0_attn_conv_b, np.float32)),
        "A_log": np.ascontiguousarray(np.asarray(L0_attn_A_log, np.float32)),
        "Dv": np.ascontiguousarray(np.asarray(L0_attn_D, np.float32)),
        "dt_bias": np.ascontiguousarray(np.asarray(L0_attn_dt_bias, np.float32)),
        "norm_w": np.ascontiguousarray(np.asarray(L0_attn_norm_w, np.float32)),
        "consts": p.consts_array(),
        "ident": np.eye(128).astype(ml_dtypes.bfloat16),
        "identf": np.eye(128).astype(np.float32),
        "L1": (kk > ll).astype(ml_dtypes.bfloat16),
        "M1m": (kk <= ll).astype(ml_dtypes.bfloat16),
        "ones1": np.ones((1, 128), ml_dtypes.bfloat16),
        "onescol": np.ones((128, 1), ml_dtypes.bfloat16),
        "Ieye64": np.eye(64).astype(ml_dtypes.bfloat16),
        "onesf": np.ones((1, 128), np.float32),
    }
    in_maps = [dict(common, x=np.ascontiguousarray(x[b])) for b in range(8)]
    res = run_bass_kernel_spmd(nc, in_maps, core_ids=list(range(8)))
    return np.stack([np.asarray(res.results[b]["out"]) for b in range(8)], axis=0)
